# version 11
# baseline (speedup 1.0000x reference)
import numpy as np
import ml_dtypes
from contextlib import ExitStack

import concourse.bass as bass
import concourse.mybir as mybir
from concourse.bass_utils import run_bass_kernel_spmd

F32 = mybir.dt.float32
BF16 = mybir.dt.bfloat16
I32 = mybir.dt.int32
AF = mybir.ActivationFunctionType
ALU = mybir.AluOpType
BF16NP = ml_dtypes.bfloat16

B = 8
S = 2048
D = 2048
I4 = 4096
NH = 2
HD = 2048
NST = 16
KC = 4
HID = 5632
NSLAB = HID // 128

NT_D = D // 128
NT_I = I4 // 128

LN_EPS = 1e-5
FFN_EPS = 1e-7

ENGINES = ("sync", "tensor", "vector", "scalar", "gpsimd")


def _flat(deps):
    out = []
    for d in deps:
        if d is None:
            continue
        if isinstance(d, list):
            for x in d:
                if x is not None:
                    out.append(x)
        else:
            out.append(d)
    return out


class Emit:
    def __init__(self, nc, stack):
        self.nc = nc
        self.stack = stack
        self.recs = {e: [] for e in ENGINES}
        self.last_wait = {e: {} for e in ENGINES}
        self.semcnt = {}
        self.cur = None
        self.dmas = None
        self._semi = 0
        self.consts = {}

    def _new_sem(self):
        self._semi += 1
        h = self.stack.enter_context(self.nc.semaphore(f"s{self._semi}"))
        self.semcnt[h] = 0
        return h

    NPOOL = 12

    def new_phase(self):
        barrier = [(s, c) for s, c in self.semcnt.items() if c > 0]
        self.cur = {e: self._new_sem() for e in ("tensor", "vector", "scalar", "gpsimd")}
        if not hasattr(self, "pools"):
            self.pools = {q: [[self._new_sem(), None] for _ in range(self.NPOOL)]
                          for q in ("sync", "gpsimd", "scalar")}
            self.pool_idx = {q: 0 for q in self.pools}
        if barrier:
            for e in ENGINES:
                self._record(e, None, barrier, None, 0)

    def _record(self, eng, fn, deps, sem, amt):
        waits = []
        for dsem, dval in _flat(deps):
            if dval <= self.last_wait[eng].get(dsem, -1):
                continue
            waits.append((dsem, dval))
            self.last_wait[eng][dsem] = dval
        self.recs[eng].append((waits, fn, sem, amt))

    def op(self, eng, fn, deps=(), inc=True):
        sem = self.cur[eng] if inc else None
        tok = None
        if inc:
            self.semcnt[sem] += 1
            tok = (sem, self.semcnt[sem])
        self._record(eng, fn, deps, sem, 1)
        return tok

    def dma(self, q, out, in_, deps=(), slow=False):
        slot = self.pools[q][self.pool_idx[q] % self.NPOOL]
        self.pool_idx[q] += 1
        sem, last = slot
        self.semcnt[sem] += 16
        kw = {"allow_slow_non_contiguous": True} if slow else {}

        def fn(e, out=out, in_=in_, kw=kw):
            return e.dma_start(out=out, in_=in_, **kw)

        self._record(q, fn, list(deps) + [last], sem, 16)
        tok = (sem, self.semcnt[sem])
        slot[1] = tok
        return tok

    def wait_only(self, eng, deps):
        self._record(eng, None, deps, None, 0)

    pump_steps = None

    def pump(self):
        if self.pump_steps:
            self.pump_steps.pop(0)()

    def finalize(self):
        with self.nc.Block() as block:
            for eng in ENGINES:
                recs = self.recs[eng]
                if not recs:
                    continue

                def runner(e, recs=recs):
                    for waits, fn, sem, amt in recs:
                        for (s, v) in waits:
                            e.wait_ge(s, v)
                        if fn is not None:
                            inst = fn(e)
                            if sem is not None and inst is not None:
                                inst.then_inc(sem, amt)

                getattr(block, eng)(runner)


class Arena:
    def __init__(self, nc, nbytes):
        self.t = nc.alloc_sbuf_tensor("arena", [128, nbytes], mybir.dt.uint8)
        self.nbytes = nbytes
        self.off = 0

    def alloc(self, cols, dtype):
        esz = {F32: 4, BF16: 2, I32: 4}[dtype]
        self.off = (self.off + 31) & ~31
        start = self.off
        self.off += cols * esz
        assert self.off <= self.nbytes, f"arena overflow: {self.off} > {self.nbytes}"
        return self.t[:, start : start + cols * esz].bitcast(dtype)

    def reset(self, off=None):
        self.off = self.base if off is None else off

    base = 0


PHASES = {}


def phase(name, needs, produces):
    def deco(fn):
        PHASES[name] = (fn, needs, produces)
        return fn

    return deco


SPEC = {
    "x0": ((D, S), F32),
    "wxT": ((16, 128, I4), F32),
    "wgT": ((16, 128, I4), F32),
    "cw0": ((I4, KC), F32),
    "cb0": ((I4,), F32),
    "xp0T": ((NH, 16, 128, 160), F32),
    "dt0T": ((NH, 128, HD), F32),
    "dtb0": ((I4,), F32),
    "al0": ((I4, NST), F32),
    "d0": ((I4,), F32),
    "woT": ((32, 128, D), F32),
    "fw0": ((16, 128, D), F32),
    "fs0": ((D,), F32),
    "cw1": ((D, KC), F32),
    "cb1": ((D,), F32),
    "xp1T": ((16, 128, 160), F32),
    "dt1T": ((128, D), F32),
    "dtb1": ((D,), F32),
    "al1": ((D, NST), F32),
    "d1": ((D,), F32),
    "wqgT": ((16, 128, HID), I32),
    "wquT": ((16, 128, HID), I32),
    "wqdT": ((NSLAB, 128, D), I32),
    "sg": ((HID,), F32),
    "su": ((HID,), F32),
    "sd": ((D,), F32),
    "xb_s": ((NT_I, 128, S), BF16),
    "gate_s": ((NT_I, 128, S), BF16),
    "u0_s": ((NT_I, 128, S), BF16),
    "du0_s": ((NT_I, 128, S), BF16),
    "de0_s": ((NT_I, 128, S), BF16),
    "bc0_s": ((NH, 32, S), BF16),
    "ybf_s": ((NT_I, 128, S), BF16),
    "xr0_s": ((NT_D, 128, S), F32),
    "x1_s": ((NT_D, 128, S), F32),
    "x1b_s": ((NT_D, 128, S), BF16),
    "x2_s": ((NT_D, 128, S), F32),
    "x2b_s": ((NT_D, 128, S), BF16),
    "u1_s": ((NT_D, 128, S), BF16),
    "du1_s": ((NT_D, 128, S), BF16),
    "de1_s": ((NT_D, 128, S), BF16),
    "bc1_s": ((1, 32, S), BF16),
    "x3_s": ((NT_D, 128, S), F32),
    "x3b_s": ((NT_D, 128, S), BF16),
    "wgb_s": ((16, 128, HID), BF16),
    "wub_s": ((16, 128, HID), BF16),
    "wdb_s": ((NSLAB, 128, D), BF16),
    "wob_s": ((32, 128, D), BF16),
    "h_s": ((NSLAB, 128, S), BF16),
    "out_s": ((NT_D, 128, S), F32),
}


def act(E, out, in_, func, bias=0.0, scale=1.0, deps=()):
    def fn(e, out=out, in_=in_, func=func, bias=bias, scale=scale):
        return e.activation(out=out, in_=in_, func=func, bias=bias, scale=scale)

    return E.op("scalar", fn, deps=deps)


def vtt(E, out, in0, in1, alu, deps=(), eng="vector"):
    def fn(e, out=out, in0=in0, in1=in1, alu=alu):
        return e.tensor_tensor(out=out, in0=in0, in1=in1, op=alu)

    return E.op(eng, fn, deps=deps)


def vts(E, out, in0, s1, s2, op0, op1=None, deps=(), eng="vector"):
    def fn(e, out=out, in0=in0, s1=s1, s2=s2, op0=op0, op1=op1):
        if op1 is None:
            return e.tensor_scalar(out=out, in0=in0, scalar1=s1, scalar2=None, op0=op0)
        return e.tensor_scalar(out=out, in0=in0, scalar1=s1, scalar2=s2, op0=op0, op1=op1)

    return E.op(eng, fn, deps=deps)


def vstt(E, out, in0, scalar, in1, op0, op1, deps=(), eng="vector"):
    def fn(e, out=out, in0=in0, scalar=scalar, in1=in1, op0=op0, op1=op1):
        return e.scalar_tensor_tensor(out=out, in0=in0, scalar=scalar, in1=in1,
                                      op0=op0, op1=op1)

    return E.op(eng, fn, deps=deps)


def vcopy(E, out, in_, deps=(), eng="vector"):
    def fn(e, out=out, in_=in_):
        return e.tensor_copy(out=out, in_=in_)

    return E.op(eng, fn, deps=deps)


def vmemset(E, ap, val, deps=(), eng="vector"):
    def fn(e, ap=ap, val=val):
        return e.memset(ap, val)

    return E.op(eng, fn, deps=deps)


def vscan(E, out, d0, d1, initial, deps=()):
    def fn(e, out=out, d0=d0, d1=d1, initial=initial):
        return e.tensor_tensor_scan(out=out, data0=d0, data1=d1, initial=initial,
                                    op0=ALU.mult, op1=ALU.add)

    return E.op("vector", fn, deps=deps)


def vrecip(E, out, in_, deps=()):
    def fn(e, out=out, in_=in_):
        return e.reciprocal(out=out, in_=in_)

    return E.op("vector", fn, deps=deps)


def mm(E, out, lhsT, rhs, start, stop, deps=(), inc=None):
    if inc is None:
        inc = stop

    def fn(e, out=out, lhsT=lhsT, rhs=rhs, start=start, stop=stop):
        return e.matmul(out, lhsT, rhs, start=start, stop=stop)

    return E.op("tensor", fn, deps=deps, inc=inc)


def ptr_barrier(E, deps):
    scr = E.consts["scr"]
    for eng in ("scalar", "vector"):
        E.wait_only(eng, deps)
        for i in range(10):
            if eng == "scalar":
                E.op(eng, lambda e: e.activation(out=scr[0:1, 0:1], in_=scr[0:1, 1:2],
                                                 func=AF.Copy), inc=False)
            else:
                E.op(eng, lambda e: e.tensor_copy(out=scr[0:1, 2:3], in_=scr[0:1, 3:4]),
                     inc=False)


def load_pp_vec(E, dram, name, sb, ncols, deps=()):
    toks = []
    for k in range(ncols):
        src = dram[name][k * 128:(k + 1) * 128].unsqueeze(1)
        toks.append(E.dma("sync", sb[:, k:k + 1], src, deps=deps if k == 0 else (),
                          slow=True))
    return toks


def conv4(E, acc, xbc, cw, m, bias_col, deps):
    TC = acc[0].shape[-1]
    a, b = acc
    if bias_col is not None:
        t0 = vts(E, a, xbc[:, 0:TC], cw[:, m * KC:m * KC + 1], bias_col,
                 ALU.mult, ALU.add, deps=deps)
    else:
        t0 = vts(E, a, xbc[:, 0:TC], cw[:, m * KC:m * KC + 1], None, ALU.mult,
                 deps=deps)
    t1 = vstt(E, b, xbc[:, 1:TC + 1], cw[:, m * KC + 1:m * KC + 2], a, ALU.mult,
              ALU.add, deps=[t0])
    t2 = vstt(E, a, xbc[:, 2:TC + 2], cw[:, m * KC + 2:m * KC + 3], b, ALU.mult,
              ALU.add, deps=[t1])
    t3 = vstt(E, b, xbc[:, 3:TC + 3], cw[:, m * KC + 3:m * KC + 4], a, ALU.mult,
              ALU.add, deps=[t2])
    return t3


def layernorm_tiles(E, stat_bank, bc_bank, tiles, tile_toks, tc, msb, sq, war):
    ones_col = E.consts["ones_col"]
    ones_row = E.consts["ones_row"]
    ntile = len(tiles)
    ntot = ntile * 128
    ps_s = stat_bank[0:1, 0:tc]
    ps_b = bc_bank[:, 0:tc]

    mt1 = None
    for i, tl in enumerate(tiles):
        mt1 = mm(E, ps_s, ones_col[:, 0:1], tl, start=(i == 0), stop=(i == ntile - 1),
                 deps=[tile_toks[i], war.get("stat"), E.consts_tok] if i == 0
                 else [tile_toks[i]], inc=(i == ntile - 1))
    m_row = msb[0:1, 0:tc]
    e2_row = msb[0:1, tc:2 * tc]
    r_row = msb[0:1, 2 * tc:3 * tc]
    t_row = msb[0:1, 3 * tc:4 * tc]
    a = vts(E, m_row, ps_s, 1.0 / ntot, None, ALU.mult, deps=[mt1, war.get("rows")])
    mt2 = None
    sq_toks = []
    for i, tl in enumerate(tiles):
        sqt = act(E, sq[:, 0:tc], tl, AF.Square, deps=[tile_toks[i], war.get("sq")])
        sq_toks.append(sqt)
        mt2 = mm(E, ps_s, ones_col[:, 0:1], sq[:, 0:tc], start=(i == 0),
                 stop=(i == ntile - 1),
                 deps=[sqt, a] if i == 0 else [sqt], inc=True)
        war["sq"] = mt2
    bb = vts(E, e2_row, ps_s, 1.0 / ntot, None, ALU.mult, deps=[mt2])
    war["stat"] = bb
    c1 = vtt(E, t_row, m_row, m_row, ALU.mult, deps=[a])
    c2 = vtt(E, e2_row, e2_row, t_row, ALU.subtract, deps=[bb, c1])
    c3 = act(E, t_row, e2_row, AF.Sqrt, bias=LN_EPS, deps=[c2])
    c4 = vrecip(E, r_row, t_row, deps=[c3])
    bmt = mm(E, ps_b, ones_row[0:1, :], m_row, start=True, stop=True,
             deps=[a, war.get("bc"), E.consts_tok2])
    subs = []
    for i, tl in enumerate(tiles):
        subs.append(vtt(E, tl, tl, ps_b, ALU.subtract,
                        deps=[bmt, mt1, mt2, sq_toks[i]]))
    brt = mm(E, ps_b, ones_row[0:1, :], r_row, start=True, stop=True,
             deps=[c4] + subs)
    out = []
    for i, tl in enumerate(tiles):
        out.append(vtt(E, tl, tl, ps_b, ALU.mult, deps=[brt, subs[i]]))
    war["bc"] = out[-1]
    war["rows"] = [bmt, brt]
    return out


@phase("A1", needs={"x0": "w", "wxT": "w", "wgT": "w"}, produces=["xb_s", "gate_s"])
def p_a1(E, dram, ar, ps):
    E.new_phase()
    xbf = ar.alloc(NT_D * S, BF16)
    wtmp = [ar.alloc(2048, F32) for _ in range(2)]
    whalf = ar.alloc(16 * 2048, BF16)
    xe = [ar.alloc(S, BF16) for _ in range(2)]

    wtmp_war = [None, None]
    ctoks = []
    for k in range(NT_D):
        t = E.dma("sync", wtmp[k % 2], dram["x0"][k * 128:(k + 1) * 128, :],
                  deps=[wtmp_war[k % 2]])
        ct = act(E, xbf[:, k * S:(k + 1) * S], wtmp[k % 2][:], AF.Copy, deps=[t])
        wtmp_war[k % 2] = ct
        ctoks.append(ct)
    xready = ctoks[-1]

    grp = [ps["A"], ps["B"]]
    grp_war = [None, None]
    xe_war = [None, None]
    prev_mm = None

    for wname, oname in (("wxT", "xb_s"), ("wgT", "gate_s")):
        for half in range(2):
            wt = None
            for k in range(NT_D):
                t = E.dma("sync", wtmp[k % 2],
                          dram[wname][k, :, half * 2048:(half + 1) * 2048],
                          deps=[wtmp_war[k % 2]])
                wt = act(E, whalf[:, k * 2048:(k + 1) * 2048], wtmp[k % 2][:],
                         AF.Copy, deps=[t, prev_mm])
                wtmp_war[k % 2] = wt
            for mloc in range(16):
                m = half * 16 + mloc
                gi = mloc % 2
                pp = grp[gi]
                mtok = None
                for k in range(NT_D):
                    for tq in range(4):
                        first = (k == 0 and tq == 0)
                        mtok = mm(E, pp[:, tq * 512:(tq + 1) * 512],
                                  whalf[:, k * 2048 + mloc * 128: k * 2048 + mloc * 128 + 128],
                                  xbf[:, k * S + tq * 512: k * S + tq * 512 + 512],
                                  start=(k == 0), stop=(k == NT_D - 1),
                                  deps=[wt, xready, grp_war[gi]] if first else (),
                                  inc=(k == NT_D - 1 and tq == 3))
                prev_mm = mtok
                fu = AF.Silu if oname == "gate_s" else AF.Copy
                ev = act(E, xe[gi], pp[:], fu, deps=[mtok, xe_war[gi]])
                grp_war[gi] = ev
                xe_war[gi] = E.dma("gpsimd", dram[oname][m], xe[gi], deps=[ev])
    if "dbg_s" in dram:
        for k in range(NT_D):
            E.dma("gpsimd", dram["dbg_s"][k], xbf[:, k * S:(k + 1) * S],
                  deps=[prev_mm])


def proj_phase(E, dram, ar, ps, *, ntile, nheads, src, cwn, cbn, xpn, dtn, dtbn,
               u_s, du_s, de_s, bc_s, silu):
    E.new_phase()
    TC = 512
    NCH = S // TC
    tph = ntile // nheads

    cw = ar.alloc(ntile * KC, F32)
    cb = ar.alloc(ntile, F32)
    dtb = ar.alloc(ntile, F32)
    xpb = ar.alloc(nheads * 16 * 160, BF16)
    dttb = ar.alloc(nheads * HD, BF16)
    wtmp = ar.alloc(2048, F32)
    xbc = [ar.alloc(TC + KC, BF16) for _ in range(2)]
    acc = [ar.alloc(TC, F32) for _ in range(2)]
    ubf = ar.alloc(ntile * TC, BF16)
    dtinb = ar.alloc(TC, BF16)
    bct = ar.alloc(TC, BF16)
    t_r = [ar.alloc(TC, F32) for _ in range(2)]
    t_a = [ar.alloc(TC, F32) for _ in range(2)]
    t_l = [ar.alloc(TC, F32) for _ in range(2)]
    de_t = [ar.alloc(TC, BF16) for _ in range(2)]
    du_t = [ar.alloc(TC, BF16) for _ in range(2)]

    vready = []
    for k in range(ntile):
        vready.append(E.dma("sync", cw[:, k * KC:(k + 1) * KC],
                            dram[cwn][k * 128:(k + 1) * 128, :]))
    vready += load_pp_vec(E, dram, cbn, cb, ntile)
    vready += load_pp_vec(E, dram, dtbn, dtb, ntile)
    wtok = None
    for h in range(nheads):
        for k in range(16):
            src_ap = dram[xpn][h, k] if nheads > 1 else dram[xpn][k]
            t = E.dma("sync", wtmp[:, 0:160], src_ap, deps=[wtok])
            wtok = act(E, xpb[:, (h * 16 + k) * 160:(h * 16 + k + 1) * 160],
                       wtmp[:, 0:160], AF.Copy, deps=[t])
    for h in range(nheads):
        src_ap = dram[dtn][h] if nheads > 1 else dram[dtn][:]
        t = E.dma("sync", wtmp, src_ap, deps=[wtok])
        wtok = act(E, dttb[:, h * HD:(h + 1) * HD], wtmp[:], AF.Copy, deps=[t])
    wready = wtok
    ptr_barrier(E, vready)

    ps_dt = ps["A"][:, 0:512]
    ps_bc = ps["A"][0:32, 512:1024]
    ps_dp = [ps["B"][:, 0:512], ps["B"][:, 512:1024]]
    ps_dt_war = None
    ps_bc_war = None
    ps_dp_war = [None, None]
    dtinb_war = None
    bct_war = None
    chain_war = [None, None]
    dd_war = [None, None]

    u_war = [None] * ntile
    xbc_war = [None, None]
    acc_war = None
    for c in range(NCH):
        utoks = [None] * ntile
        ustoks = [None] * ntile
        for m in range(ntile):
            bslot = m % 2
            if c == 0:
                mz = vmemset(E, xbc[bslot][:, 0:KC - 1], 0.0, deps=[xbc_war[bslot]])
                lt = E.dma("sync", xbc[bslot][:, KC - 1:KC - 1 + TC],
                           dram[src][m][:, 0:TC], deps=[mz])
            else:
                lt = E.dma("sync", xbc[bslot][:, 0:TC + KC - 1],
                           dram[src][m][:, c * TC - (KC - 1): c * TC + TC],
                           deps=[xbc_war[bslot]])
            t3 = conv4(E, acc, xbc[bslot], cw, m,
                       None if silu else cb[:, m:m + 1],
                       [lt, vready, u_war[m], acc_war])
            xbc_war[bslot] = t3
            fu = AF.Silu if silu else AF.Copy
            bias = cb[:, m:m + 1] if silu else 0.0
            ub = act(E, ubf[:, m * TC:(m + 1) * TC], acc[1][:, :], fu, bias=bias,
                     deps=[t3])
            acc_war = [ub]
            ustoks[m] = ub
            utoks[m] = ub
            E.dma("gpsimd", dram[u_s][m][:, c * TC:(c + 1) * TC],
                  ubf[:, m * TC:(m + 1) * TC], deps=[ub])
            E.pump()
        prev_head_mt = None
        for h in range(nheads):
            mt_dt = None
            mt_bc = None
            for k in range(tph):
                tile = h * tph + k
                base = (h * 16 + k) * 160 if nheads > 1 else k * 160
                first_deps = [wready, ps_dt_war, ps_bc_war] + utoks if k == 0 else ()
                mt_dt = mm(E, ps_dt, xpb[:, base:base + 128],
                           ubf[:, tile * TC:(tile + 1) * TC],
                           start=(k == 0), stop=(k == tph - 1), deps=first_deps,
                           inc=(k == tph - 1))
                mt_bc = mm(E, ps_bc, xpb[:, base + 128:base + 160],
                           ubf[:, tile * TC:(tile + 1) * TC],
                           start=(k == 0), stop=(k == tph - 1), inc=(k == tph - 1))
            ev_dt = act(E, dtinb[:], ps_dt, AF.Copy, deps=[mt_dt, dtinb_war])
            ps_dt_war = ev_dt
            ev_bc = vcopy(E, bct[0:32, :], ps_bc, deps=[mt_bc, bct_war])
            ps_bc_war = ev_bc
            bdst = dram[bc_s][h] if nheads > 1 else dram[bc_s][0]
            bct_war = E.dma("gpsimd", bdst[:, c * TC:(c + 1) * TC], bct[0:32, :],
                            deps=[ev_bc])
            for mi in range(tph):
                tile = h * tph + mi
                slot = mi % 2
                mt = mm(E, ps_dp[slot],
                        dttb[:, h * HD + mi * 128: h * HD + (mi + 1) * 128],
                        dtinb[:], start=True, stop=True,
                        deps=[ev_dt, ps_dp_war[slot]])
                prev_head_mt = mt
                rr = act(E, t_r[slot], ps_dp[slot], AF.Relu,
                         bias=dtb[:, tile:tile + 1], deps=[mt, chain_war[slot]])
                aa = act(E, t_a[slot], ps_dp[slot], AF.Abs,
                         bias=dtb[:, tile:tile + 1], deps=[mt])
                ps_dp_war[slot] = aa
                ee = act(E, t_a[slot], t_a[slot][:], AF.Exp, scale=-1.0, deps=[aa])
                ll = act(E, t_l[slot], t_a[slot][:], AF.Ln, bias=1.0, deps=[ee])
                dd = vtt(E, de_t[slot], t_r[slot][:], t_l[slot][:], ALU.add,
                         deps=[ll, rr, dd_war[slot]])
                uu = vtt(E, du_t[slot], de_t[slot][:],
                         ubf[:, tile * TC:(tile + 1) * TC], ALU.mult,
                         deps=[dd, ustoks[tile]])
                chain_war[slot] = dd
                d1 = E.dma("gpsimd", dram[de_s][tile][:, c * TC:(c + 1) * TC],
                           de_t[slot], deps=[dd])
                d2 = E.dma("gpsimd", dram[du_s][tile][:, c * TC:(c + 1) * TC],
                           du_t[slot], deps=[uu])
                dd_war[slot] = [d1, d2]
                u_war[tile] = [uu, d1, d2, mt_dt, mt_bc]
            dtinb_war = prev_head_mt


@phase("A2",
       needs={"xb_s": "s", "cw0": "w", "cb0": "w", "xp0T": "w", "dt0T": "w", "dtb0": "w"},
       produces=["u0_s", "du0_s", "de0_s", "bc0_s"])
def p_a2(E, dram, ar, ps):
    proj_phase(E, dram, ar, ps, ntile=NT_I, nheads=NH, src="xb_s", cwn="cw0",
               cbn="cb0", xpn="xp0T", dtn="dt0T", dtbn="dtb0", u_s="u0_s",
               du_s="du0_s", de_s="de0_s", bc_s="bc0_s", silu=True)


@phase("A2L1",
       needs={"x2b_s": "s", "cw1": "w", "cb1": "w", "xp1T": "w", "dt1T": "w", "dtb1": "w"},
       produces=["u1_s", "du1_s", "de1_s", "bc1_s"])
def p_a2l1(E, dram, ar, ps):
    proj_phase(E, dram, ar, ps, ntile=NT_D, nheads=1, src="x2b_s", cwn="cw1",
               cbn="cb1", xpn="xp1T", dtn="dt1T", dtbn="dtb1", u_s="u1_s",
               du_s="du1_s", de_s="de1_s", bc_s="bc1_s", silu=False)


def scan_core(E, dram, ar, ps, *, ntile, nheads, u_s, du_s, de_s, bc_s, al, dvec,
              gate_s=None, ybf_s=None, post=None, TC=512, chunk_cb=None):
    tph = ntile // nheads
    NCH = S // TC

    asc = ar.alloc(ntile * NST, F32)
    dsb = ar.alloc(ntile, F32)
    carry = ar.alloc(ntile * NST, F32)
    Bt = ar.alloc(NST * TC, BF16)
    Ct = ar.alloc(NST * TC, BF16)
    dA = ar.alloc(NST * TC, BF16)
    dBu = ar.alloc(NST * TC, BF16)
    hh = ar.alloc(NST * TC, BF16)
    trA = ar.alloc(8 * TC, BF16)
    trB = ar.alloc(4 * TC, BF16)
    yv = ar.alloc(TC, F32)
    de_t = [ar.alloc(TC, BF16) for _ in range(2)]
    du_t = [ar.alloc(TC, BF16) for _ in range(2)]
    u_t = [ar.alloc(TC, BF16) for _ in range(2)]
    gt = [ar.alloc(TC, BF16) for _ in range(2)] if gate_s else None
    ybt = ar.alloc(TC, BF16)

    atoks = []
    for k in range(ntile):
        atoks.append(E.dma("sync", asc[:, k * NST:(k + 1) * NST],
                           dram[al][k * 128:(k + 1) * 128, :]))
    t = act(E, asc, asc[:], AF.Exp, deps=atoks)

    def _neg(e, out=asc, in_=asc):
        return e.mul(out=out, in_=in_[:], mul=-1.0)

    atok = E.op("scalar", _neg, deps=[t])
    dtok = load_pp_vec(E, dram, dvec, dsb, ntile)
    ptr_barrier(E, [atok] + dtok)
    carry_tok = [None] * ntile

    war = {}
    scan_last = atok
    tree_last = None
    y_last = None
    for c in range(NCH):
        chunk_stores = []
        for h in range(nheads):
            bsrc = dram[bc_s][h]
            bt = E.dma("sync", Bt[:].rearrange("p (n t) -> p n t", n=NST),
                       bsrc[0:NST, c * TC:(c + 1) * TC].unsqueeze(0)
                       .broadcast_to((128, NST, TC)),
                       deps=[war.get("Bt")])
            ct2 = E.dma("sync", Ct[:].rearrange("p (n t) -> p n t", n=NST),
                        bsrc[NST:2 * NST, c * TC:(c + 1) * TC].unsqueeze(0)
                        .broadcast_to((128, NST, TC)),
                        deps=[war.get("Ct")])
            for mi in range(tph):
                tile = h * tph + mi
                slot = mi % 2
                ld = E.dma("sync", de_t[slot], dram[de_s][tile][:, c * TC:(c + 1) * TC],
                           deps=[war.get(("de", slot))])
                lu = E.dma("sync", du_t[slot], dram[du_s][tile][:, c * TC:(c + 1) * TC],
                           deps=[war.get(("du", slot))])
                lt = E.dma("sync", u_t[slot], dram[u_s][tile][:, c * TC:(c + 1) * TC],
                           deps=[war.get(("ut", slot))])
                if gate_s:
                    lg = E.dma("sync", gt[slot],
                               dram[gate_s][tile][:, c * TC:(c + 1) * TC],
                               deps=[war.get(("gt", slot))])
                extoks = []
                for n in range(NST):
                    extoks.append(act(E, dA[:, n * TC:(n + 1) * TC], de_t[slot][:],
                                      AF.Exp,
                                      scale=asc[:, tile * NST + n: tile * NST + n + 1],
                                      deps=[ld, atok, scan_last]))
                war[("de", slot)] = extoks[-1]
                du3 = du_t[slot][:].unsqueeze(1).broadcast_to((128, NST, TC))
                dbu_tok = vtt(E, dBu[:].rearrange("p (n t) -> p n t", n=NST), du3,
                              Bt[:].rearrange("p (n t) -> p n t", n=NST), ALU.mult,
                              deps=[lu, bt, tree_last], eng="gpsimd")
                war[("du", slot)] = dbu_tok
                war["Bt"] = dbu_tok
                stoks = []
                for n in range(NST):
                    init = 0.0 if c == 0 else \
                        carry[:, tile * NST + n: tile * NST + n + 1]
                    stoks.append(vscan(E, hh[:, n * TC:(n + 1) * TC],
                                       dA[:, n * TC:(n + 1) * TC],
                                       dBu[:, n * TC:(n + 1) * TC],
                                       init,
                                       deps=[extoks[n], dbu_tok, carry_tok[tile]]))
                scan_last = stoks[-1]
                hl = hh[:].rearrange("p (n t) -> p n t", n=NST)[:, :, TC - 1:TC]
                cext = vcopy(E, carry[:, tile * NST:(tile + 1) * NST].unsqueeze(2),
                             hl, deps=stoks)
                carry_tok[tile] = cext
                gtok = vtt(E, dBu, hh[:], Ct[:], ALU.mult, deps=stoks + [ct2, cext])
                war["Ct"] = gtok
                r1 = vtt(E, trA[:, 0:8 * TC], dBu[:, 0:8 * TC], dBu[:, 8 * TC:16 * TC],
                         ALU.add, deps=[gtok, war.get("trA")])
                r2 = vtt(E, trB[:, 0:4 * TC], trA[:, 0:4 * TC], trA[:, 4 * TC:8 * TC],
                         ALU.add, deps=[r1, war.get("trB")])
                r3 = vtt(E, trA[:, 0:2 * TC], trB[:, 0:2 * TC], trB[:, 2 * TC:4 * TC],
                         ALU.add, deps=[r2])
                war["trB"] = r3
                r4 = vtt(E, yv, trA[:, 0:TC], trA[:, TC:2 * TC], ALU.add, deps=[r3])
                war["trA"] = r4
                tree_last = r1
                y2 = vstt(E, yv, u_t[slot][:], dsb[:, tile:tile + 1], yv[:],
                          ALU.mult, ALU.add, deps=[r4, lt, dtok])
                war[("ut", slot)] = y2
                if gate_s:
                    y3 = vtt(E, ybt, yv[:], gt[slot][:], ALU.mult, deps=[y2, lg, y_last])
                    war[("gt", slot)] = y3
                    y_last = E.dma("gpsimd", dram[ybf_s][tile][:, c * TC:(c + 1) * TC],
                                   ybt, deps=[y3])
                    chunk_stores.append(y_last)
                else:
                    y_last = post(c, tile, yv, y2)
                E.pump()
                E.pump()
        if chunk_cb is not None:
            chunk_cb(c, chunk_stores)


@phase("B0C",
       needs={"u0_s": "s", "du0_s": "s", "de0_s": "s", "bc0_s": "s", "gate_s": "s",
              "wob_s": "s", "al0": "w", "d0": "w", "x0": "w"},
       produces=["ybf_s", "x1_s", "x1b_s"])
def p_b0c(E, dram, ar, ps):
    E.new_phase()
    TC = 512
    ybc = ar.alloc(32 * 512, BF16)
    wbuf = [ar.alloc(4 * 1024, BF16) for _ in range(2)]
    xr = [ar.alloc(512, F32) for _ in range(NT_D)]
    xtmp = [ar.alloc(512, F32) for _ in range(2)]
    x1b = ar.alloc(512, BF16)
    msb = ar.alloc(4 * 512, F32)
    sq = ar.alloc(512, F32)
    banks = [ps["A"][:, i * 512:(i + 1) * 512] for i in range(4)] + \
            [ps["B"][:, i * 512:(i + 1) * 512] for i in range(4)]
    st = {"war": {}, "xr_war": [[] for _ in range(NT_D)], "bank_war": [None] * 8,
          "wbuf_war": [None, None], "ybc_war": None, "xt_war": [None, None]}

    def c_chunk(c, ytoks):
        steps = []
        steps.append(lambda c=c, ytoks=ytoks: c_ybc(c, ytoks))
        for mh in range(2):
            for kg in range(8):
                steps.append(lambda c=c, mh=mh, kg=kg: c_kg(c, mh, kg))
            steps.append(lambda c=c, mh=mh: c_evac(c, mh))
        steps.append(lambda c=c: c_post(c))
        if E.pump_steps:
            E.pump_steps.extend(steps)
        else:
            E.pump_steps = steps

    def c_ybc(c, ytoks):
        ylds = []
        for k in range(32):
            ylds.append(E.dma("scalar", ybc[:, k * 512:(k + 1) * 512],
                              dram["ybf_s"][k][:, c * TC:(c + 1) * TC],
                              deps=[ytoks[k], st["ybc_war"]]))
        st["ylds"] = ylds
        st["etoks"] = []

    def c_kg(c, mh, kg):
        slot = kg % 2
        ylds = st["ylds"]
        wlds = []
        for j in range(4):
            k = kg * 4 + j
            wlds.append(E.dma("scalar",
                              wbuf[slot][:, j * 1024:(j + 1) * 1024],
                              dram["wob_s"][k][:, mh * 1024:(mh + 1) * 1024],
                              deps=[st["wbuf_war"][slot]] if j == 0 else ()))
        last_mm = None
        for j in range(4):
            k = kg * 4 + j
            for mloc in range(8):
                deps = ()
                if mloc == 0:
                    deps = [wlds[j], ylds[k]]
                if k == 0:
                    deps = list(deps) + [st["bank_war"][mloc]]
                last_mm = mm(E, banks[mloc],
                             wbuf[slot][:, j * 1024 + mloc * 128:
                                        j * 1024 + mloc * 128 + 128],
                             ybc[:, k * 512:(k + 1) * 512],
                             start=(k == 0), stop=(k == 31),
                             deps=deps, inc=(j == 3 and mloc == 7))
        st["wbuf_war"][slot] = last_mm
        st["last_mm"] = last_mm

    def c_evac(c, mh):
        for mloc in range(8):
            m = mh * 8 + mloc
            ev = act(E, xr[m], banks[mloc], AF.Copy,
                     deps=[st["last_mm"]] + st["xr_war"][m])
            st["bank_war"][mloc] = ev
            st["etoks"].append(ev)
        if mh == 1:
            st["ybc_war"] = st["last_mm"]

    def c_post(c):
        etoks = st["etoks"]
        war = st["war"]
        ttoks = []
        for m in range(NT_D):
            lx0 = E.dma("scalar", xtmp[m % 2], dram["x0"][m * 128:(m + 1) * 128,
                                                          c * TC:(c + 1) * TC],
                        deps=[st["xt_war"][m % 2]])
            at = vtt(E, xr[m], xr[m][:], xtmp[m % 2][:], ALU.add,
                     deps=[lx0, etoks[m]])
            st["xt_war"][m % 2] = at
            ttoks.append(at)
        ws = war.get("stat")
        war["stat"] = [etoks[8]] + (ws if isinstance(ws, list) else [ws])
        wb2 = war.get("bc")
        war["bc"] = [etoks[9]] + (wb2 if isinstance(wb2, list) else [wb2])
        lnt = layernorm_tiles(E, banks[0], banks[1],
                              [xr[i][:] for i in range(NT_D)], ttoks, TC, msb, sq,
                              war)
        st["bank_war"][0] = [st["bank_war"][0], lnt[-1]]
        st["bank_war"][1] = [st["bank_war"][1], lnt[-1]]
        for i in range(NT_D):
            bt = act(E, x1b, xr[i][:], AF.Copy, deps=[lnt[i], war.get("x1b")])
            da = E.dma("gpsimd", dram["x1_s"][i][:, c * TC:(c + 1) * TC], xr[i],
                       deps=[lnt[i]])
            db = E.dma("gpsimd", dram["x1b_s"][i][:, c * TC:(c + 1) * TC], x1b,
                       deps=[bt])
            war["x1b"] = db
            st["xr_war"][i] = [bt, da, db]

    scan_core(E, dram, ar, ps, ntile=NT_I, nheads=NH, u_s="u0_s", du_s="du0_s",
              de_s="de0_s", bc_s="bc0_s", al="al0", dvec="d0",
              gate_s="gate_s", ybf_s="ybf_s", chunk_cb=c_chunk)
    while E.pump_steps:
        E.pump()



@phase("F0", needs={"x1_s": "s", "x1b_s": "s", "fw0": "w", "fs0": "w"},
       produces=["x2_s", "x2b_s"])
def p_f0(E, dram, ar, ps):
    E.new_phase()
    TC = 512
    NCH = S // TC
    wtmp = [ar.alloc(2048, F32) for _ in range(2)]
    wb = ar.alloc(16 * 2048, BF16)
    xb = ar.alloc(16 * 2048, BF16)
    ssb = ar.alloc(16, F32)
    oc = [ar.alloc(512, F32) for _ in range(NT_D)]
    xtmp = [ar.alloc(512, F32) for _ in range(2)]
    x2b = ar.alloc(512, BF16)
    msb = ar.alloc(4 * 512, F32)
    sq = ar.alloc(512, F32)
    rowtmp = ar.alloc(7 * 512, F32)

    stok = load_pp_vec(E, dram, "fs0", ssb, NT_D)
    ptr_barrier(E, stok)
    wtmp_war = [None, None]
    wtoks = None
    for k in range(16):
        t = E.dma("sync", wtmp[k % 2], dram["fw0"][k], deps=[wtmp_war[k % 2]])
        wtoks = act(E, wb[:, k * 2048:(k + 1) * 2048], wtmp[k % 2][:], AF.Copy,
                    deps=[t])
        wtmp_war[k % 2] = wtoks
    xtoks = [None] * NT_D
    for k in range(16):
        xtoks[k] = E.dma("sync", xb[:, k * 2048:(k + 1) * 2048], dram["x1b_s"][k])

    banks = [ps["A"][:, i * 512:(i + 1) * 512] for i in range(4)] + \
            [ps["B"][:, 0:512], ps["B"][:, 512:1024]]
    stat_bank = ps["B"][:, 1024:1536]
    bc_bank = ps["B"][:, 1536:2048]
    bank_war = [None] * 6
    war = {}
    oc_war = [None] * NT_D
    ones_col = E.consts["ones_col"]
    ones_row = E.consts["ones_row"]

    for c in range(NCH):
        etoks = []
        for m in range(NT_D):
            bk = m % 6
            mtok = None
            for k in range(16):
                mtok = mm(E, banks[bk],
                          wb[:, k * 2048 + m * 128: k * 2048 + m * 128 + 128],
                          xb[:, k * 2048 + c * TC: k * 2048 + c * TC + TC],
                          start=(k == 0), stop=(k == 15),
                          deps=[wtoks, bank_war[bk]] + xtoks + stok
                          if k == 0 else (),
                          inc=(k == 15))
            ev = act(E, oc[m], banks[bk], AF.Copy, scale=ssb[:, m:m + 1],
                     deps=[mtok, oc_war[m]])
            bank_war[bk] = ev
            etoks.append(ev)
        mt2 = None
        for m in range(NT_D):
            sqt = act(E, sq, oc[m][:, :], AF.Square, deps=[etoks[m], war.get("sq")])
            mt2 = mm(E, stat_bank[0:1, 0:TC], ones_col[:, 0:1], sq[:, 0:TC],
                     start=(m == 0), stop=(m == NT_D - 1),
                     deps=[sqt, war.get("stat"), E.consts_tok] if m == 0 else [sqt],
                     inc=True)
            war["sq"] = mt2
        vrow = rowtmp[0:1, 0:512]
        frow = rowtmp[0:1, 512:1024]
        trow = rowtmp[0:1, 1024:1536]
        grow = rowtmp[0:1, 1536:2048]
        hrow = rowtmp[0:1, 2048:2560]
        xrow = rowtmp[0:1, 2560:3072]
        srow = rowtmp[0:1, 3072:3584]
        v0 = vts(E, srow, stat_bank[0:1, 0:TC], 1.0, None, ALU.mult,
                 deps=[mt2, war.get("rows2")])
        v1 = act(E, xrow, srow, AF.Sqrt, deps=[v0])
        v2 = vts(E, vrow, xrow, FFN_EPS, None, ALU.max, deps=[v1])
        war["stat"] = v0
        th = act(E, trow, vrow, AF.Tanh, deps=[v2])
        dn = vts(E, xrow, vrow, FFN_EPS, None, ALU.add, deps=[v2, v1])
        rc = vrecip(E, grow, xrow, deps=[dn])
        f1 = vtt(E, frow, trow, grow, ALU.mult, deps=[th, rc])
        on = vtt(E, xrow, frow, vrow, ALU.mult, deps=[f1, rc])
        on2 = vts(E, trow, xrow, FFN_EPS, None, ALU.max, deps=[on, th])
        rc2 = vrecip(E, xrow, trow, deps=[on2])
        g1 = vts(E, grow, xrow, 0.99, 1.0, ALU.mult, ALU.min, deps=[rc2])
        hf = vtt(E, hrow, frow, grow, ALU.mult, deps=[g1, f1])
        bht = mm(E, bc_bank[:, 0:TC], ones_row[0:1, :], hrow, start=True, stop=True,
                 deps=[hf, war.get("bc"), E.consts_tok2])
        war["rows2"] = [bht, hf]
        ttoks = []
        for m in range(NT_D):
            ap1 = vtt(E, oc[m], oc[m][:], bc_bank[:, 0:TC], ALU.mult,
                      deps=[bht, etoks[m]])
            lx = E.dma("sync", xtmp[m % 2], dram["x1_s"][m][:, c * TC:(c + 1) * TC],
                       deps=[war.get(("xt", m % 2))])
            at = vtt(E, oc[m], oc[m][:], xtmp[m % 2][:], ALU.add, deps=[ap1, lx])
            war[("xt", m % 2)] = at
            ttoks.append(at)
        war["bc"] = ttoks[-1]
        lnt = layernorm_tiles(E, stat_bank, bc_bank,
                              [oc[i][:] for i in range(NT_D)], ttoks, TC, msb, sq,
                              war)
        for i in range(NT_D):
            bt = act(E, x2b, oc[i][:], AF.Copy, deps=[lnt[i], war.get("x2b")])
            da = E.dma("gpsimd", dram["x2_s"][i][:, c * TC:(c + 1) * TC], oc[i],
                       deps=[lnt[i]])
            db = E.dma("gpsimd", dram["x2b_s"][i][:, c * TC:(c + 1) * TC], x2b,
                       deps=[bt])
            war["x2b"] = db
            oc_war[i] = [bt, da, db]


@phase("W1", needs={"wqgT": "w", "wquT": "w", "wqdT": "w", "woT": "w"},
       produces=["wgb_s", "wub_s", "wdb_s", "wob_s"])
def p_w1(E, dram, ar, ps):
    HW = HID // 2
    it = [ar.alloc(HW, I32) for _ in range(2)]
    bt = [ar.alloc(HW, BF16) for _ in range(2)]
    ar.base = ar.off
    state = {"war": [None, None], "dtok": [None, None], "i": 0}
    jobs = []
    for srcn, dst, nstrip, width in [("woT", "wob_s", 32, D),
                                     ("wqgT", "wgb_s", 16, HID),
                                     ("wquT", "wub_s", 16, HID),
                                     ("wqdT", "wdb_s", NSLAB, D)]:
        nh = (width + HW - 1) // HW
        for k in range(nstrip):
            for hwi in range(nh):
                lo = hwi * HW
                hi = min(width, lo + HW)
                jobs.append((srcn, dst, k, lo, hi))

    def make_step(j):
        srcn, dst, k, lo, hi = j

        def step():
            slot = state["i"] % 2
            state["i"] += 1
            n = hi - lo
            if srcn == "woT":
                src_i = it[slot][:, 0:n].bitcast(F32)
                bias = 0.0
            else:
                src_i = it[slot][:, 0:n]
                bias = -128.0
            dst_b = bt[slot][:, 0:n]
            t = E.dma("sync", src_i, dram[srcn][k][:, lo:hi],
                      deps=[state["war"][slot]])
            cv = act(E, dst_b, src_i, AF.Copy, bias=bias,
                     deps=[t, state["dtok"][slot]])
            state["war"][slot] = cv
            state["dtok"][slot] = E.dma("gpsimd", dram[dst][k][:, lo:hi],
                                        dst_b, deps=[cv])

        return step

    E.pump_steps = [make_step(j) for j in jobs]


@phase("B1F1",
       needs={"u1_s": "s", "du1_s": "s", "de1_s": "s", "bc1_s": "s", "x2_s": "s",
              "wgb_s": "s", "wub_s": "s", "wdb_s": "s",
              "al1": "w", "d1": "w", "sg": "w", "su": "w", "sd": "w"},
       produces=["x3_s", "x3b_s", "h_s", "out_s"])
def p_b1f1(E, dram, ar, ps):
    E.new_phase()
    TC = 256
    TB = 1024
    NCH = S // TC
    yc = [ar.alloc(TC, F32) for _ in range(NT_D)]
    xtmp = [ar.alloc(512, F32) for _ in range(2)]
    x3b = ar.alloc(TC, BF16)
    msb = ar.alloc(4 * 512, F32)
    sq = ar.alloc(512, F32)
    sgb = ar.alloc(NSLAB, F32)
    sub = ar.alloc(NSLAB, F32)
    sdb = ar.alloc(NT_D, F32)
    x3bb = ar.alloc(16 * TB, BF16)
    wgt = [ar.alloc(16 * 128, BF16) for _ in range(2)]
    wut = [ar.alloc(16 * 128, BF16) for _ in range(2)]
    wdt = [ar.alloc(512, BF16) for _ in range(4)]
    hdt = [ar.alloc(512, BF16) for _ in range(4)]
    gs = ar.alloc(TB, BF16)
    us = ar.alloc(TB, BF16)
    htmp = [ar.alloc(TB, BF16) for _ in range(2)]
    oc = [ar.alloc(512, F32) for _ in range(NT_D)]
    fxtmp = [ar.alloc(512, F32) for _ in range(2)]

    svec = load_pp_vec(E, dram, "sg", sgb, NSLAB)
    svec += load_pp_vec(E, dram, "su", sub, NSLAB)
    svec += load_pp_vec(E, dram, "sd", sdb, NT_D)
    ptr_barrier(E, svec)

    banksA = [ps["A"][:, i * 512:(i + 1) * 512] for i in range(4)]
    banksB = [ps["B"][:, i * 512:(i + 1) * 512] for i in range(4)]

    st = {"war": {}, "ttoks": [None] * NT_D, "yc_war": [None] * NT_D,
          "x3db": [[] for _ in range(NCH)],
          "hb_war": [None] * NSLAB, "grp_war": [None, None], "gs_war": None,
          "wg_war": [None, None], "wd_war": [None] * 4,
          "hd_war": [None] * 4,
          "x3bb_war": [], "xt_war": [None, None],
          "abank": [None] * 4, "oc_war": [[] for _ in range(NT_D)],
          "fwar": {}, "out_last": None}
    war = st["war"]

    def post(c, tile, yv, y2tok):
        lx = E.dma("scalar", xtmp[tile % 2][:, 0:TC],
                   dram["x2_s"][tile][:, c * TC:(c + 1) * TC],
                   deps=[st["xt_war"][tile % 2]])
        at = vtt(E, yc[tile], yv[:], xtmp[tile % 2][:, 0:TC], ALU.add,
                 deps=[y2tok, lx, st["yc_war"][tile]])
        st["xt_war"][tile % 2] = at
        st["ttoks"][tile] = at
        if tile == NT_D - 1:
            lnt = layernorm_tiles(E, banksB[2], banksB[3],
                                  [yc[i][:] for i in range(NT_D)], st["ttoks"],
                                  TC, msb, sq, war)
            last = None
            for i in range(NT_D):
                bt = act(E, x3b, yc[i][:], AF.Copy, deps=[lnt[i], war.get("x3b")])
                da = E.dma("gpsimd", dram["x3_s"][i][:, c * TC:(c + 1) * TC],
                           yc[i], deps=[lnt[i]])
                db = E.dma("gpsimd", dram["x3b_s"][i][:, c * TC:(c + 1) * TC],
                           x3b, deps=[bt])
                war["x3b"] = db
                st["yc_war"][i] = [bt, da, db]
                st["x3db"][c].append(db)
                last = db
            return last
        return at

    def f_x3load(blk):
        toks = []
        nb = NCH // 2
        dbs = [t for c in range(blk * nb, blk * nb + nb) for t in st["x3db"][c]]
        for k in range(NT_D):
            toks.append(E.dma("sync", x3bb[:, k * TB:(k + 1) * TB],
                              dram["x3b_s"][k][:, blk * TB:(blk + 1) * TB],
                              deps=(dbs if k == 0 else []) + st["x3bb_war"]))
        st["xtk"] = toks

    def f_wload(blk, slab):
        slot = slab % 2
        wg_l, wu_l = [], []
        for k in range(NT_D):
            wg_l.append(E.dma("sync", wgt[slot][:, k * 128:(k + 1) * 128],
                              dram["wgb_s"][k][:, slab * 128:(slab + 1) * 128],
                              deps=[st["wg_war"][slot]] if k == 0 else ()))
            wu_l.append(E.dma("sync", wut[slot][:, k * 128:(k + 1) * 128],
                              dram["wub_s"][k][:, slab * 128:(slab + 1) * 128]))
        st["wg_l"], st["wu_l"] = wg_l, wu_l

    def f_slab(blk, slab):
        slot = slab % 2
        wg_l, wu_l = st["wg_l"], st["wu_l"]
        mg = None
        mu = None
        for k in range(NT_D):
            for th in range(2):
                deps = [wg_l[k], st["xtk"][k]] if th == 0 else ()
                if k == 0:
                    deps = list(deps) + [st["abank"][th]]
                mg = mm(E, banksA[th],
                        wgt[slot][:, k * 128:(k + 1) * 128],
                        x3bb[:, k * TB + th * 512: k * TB + th * 512 + 512],
                        start=(k == 0), stop=(k == NT_D - 1),
                        deps=deps, inc=(k == NT_D - 1 and th == 1))
            for th in range(2):
                deps = [wu_l[k]] if th == 0 else ()
                if k == 0:
                    deps = list(deps) + [st["abank"][2 + th]]
                mu = mm(E, banksA[2 + th],
                        wut[slot][:, k * 128:(k + 1) * 128],
                        x3bb[:, k * TB + th * 512: k * TB + th * 512 + 512],
                        start=(k == 0), stop=(k == NT_D - 1),
                        deps=deps, inc=(k == NT_D - 1 and th == 1))
        pp = ps["A"]
        ge = act(E, gs, pp[:, 0:1024], AF.Silu, scale=sgb[:, slab:slab + 1],
                 deps=[mg, st["gs_war"]])
        ue = act(E, us, pp[:, 1024:2048], AF.Copy, scale=sub[:, slab:slab + 1],
                 deps=[mu])
        ht = vtt(E, htmp[slot], gs[:], us[:], ALU.mult,
                 deps=[ge, ue, st["hb_war"][slab]])
        hd = E.dma("gpsimd", dram["h_s"][slab][:, blk * TB:(blk + 1) * TB],
                   htmp[slot], deps=[ht])
        st["gs_war"] = ht
        st["hb_war"][slab] = hd
        st["abank"][0] = ge
        st["abank"][1] = ge
        st["abank"][2] = ue
        st["abank"][3] = ue
        st["wg_war"][slot] = ht

    def f_down(blk, th, q, kg):
        mt = None
        lds = {}
        mtk = {}

        def issue_load(ki):
            k = kg * 11 + ki
            slot = k % 4
            dep = mtk.get(k - 4, st["wd_war"][slot])
            wl = E.dma("sync", wdt[slot],
                       dram["wdb_s"][k][:, q * 512:(q + 1) * 512], deps=[dep])
            hl = E.dma("sync", hdt[slot],
                       dram["h_s"][k][:, blk * TB + th * 512: blk * TB + th * 512 + 512],
                       deps=[st["hb_war"][k], dep])
            lds[k] = (wl, hl)

        for ki in range(4):
            issue_load(ki)
        for ki in range(11):
            k = kg * 11 + ki
            slot = k % 4
            wl, hl = lds[k]
            for mloc in range(4):
                deps = [wl, hl] if mloc == 0 else ()
                if k == 0:
                    deps = list(deps) + [st["abank"][mloc]] + svec
                mt = mm(E, banksA[mloc],
                        wdt[slot][:, mloc * 128:(mloc + 1) * 128],
                        hdt[slot][:],
                        start=(k == 0), stop=(k == NSLAB - 1),
                        deps=deps, inc=(mloc == 3))
            mtk[k] = mt
            st["wd_war"][slot] = mt
            st["hd_war"][slot] = mt
            if ki + 4 < 11:
                issue_load(ki + 4)
        if kg == 3:
            for mloc in range(4):
                m = q * 4 + mloc
                ev = act(E, oc[m], banksA[mloc], AF.Copy, scale=sdb[:, m:m + 1],
                         deps=[mt] + st["oc_war"][m])
                st["abank"][mloc] = ev
                st["fwar"][("ev", m)] = ev

    def f_ln(blk, th):
        fwar = st["fwar"]
        ttoks = []
        for m in range(NT_D):
            lx = E.dma("sync", fxtmp[m % 2],
                       dram["x3_s"][m][:, blk * TB + th * 512: blk * TB + th * 512 + 512],
                       deps=[fwar.get(("xt", m % 2))])
            at = vtt(E, oc[m], oc[m][:], fxtmp[m % 2][:], ALU.add,
                     deps=[fwar[("ev", m)], lx])
            fwar[("xt", m % 2)] = at
            ttoks.append(at)
        lnt = layernorm_tiles(E, banksB[2], banksB[3],
                              [oc[i][:] for i in range(NT_D)], ttoks, 512, msb, sq,
                              st["war"])
        for i in range(NT_D):
            da = E.dma("gpsimd",
                       dram["out_s"][i][:, blk * TB + th * 512: blk * TB + th * 512 + 512],
                       oc[i], deps=[lnt[i]])
            st["oc_war"][i] = [da]
            st["out_last"] = da
        if th == 1 and blk == 0:
            st["x3bb_war"] = [st["hb_war"][NSLAB - 1]]

    def register_block(blk):
        steps = [lambda blk=blk: f_x3load(blk)]
        for slab in range(NSLAB):
            steps.append(lambda blk=blk, slab=slab: f_wload(blk, slab))
            steps.append(lambda blk=blk, slab=slab: f_slab(blk, slab))
        for th in range(2):
            for q in range(4):
                for kg in range(4):
                    steps.append(lambda blk=blk, th=th, q=q, kg=kg:
                                 f_down(blk, th, q, kg))
            steps.append(lambda blk=blk, th=th: f_ln(blk, th))
        if E.pump_steps:
            E.pump_steps.extend(steps)
        else:
            E.pump_steps = steps

    def b1_cb(c, stores):
        if c == NCH // 2 - 1:
            register_block(0)
        elif c == NCH - 1:
            register_block(1)

    scan_core(E, dram, ar, ps, ntile=NT_D, nheads=1, u_s="u1_s", du_s="du1_s",
              de_s="de1_s", bc_s="bc1_s", al="al1", dvec="d1", post=post, TC=TC,
              chunk_cb=b1_cb)
    while E.pump_steps:
        E.pump()


def make_consts(E, ar):
    nc = E.nc
    ones_col = nc.alloc_sbuf_tensor("ones_col", [128, 1], F32)
    ones_row = nc.alloc_sbuf_tensor("ones_row", [1, 128], F32)
    scr = nc.alloc_sbuf_tensor("scr", [128, 8], F32)
    cvals = nc.alloc_sbuf_tensor("cvals", [128, 4], F32)
    E.consts = {"ones_col": ones_col, "ones_row": ones_row, "scr": scr}
    E.new_phase()
    E.consts_tok = vmemset(E, ones_col[:], 1.0)
    E.consts_tok2 = vmemset(E, ones_row[:], 1.0)
    vmemset(E, scr[:], 0.0)
    for i, v in enumerate([LN_EPS, FFN_EPS, 0.99]):
        E.consts_tok2 = vmemset(E, cvals[:, i:i + 1], v)
        nc.const_aps.register(F32, v, cvals[:, i:i + 1])


_BUILD_CACHE = {}


def build_launch(group, final_outs):
    key = (tuple(group), tuple(sorted(final_outs)))
    if key in _BUILD_CACHE:
        return _BUILD_CACHE[key]
    nc = bass.Bass("TRN2", target_bir_lowering=False)
    stack = ExitStack()
    produced = set()
    ext_in = set()
    for g in group:
        _, needs, produces = PHASES[g]
        for n in needs:
            if n not in produced:
                ext_in.add(n)
        produced.update(produces)
    ext_out = set(final_outs) & produced
    internal = produced - ext_out
    dram = {}
    for n in sorted(ext_in):
        shape, dt = SPEC[n]
        dram[n] = nc.dram_tensor(n, list(shape), dt, kind="ExternalInput")
    for n in sorted(ext_out):
        shape, dt = SPEC[n]
        dram[n] = nc.dram_tensor(n, list(shape), dt, kind="ExternalOutput")
    for n in sorted(internal):
        shape, dt = SPEC[n]
        dram[n] = nc.dram_tensor(n, list(shape), dt, kind="Internal")

    E = Emit(nc, stack)
    ar = Arena(nc, 206 * 1024)
    psA = nc.alloc_psum_tensor("psA", [128, 2048], F32)
    psB = nc.alloc_psum_tensor("psB", [128, 2048], F32)
    ps = {"A": psA, "B": psB}
    make_consts(E, ar)
    for g in group:
        fn, _, _ = PHASES[g]
        ar.reset()
        fn(E, dram, ar, ps)
        if g == "A2":
            while E.pump_steps:
                E.pump()
            ar.base = 0
    final_deps = [(s, E.semcnt[s]) for s in E.semcnt if E.semcnt[s] > 0]
    E.wait_only("gpsimd", final_deps)
    E.finalize()
    res = (nc, sorted(ext_in), sorted(ext_out))
    _BUILD_CACHE[key] = res
    return res


def prep_weights(inputs):
    def C(a):
        return np.ascontiguousarray(np.asarray(a))

    w = {}
    ip = np.asarray(inputs["L0_attn_in_proj_w"])
    w["wxT"] = C(ip[0::2].T).reshape(16, 128, I4)
    w["wgT"] = C(ip[1::2].T).reshape(16, 128, I4)
    w["cw0"] = C(inputs["L0_attn_conv_w"])
    w["cb0"] = C(inputs["L0_attn_conv_b"])
    xp = np.asarray(inputs["L0_attn_x_proj_weight"])
    w["xp0T"] = C(np.transpose(xp, (0, 2, 1))).reshape(NH, 16, 128, 160)
    dt = np.asarray(inputs["L0_attn_dt_proj_weight"])
    w["dt0T"] = C(np.transpose(dt, (0, 2, 1)))
    w["dtb0"] = C(np.asarray(inputs["L0_attn_dt_proj_bias"]).reshape(-1))
    w["al0"] = C(np.asarray(inputs["L0_attn_A_log"]).reshape(I4, NST))
    w["d0"] = C(np.asarray(inputs["L0_attn_D"]).reshape(-1))
    w["woT"] = C(np.asarray(inputs["L0_attn_out_proj_w"]).T).reshape(32, 128, D)
    w["fw0"] = C(inputs["L0_ffn_weight"]).reshape(16, 128, D)
    w["fs0"] = C(inputs["L0_ffn_scale"])
    w["cw1"] = C(np.asarray(inputs["L1_attn_conv_w"]).reshape(D, KC))
    w["cb1"] = C(inputs["L1_attn_conv_b"])
    w["xp1T"] = C(np.asarray(inputs["L1_attn_xproj_w"]).T).reshape(16, 128, 160)
    w["dt1T"] = C(np.asarray(inputs["L1_attn_dtproj_w"]).T)
    w["dtb1"] = C(inputs["L1_attn_dtproj_b"])
    w["al1"] = C(inputs["L1_attn_A_log"])
    w["d1"] = C(inputs["L1_attn_D_skip"])
    w["wqgT"] = C(np.asarray(inputs["L1_ffn_wq_gate"]).T).reshape(16, 128, HID)
    w["wquT"] = C(np.asarray(inputs["L1_ffn_wq_up"]).T).reshape(16, 128, HID)
    w["wqdT"] = C(np.asarray(inputs["L1_ffn_wq_down"]).T).reshape(NSLAB, 128, D)
    w["sg"] = C(inputs["L1_ffn_s_gate"])
    w["su"] = C(inputs["L1_ffn_s_up"])
    w["sd"] = C(inputs["L1_ffn_s_down"])
    for k in w:
        assert w[k].shape == SPEC[k][0], (k, w[k].shape, SPEC[k][0])
        want = np.int32 if SPEC[k][1] == I32 else np.float32
        assert w[k].dtype == want, (k, w[k].dtype)
    return w


LAUNCH_GROUPS = [["W1", "A1", "A2", "B0C", "F0", "A2L1", "B1F1"]]
FINAL = "out_s"


def kernel(**inputs):
    w = prep_weights(inputs)
    x = np.asarray(inputs["L0_attn_hidden_states"])
    pool = [dict(w) for _ in range(B)]
    for i in range(B):
        pool[i]["x0"] = np.ascontiguousarray(x[i].T)

    groups = LAUNCH_GROUPS
    for gi, group in enumerate(groups):
        later_needs = set()
        for g2 in [g for grp in groups[gi + 1:] for g in grp]:
            later_needs.update(PHASES[g2][1].keys())
        produced = set()
        for g in group:
            produced.update(PHASES[g][2])
        outs_needed = (produced & later_needs) | ({FINAL} & produced)
        nc, ins, outs = build_launch(group, outs_needed)
        in_maps = [{n: pool[i][n] for n in ins} for i in range(B)]
        res = run_bass_kernel_spmd(nc, in_maps, core_ids=list(range(B)))
        for i in range(B):
            for n in outs:
                pool[i][n] = res.results[i][n]

    out = np.stack([pool[i][FINAL].reshape(D, S).T for i in range(B)])
    return np.ascontiguousarray(out.astype(np.float32))


if __name__ == "__main__":
    pass


# version 12
# speedup vs baseline: 1.1065x; 1.1065x over previous
import numpy as np
import ml_dtypes
from contextlib import ExitStack

import concourse.bass as bass
import concourse.mybir as mybir
from concourse.bass_utils import run_bass_kernel_spmd

F32 = mybir.dt.float32
BF16 = mybir.dt.bfloat16
I32 = mybir.dt.int32
AF = mybir.ActivationFunctionType
ALU = mybir.AluOpType
BF16NP = ml_dtypes.bfloat16

B = 8
S = 2048
D = 2048
I4 = 4096
NH = 2
HD = 2048
NST = 16
KC = 4
HID = 5632
NSLAB = HID // 128

NT_D = D // 128
NT_I = I4 // 128

LN_EPS = 1e-5
FFN_EPS = 1e-7

ENGINES = ("sync", "tensor", "vector", "scalar", "gpsimd")


def _flat(deps):
    out = []
    for d in deps:
        if d is None:
            continue
        if isinstance(d, list):
            for x in d:
                if x is not None:
                    out.append(x)
        else:
            out.append(d)
    return out


class Emit:
    def __init__(self, nc, stack):
        self.nc = nc
        self.stack = stack
        self.recs = {e: [] for e in ENGINES}
        self.last_wait = {e: {} for e in ENGINES}
        self.semcnt = {}
        self.cur = None
        self.dmas = None
        self._semi = 0
        self.consts = {}

    def _new_sem(self):
        self._semi += 1
        h = self.stack.enter_context(self.nc.semaphore(f"s{self._semi}"))
        self.semcnt[h] = 0
        return h

    NPOOL = 12

    def new_phase(self):
        barrier = [(s, c) for s, c in self.semcnt.items() if c > 0]
        self.cur = {e: self._new_sem() for e in ("tensor", "vector", "scalar", "gpsimd")}
        if not hasattr(self, "pools"):
            self.pools = {q: [[self._new_sem(), None] for _ in range(self.NPOOL)]
                          for q in ("sync", "gpsimd", "scalar")}
            self.pool_idx = {q: 0 for q in self.pools}
        if barrier:
            for e in ENGINES:
                self._record(e, None, barrier, None, 0)

    def _record(self, eng, fn, deps, sem, amt):
        waits = []
        for dsem, dval in _flat(deps):
            if dval <= self.last_wait[eng].get(dsem, -1):
                continue
            waits.append((dsem, dval))
            self.last_wait[eng][dsem] = dval
        self.recs[eng].append((waits, fn, sem, amt))

    def op(self, eng, fn, deps=(), inc=True):
        sem = self.cur[eng] if inc else None
        tok = None
        if inc:
            self.semcnt[sem] += 1
            tok = (sem, self.semcnt[sem])
        self._record(eng, fn, deps, sem, 1)
        return tok

    def dma(self, q, out, in_, deps=(), slow=False):
        slot = self.pools[q][self.pool_idx[q] % self.NPOOL]
        self.pool_idx[q] += 1
        sem, last = slot
        self.semcnt[sem] += 16
        kw = {"allow_slow_non_contiguous": True} if slow else {}

        def fn(e, out=out, in_=in_, kw=kw):
            return e.dma_start(out=out, in_=in_, **kw)

        self._record(q, fn, list(deps) + [last], sem, 16)
        tok = (sem, self.semcnt[sem])
        slot[1] = tok
        return tok

    def wait_only(self, eng, deps):
        self._record(eng, None, deps, None, 0)

    pump_steps = None

    def pump(self):
        if self.pump_steps:
            self.pump_steps.pop(0)()

    def finalize(self):
        with self.nc.Block() as block:
            for eng in ENGINES:
                recs = self.recs[eng]
                if not recs:
                    continue

                def runner(e, recs=recs):
                    for waits, fn, sem, amt in recs:
                        for (s, v) in waits:
                            e.wait_ge(s, v)
                        if fn is not None:
                            inst = fn(e)
                            if sem is not None and inst is not None:
                                inst.then_inc(sem, amt)

                getattr(block, eng)(runner)


class Arena:
    def __init__(self, nc, nbytes):
        self.t = nc.alloc_sbuf_tensor("arena", [128, nbytes], mybir.dt.uint8)
        self.nbytes = nbytes
        self.off = 0

    def alloc(self, cols, dtype):
        esz = {F32: 4, BF16: 2, I32: 4}[dtype]
        self.off = (self.off + 31) & ~31
        start = self.off
        self.off += cols * esz
        assert self.off <= self.nbytes, f"arena overflow: {self.off} > {self.nbytes}"
        return self.t[:, start : start + cols * esz].bitcast(dtype)

    def reset(self, off=None):
        self.off = self.base if off is None else off

    base = 0


PHASES = {}


def phase(name, needs, produces):
    def deco(fn):
        PHASES[name] = (fn, needs, produces)
        return fn

    return deco


SPEC = {
    "x0": ((D, S), F32),
    "wxT": ((16, 128, I4), F32),
    "wgT": ((16, 128, I4), F32),
    "cw0": ((I4, KC), F32),
    "cb0": ((I4,), F32),
    "xp0T": ((NH, 16, 128, 160), F32),
    "dt0T": ((NH, 128, HD), F32),
    "dtb0": ((I4,), F32),
    "al0": ((I4, NST), F32),
    "d0": ((I4,), F32),
    "woT": ((32, 128, D), F32),
    "fw0": ((16, 128, D), F32),
    "fs0": ((D,), F32),
    "cw1": ((D, KC), F32),
    "cb1": ((D,), F32),
    "xp1T": ((16, 128, 160), F32),
    "dt1T": ((128, D), F32),
    "dtb1": ((D,), F32),
    "al1": ((D, NST), F32),
    "d1": ((D,), F32),
    "wqgT": ((16, 128, HID), I32),
    "wquT": ((16, 128, HID), I32),
    "wqdT": ((NSLAB, 128, D), I32),
    "sg": ((HID,), F32),
    "su": ((HID,), F32),
    "sd": ((D,), F32),
    "xb_s": ((NT_I, 128, S), BF16),
    "gate_s": ((NT_I, 128, S), BF16),
    "u0_s": ((NT_I, 128, S), BF16),
    "du0_s": ((NT_I, 128, S), BF16),
    "de0_s": ((NT_I, 128, S), BF16),
    "bc0_s": ((NH, 32, S), BF16),
    "ybf_s": ((NT_I, 128, S), BF16),
    "xr0_s": ((NT_D, 128, S), F32),
    "x1_s": ((NT_D, 128, S), F32),
    "x1b_s": ((NT_D, 128, S), BF16),
    "x2_s": ((NT_D, 128, S), F32),
    "x2b_s": ((NT_D, 128, S), BF16),
    "u1_s": ((NT_D, 128, S), BF16),
    "du1_s": ((NT_D, 128, S), BF16),
    "de1_s": ((NT_D, 128, S), BF16),
    "bc1_s": ((1, 32, S), BF16),
    "x3_s": ((NT_D, 128, S), F32),
    "x3b_s": ((NT_D, 128, S), BF16),
    "wgb_s": ((16, 128, HID), BF16),
    "wub_s": ((16, 128, HID), BF16),
    "wdb_s": ((NSLAB, 128, D), BF16),
    "wob_s": ((32, 128, D), BF16),
    "h_s": ((NSLAB, 128, S), BF16),
    "fb0_s": ((16, 128, D), BF16),
    "out_s": ((NT_D, 128, S), F32),
}


def act(E, out, in_, func, bias=0.0, scale=1.0, deps=()):
    def fn(e, out=out, in_=in_, func=func, bias=bias, scale=scale):
        return e.activation(out=out, in_=in_, func=func, bias=bias, scale=scale)

    return E.op("scalar", fn, deps=deps)


def vtt(E, out, in0, in1, alu, deps=(), eng="vector"):
    def fn(e, out=out, in0=in0, in1=in1, alu=alu):
        return e.tensor_tensor(out=out, in0=in0, in1=in1, op=alu)

    return E.op(eng, fn, deps=deps)


def vts(E, out, in0, s1, s2, op0, op1=None, deps=(), eng="vector"):
    def fn(e, out=out, in0=in0, s1=s1, s2=s2, op0=op0, op1=op1):
        if op1 is None:
            return e.tensor_scalar(out=out, in0=in0, scalar1=s1, scalar2=None, op0=op0)
        return e.tensor_scalar(out=out, in0=in0, scalar1=s1, scalar2=s2, op0=op0, op1=op1)

    return E.op(eng, fn, deps=deps)


def vstt(E, out, in0, scalar, in1, op0, op1, deps=(), eng="vector"):
    def fn(e, out=out, in0=in0, scalar=scalar, in1=in1, op0=op0, op1=op1):
        return e.scalar_tensor_tensor(out=out, in0=in0, scalar=scalar, in1=in1,
                                      op0=op0, op1=op1)

    return E.op(eng, fn, deps=deps)


def vcopy(E, out, in_, deps=(), eng="vector"):
    def fn(e, out=out, in_=in_):
        return e.tensor_copy(out=out, in_=in_)

    return E.op(eng, fn, deps=deps)


def vmemset(E, ap, val, deps=(), eng="vector"):
    def fn(e, ap=ap, val=val):
        return e.memset(ap, val)

    return E.op(eng, fn, deps=deps)


def vscan(E, out, d0, d1, initial, deps=()):
    def fn(e, out=out, d0=d0, d1=d1, initial=initial):
        return e.tensor_tensor_scan(out=out, data0=d0, data1=d1, initial=initial,
                                    op0=ALU.mult, op1=ALU.add)

    return E.op("vector", fn, deps=deps)


def vrecip(E, out, in_, deps=()):
    def fn(e, out=out, in_=in_):
        return e.reciprocal(out=out, in_=in_)

    return E.op("vector", fn, deps=deps)


def mm(E, out, lhsT, rhs, start, stop, deps=(), inc=None):
    if inc is None:
        inc = stop

    def fn(e, out=out, lhsT=lhsT, rhs=rhs, start=start, stop=stop):
        return e.matmul(out, lhsT, rhs, start=start, stop=stop)

    return E.op("tensor", fn, deps=deps, inc=inc)


def ptr_barrier(E, deps):
    scr = E.consts["scr"]
    for eng in ("scalar", "vector"):
        E.wait_only(eng, deps)
        for i in range(10):
            if eng == "scalar":
                E.op(eng, lambda e: e.activation(out=scr[0:1, 0:1], in_=scr[0:1, 1:2],
                                                 func=AF.Copy), inc=False)
            else:
                E.op(eng, lambda e: e.tensor_copy(out=scr[0:1, 2:3], in_=scr[0:1, 3:4]),
                     inc=False)


def load_pp_vec(E, dram, name, sb, ncols, deps=()):
    toks = []
    for k in range(ncols):
        src = dram[name][k * 128:(k + 1) * 128].unsqueeze(1)
        toks.append(E.dma("sync", sb[:, k:k + 1], src, deps=deps if k == 0 else (),
                          slow=True))
    return toks


def conv4(E, acc, xbc, cw, m, bias_col, deps):
    TC = acc[0].shape[-1]
    a, b = acc
    if bias_col is not None:
        t0 = vts(E, a, xbc[:, 0:TC], cw[:, m * KC:m * KC + 1], bias_col,
                 ALU.mult, ALU.add, deps=deps)
    else:
        t0 = vts(E, a, xbc[:, 0:TC], cw[:, m * KC:m * KC + 1], None, ALU.mult,
                 deps=deps)
    t1 = vstt(E, b, xbc[:, 1:TC + 1], cw[:, m * KC + 1:m * KC + 2], a, ALU.mult,
              ALU.add, deps=[t0])
    t2 = vstt(E, a, xbc[:, 2:TC + 2], cw[:, m * KC + 2:m * KC + 3], b, ALU.mult,
              ALU.add, deps=[t1])
    t3 = vstt(E, b, xbc[:, 3:TC + 3], cw[:, m * KC + 3:m * KC + 4], a, ALU.mult,
              ALU.add, deps=[t2])
    return t3


def layernorm_tiles(E, stat_bank, bc_bank, tiles, tile_toks, tc, msb, sq, war):
    ones_col = E.consts["ones_col"]
    ones_row = E.consts["ones_row"]
    ntile = len(tiles)
    ntot = ntile * 128
    ps_s = stat_bank[0:1, 0:tc]
    ps_b = bc_bank[:, 0:tc]

    mt1 = None
    for i, tl in enumerate(tiles):
        mt1 = mm(E, ps_s, ones_col[:, 0:1], tl, start=(i == 0), stop=(i == ntile - 1),
                 deps=[tile_toks[i], war.get("stat"), E.consts_tok] if i == 0
                 else [tile_toks[i]], inc=(i == ntile - 1))
    m_row = msb[0:1, 0:tc]
    e2_row = msb[0:1, tc:2 * tc]
    r_row = msb[0:1, 2 * tc:3 * tc]
    t_row = msb[0:1, 3 * tc:4 * tc]
    a = vts(E, m_row, ps_s, 1.0 / ntot, None, ALU.mult, deps=[mt1, war.get("rows")])
    mt2 = None
    sq_toks = []
    for i, tl in enumerate(tiles):
        sqt = act(E, sq[:, 0:tc], tl, AF.Square, deps=[tile_toks[i], war.get("sq")])
        sq_toks.append(sqt)
        mt2 = mm(E, ps_s, ones_col[:, 0:1], sq[:, 0:tc], start=(i == 0),
                 stop=(i == ntile - 1),
                 deps=[sqt, a] if i == 0 else [sqt], inc=True)
        war["sq"] = mt2
    bb = vts(E, e2_row, ps_s, 1.0 / ntot, None, ALU.mult, deps=[mt2])
    war["stat"] = bb
    c1 = vtt(E, t_row, m_row, m_row, ALU.mult, deps=[a])
    c2 = vtt(E, e2_row, e2_row, t_row, ALU.subtract, deps=[bb, c1])
    c3 = act(E, t_row, e2_row, AF.Sqrt, bias=LN_EPS, deps=[c2])
    c4 = vrecip(E, r_row, t_row, deps=[c3])
    bmt = mm(E, ps_b, ones_row[0:1, :], m_row, start=True, stop=True,
             deps=[a, war.get("bc"), E.consts_tok2])
    subs = []
    for i, tl in enumerate(tiles):
        subs.append(vtt(E, tl, tl, ps_b, ALU.subtract,
                        deps=[bmt, mt1, mt2, sq_toks[i]]))
    brt = mm(E, ps_b, ones_row[0:1, :], r_row, start=True, stop=True,
             deps=[c4] + subs)
    out = []
    for i, tl in enumerate(tiles):
        out.append(vtt(E, tl, tl, ps_b, ALU.mult, deps=[brt, subs[i]]))
    war["bc"] = out[-1]
    war["rows"] = [bmt, brt]
    return out


@phase("A1", needs={"x0": "w", "wxT": "w", "wgT": "w"}, produces=["xb_s", "gate_s"])
def p_a1(E, dram, ar, ps):
    E.new_phase()
    xbf = ar.alloc(NT_D * S, BF16)
    wtmp = [ar.alloc(2048, F32) for _ in range(2)]
    whalf = ar.alloc(16 * 2048, BF16)
    xe = [ar.alloc(S, BF16) for _ in range(2)]

    wtmp_war = [None, None]
    ctoks = []
    for k in range(NT_D):
        t = E.dma("sync", wtmp[k % 2], dram["x0"][k * 128:(k + 1) * 128, :],
                  deps=[wtmp_war[k % 2]])
        ct = act(E, xbf[:, k * S:(k + 1) * S], wtmp[k % 2][:], AF.Copy, deps=[t])
        wtmp_war[k % 2] = ct
        ctoks.append(ct)
    xready = ctoks[-1]

    grp = [ps["A"], ps["B"]]
    grp_war = [None, None]
    xe_war = [None, None]
    prev_mm = None

    for wname, oname in (("wxT", "xb_s"), ("wgT", "gate_s")):
        for half in range(2):
            wt = None
            for k in range(NT_D):
                t = E.dma("sync", wtmp[k % 2],
                          dram[wname][k, :, half * 2048:(half + 1) * 2048],
                          deps=[wtmp_war[k % 2]])
                wt = act(E, whalf[:, k * 2048:(k + 1) * 2048], wtmp[k % 2][:],
                         AF.Copy, deps=[t, prev_mm])
                wtmp_war[k % 2] = wt
            for mloc in range(16):
                m = half * 16 + mloc
                gi = mloc % 2
                pp = grp[gi]
                mtok = None
                for k in range(NT_D):
                    for tq in range(4):
                        first = (k == 0 and tq == 0)
                        mtok = mm(E, pp[:, tq * 512:(tq + 1) * 512],
                                  whalf[:, k * 2048 + mloc * 128: k * 2048 + mloc * 128 + 128],
                                  xbf[:, k * S + tq * 512: k * S + tq * 512 + 512],
                                  start=(k == 0), stop=(k == NT_D - 1),
                                  deps=[wt, xready, grp_war[gi]] if first else (),
                                  inc=(k == NT_D - 1 and tq == 3))
                prev_mm = mtok
                fu = AF.Silu if oname == "gate_s" else AF.Copy
                ev = act(E, xe[gi], pp[:], fu, deps=[mtok, xe_war[gi]])
                grp_war[gi] = ev
                xe_war[gi] = E.dma("gpsimd", dram[oname][m], xe[gi], deps=[ev])
    if "dbg_s" in dram:
        for k in range(NT_D):
            E.dma("gpsimd", dram["dbg_s"][k], xbf[:, k * S:(k + 1) * S],
                  deps=[prev_mm])


def proj_phase(E, dram, ar, ps, *, ntile, nheads, src, cwn, cbn, xpn, dtn, dtbn,
               u_s, du_s, de_s, bc_s, silu, pre_chunk=None, src_toks=None):
    E.new_phase()
    TC = 512
    NCH = S // TC
    tph = ntile // nheads

    cw = ar.alloc(ntile * KC, F32)
    cb = ar.alloc(ntile, F32)
    dtb = ar.alloc(ntile, F32)
    xpb = ar.alloc(nheads * 16 * 160, BF16)
    dttb = ar.alloc(nheads * HD, BF16)
    wtmp = ar.alloc(2048, F32)
    xbc = [ar.alloc(TC + KC, BF16) for _ in range(2)]
    acc = [ar.alloc(TC, F32) for _ in range(2)]
    ubf = ar.alloc(ntile * TC, BF16)
    dtinb = ar.alloc(TC, BF16)
    bct = ar.alloc(TC, BF16)
    t_r = [ar.alloc(TC, F32) for _ in range(2)]
    t_a = [ar.alloc(TC, F32) for _ in range(2)]
    t_l = [ar.alloc(TC, F32) for _ in range(2)]
    de_t = [ar.alloc(TC, BF16) for _ in range(2)]
    du_t = [ar.alloc(TC, BF16) for _ in range(2)]

    vready = []
    for k in range(ntile):
        vready.append(E.dma("sync", cw[:, k * KC:(k + 1) * KC],
                            dram[cwn][k * 128:(k + 1) * 128, :]))
    vready += load_pp_vec(E, dram, cbn, cb, ntile)
    vready += load_pp_vec(E, dram, dtbn, dtb, ntile)
    wtok = None
    for h in range(nheads):
        for k in range(16):
            src_ap = dram[xpn][h, k] if nheads > 1 else dram[xpn][k]
            t = E.dma("sync", wtmp[:, 0:160], src_ap, deps=[wtok])
            wtok = act(E, xpb[:, (h * 16 + k) * 160:(h * 16 + k + 1) * 160],
                       wtmp[:, 0:160], AF.Copy, deps=[t])
    for h in range(nheads):
        src_ap = dram[dtn][h] if nheads > 1 else dram[dtn][:]
        t = E.dma("sync", wtmp, src_ap, deps=[wtok])
        wtok = act(E, dttb[:, h * HD:(h + 1) * HD], wtmp[:], AF.Copy, deps=[t])
    wready = wtok
    ptr_barrier(E, vready)

    ps_dt = ps["A"][:, 0:512]
    ps_bc = ps["A"][0:32, 512:1024]
    ps_dp = [ps["B"][:, 0:512], ps["B"][:, 512:1024]]
    ps_dt_war = None
    ps_bc_war = None
    ps_dp_war = [None, None]
    dtinb_war = None
    bct_war = None
    chain_war = [None, None]
    dd_war = [None, None]

    u_war = [None] * ntile
    xbc_war = [None, None]
    acc_war = None
    for c in range(NCH):
        if pre_chunk is not None:
            pre_chunk(c)
        utoks = [None] * ntile
        ustoks = [None] * ntile
        for m in range(ntile):
            bslot = m % 2
            stk = [] if src_toks is None else src_toks(c, m)
            if c == 0:
                mz = vmemset(E, xbc[bslot][:, 0:KC - 1], 0.0, deps=[xbc_war[bslot]])
                lt = E.dma("sync", xbc[bslot][:, KC - 1:KC - 1 + TC],
                           dram[src][m][:, 0:TC], deps=[mz] + stk)
            else:
                lt = E.dma("sync", xbc[bslot][:, 0:TC + KC - 1],
                           dram[src][m][:, c * TC - (KC - 1): c * TC + TC],
                           deps=[xbc_war[bslot]] + stk)
            t3 = conv4(E, acc, xbc[bslot], cw, m,
                       None if silu else cb[:, m:m + 1],
                       [lt, vready, u_war[m], acc_war])
            xbc_war[bslot] = t3
            fu = AF.Silu if silu else AF.Copy
            bias = cb[:, m:m + 1] if silu else 0.0
            ub = act(E, ubf[:, m * TC:(m + 1) * TC], acc[1][:, :], fu, bias=bias,
                     deps=[t3])
            acc_war = [ub]
            ustoks[m] = ub
            utoks[m] = ub
            E.dma("gpsimd", dram[u_s][m][:, c * TC:(c + 1) * TC],
                  ubf[:, m * TC:(m + 1) * TC], deps=[ub])
            E.pump()
        prev_head_mt = None
        for h in range(nheads):
            mt_dt = None
            mt_bc = None
            for k in range(tph):
                tile = h * tph + k
                base = (h * 16 + k) * 160 if nheads > 1 else k * 160
                first_deps = [wready, ps_dt_war, ps_bc_war] + utoks if k == 0 else ()
                mt_dt = mm(E, ps_dt, xpb[:, base:base + 128],
                           ubf[:, tile * TC:(tile + 1) * TC],
                           start=(k == 0), stop=(k == tph - 1), deps=first_deps,
                           inc=(k == tph - 1))
                mt_bc = mm(E, ps_bc, xpb[:, base + 128:base + 160],
                           ubf[:, tile * TC:(tile + 1) * TC],
                           start=(k == 0), stop=(k == tph - 1), inc=(k == tph - 1))
            ev_dt = act(E, dtinb[:], ps_dt, AF.Copy, deps=[mt_dt, dtinb_war])
            ps_dt_war = ev_dt
            ev_bc = vcopy(E, bct[0:32, :], ps_bc, deps=[mt_bc, bct_war])
            ps_bc_war = ev_bc
            bdst = dram[bc_s][h] if nheads > 1 else dram[bc_s][0]
            bct_war = E.dma("gpsimd", bdst[:, c * TC:(c + 1) * TC], bct[0:32, :],
                            deps=[ev_bc])
            for mi in range(tph):
                tile = h * tph + mi
                slot = mi % 2
                mt = mm(E, ps_dp[slot],
                        dttb[:, h * HD + mi * 128: h * HD + (mi + 1) * 128],
                        dtinb[:], start=True, stop=True,
                        deps=[ev_dt, ps_dp_war[slot]])
                prev_head_mt = mt
                rr = act(E, t_r[slot], ps_dp[slot], AF.Relu,
                         bias=dtb[:, tile:tile + 1], deps=[mt, chain_war[slot]])
                aa = act(E, t_a[slot], ps_dp[slot], AF.Abs,
                         bias=dtb[:, tile:tile + 1], deps=[mt])
                ps_dp_war[slot] = aa
                ee = act(E, t_a[slot], t_a[slot][:], AF.Exp, scale=-1.0, deps=[aa])
                ll = act(E, t_l[slot], t_a[slot][:], AF.Ln, bias=1.0, deps=[ee])
                dd = vtt(E, de_t[slot], t_r[slot][:], t_l[slot][:], ALU.add,
                         deps=[ll, rr, dd_war[slot]])
                uu = vtt(E, du_t[slot], de_t[slot][:],
                         ubf[:, tile * TC:(tile + 1) * TC], ALU.mult,
                         deps=[dd, ustoks[tile]])
                chain_war[slot] = dd
                d1 = E.dma("gpsimd", dram[de_s][tile][:, c * TC:(c + 1) * TC],
                           de_t[slot], deps=[dd])
                d2 = E.dma("gpsimd", dram[du_s][tile][:, c * TC:(c + 1) * TC],
                           du_t[slot], deps=[uu])
                dd_war[slot] = [d1, d2]
                u_war[tile] = [uu, d1, d2, mt_dt, mt_bc]
            dtinb_war = prev_head_mt


@phase("A2",
       needs={"xb_s": "s", "cw0": "w", "cb0": "w", "xp0T": "w", "dt0T": "w", "dtb0": "w"},
       produces=["u0_s", "du0_s", "de0_s", "bc0_s"])
def p_a2(E, dram, ar, ps):
    proj_phase(E, dram, ar, ps, ntile=NT_I, nheads=NH, src="xb_s", cwn="cw0",
               cbn="cb0", xpn="xp0T", dtn="dt0T", dtbn="dtb0", u_s="u0_s",
               du_s="du0_s", de_s="de0_s", bc_s="bc0_s", silu=True)


def scan_core(E, dram, ar, ps, *, ntile, nheads, u_s, du_s, de_s, bc_s, al, dvec,
              gate_s=None, ybf_s=None, post=None, TC=512, chunk_cb=None):
    tph = ntile // nheads
    NCH = S // TC

    asc = ar.alloc(ntile * NST, F32)
    dsb = ar.alloc(ntile, F32)
    carry = ar.alloc(ntile * NST, F32)
    Bt = ar.alloc(NST * TC, BF16)
    Ct = ar.alloc(NST * TC, BF16)
    dA = ar.alloc(NST * TC, BF16)
    dBu = ar.alloc(NST * TC, BF16)
    hh = ar.alloc(NST * TC, BF16)
    trA = ar.alloc(8 * TC, BF16)
    trB = ar.alloc(4 * TC, BF16)
    yv = ar.alloc(TC, F32)
    de_t = [ar.alloc(TC, BF16) for _ in range(2)]
    du_t = [ar.alloc(TC, BF16) for _ in range(2)]
    u_t = [ar.alloc(TC, BF16) for _ in range(2)]
    gt = [ar.alloc(TC, BF16) for _ in range(2)] if gate_s else None
    ybt = ar.alloc(TC, BF16)

    atoks = []
    for k in range(ntile):
        atoks.append(E.dma("sync", asc[:, k * NST:(k + 1) * NST],
                           dram[al][k * 128:(k + 1) * 128, :]))
    t = act(E, asc, asc[:], AF.Exp, deps=atoks)

    def _neg(e, out=asc, in_=asc):
        return e.mul(out=out, in_=in_[:], mul=-1.0)

    atok = E.op("scalar", _neg, deps=[t])
    dtok = load_pp_vec(E, dram, dvec, dsb, ntile)
    ptr_barrier(E, [atok] + dtok)
    carry_tok = [None] * ntile

    war = {}
    scan_last = atok
    tree_last = None
    y_last = None
    for c in range(NCH):
        chunk_stores = []
        for h in range(nheads):
            bsrc = dram[bc_s][h]
            bt = E.dma("sync", Bt[:].rearrange("p (n t) -> p n t", n=NST),
                       bsrc[0:NST, c * TC:(c + 1) * TC].unsqueeze(0)
                       .broadcast_to((128, NST, TC)),
                       deps=[war.get("Bt")])
            ct2 = E.dma("sync", Ct[:].rearrange("p (n t) -> p n t", n=NST),
                        bsrc[NST:2 * NST, c * TC:(c + 1) * TC].unsqueeze(0)
                        .broadcast_to((128, NST, TC)),
                        deps=[war.get("Ct")])
            for mi in range(tph):
                tile = h * tph + mi
                slot = mi % 2
                ld = E.dma("sync", de_t[slot], dram[de_s][tile][:, c * TC:(c + 1) * TC],
                           deps=[war.get(("de", slot))])
                lu = E.dma("sync", du_t[slot], dram[du_s][tile][:, c * TC:(c + 1) * TC],
                           deps=[war.get(("du", slot))])
                lt = E.dma("sync", u_t[slot], dram[u_s][tile][:, c * TC:(c + 1) * TC],
                           deps=[war.get(("ut", slot))])
                if gate_s:
                    lg = E.dma("sync", gt[slot],
                               dram[gate_s][tile][:, c * TC:(c + 1) * TC],
                               deps=[war.get(("gt", slot))])
                extoks = []
                for n in range(NST):
                    extoks.append(act(E, dA[:, n * TC:(n + 1) * TC], de_t[slot][:],
                                      AF.Exp,
                                      scale=asc[:, tile * NST + n: tile * NST + n + 1],
                                      deps=[ld, atok, scan_last]))
                war[("de", slot)] = extoks[-1]
                du3 = du_t[slot][:].unsqueeze(1).broadcast_to((128, NST, TC))
                dbu_tok = vtt(E, dBu[:].rearrange("p (n t) -> p n t", n=NST), du3,
                              Bt[:].rearrange("p (n t) -> p n t", n=NST), ALU.mult,
                              deps=[lu, bt, tree_last])
                war[("du", slot)] = dbu_tok
                war["Bt"] = dbu_tok
                stoks = []
                for n in range(NST):
                    init = 0.0 if c == 0 else \
                        carry[:, tile * NST + n: tile * NST + n + 1]
                    stoks.append(vscan(E, hh[:, n * TC:(n + 1) * TC],
                                       dA[:, n * TC:(n + 1) * TC],
                                       dBu[:, n * TC:(n + 1) * TC],
                                       init,
                                       deps=[extoks[n], dbu_tok, carry_tok[tile]]))
                scan_last = stoks[-1]
                hl = hh[:].rearrange("p (n t) -> p n t", n=NST)[:, :, TC - 1:TC]
                cext = vcopy(E, carry[:, tile * NST:(tile + 1) * NST].unsqueeze(2),
                             hl, deps=stoks)
                carry_tok[tile] = cext
                gtok = vtt(E, dBu, hh[:], Ct[:], ALU.mult, deps=stoks + [ct2, cext])
                war["Ct"] = gtok
                r1 = vtt(E, trA[:, 0:8 * TC], dBu[:, 0:8 * TC], dBu[:, 8 * TC:16 * TC],
                         ALU.add, deps=[gtok, war.get("trA")])
                r2 = vtt(E, trB[:, 0:4 * TC], trA[:, 0:4 * TC], trA[:, 4 * TC:8 * TC],
                         ALU.add, deps=[r1, war.get("trB")])
                r3 = vtt(E, trA[:, 0:2 * TC], trB[:, 0:2 * TC], trB[:, 2 * TC:4 * TC],
                         ALU.add, deps=[r2])
                war["trB"] = r3
                r4 = vtt(E, yv, trA[:, 0:TC], trA[:, TC:2 * TC], ALU.add, deps=[r3])
                war["trA"] = r4
                tree_last = r1
                y2 = vstt(E, yv, u_t[slot][:], dsb[:, tile:tile + 1], yv[:],
                          ALU.mult, ALU.add, deps=[r4, lt, dtok])
                war[("ut", slot)] = y2
                if gate_s:
                    y3 = vtt(E, ybt, yv[:], gt[slot][:], ALU.mult, deps=[y2, lg, y_last])
                    war[("gt", slot)] = y3
                    y_last = E.dma("gpsimd", dram[ybf_s][tile][:, c * TC:(c + 1) * TC],
                                   ybt, deps=[y3])
                    chunk_stores.append(y_last)
                else:
                    y_last = post(c, tile, yv, y2)
                E.pump()
                E.pump()
        if chunk_cb is not None:
            chunk_cb(c, chunk_stores)


@phase("B0C",
       needs={"u0_s": "s", "du0_s": "s", "de0_s": "s", "bc0_s": "s", "gate_s": "s",
              "wob_s": "s", "al0": "w", "d0": "w", "x0": "w"},
       produces=["ybf_s", "x1_s", "x1b_s"])
def p_b0c(E, dram, ar, ps):
    E.new_phase()
    TC = 512
    ybc = ar.alloc(32 * 512, BF16)
    wbuf = [ar.alloc(4 * 1024, BF16) for _ in range(2)]
    xr = [ar.alloc(512, F32) for _ in range(NT_D)]
    xtmp = [ar.alloc(512, F32) for _ in range(2)]
    x1b = ar.alloc(512, BF16)
    msb = ar.alloc(4 * 512, F32)
    sq = ar.alloc(512, F32)
    banks = [ps["A"][:, i * 512:(i + 1) * 512] for i in range(4)] + \
            [ps["B"][:, i * 512:(i + 1) * 512] for i in range(4)]
    st = {"war": {}, "xr_war": [[] for _ in range(NT_D)], "bank_war": [None] * 8,
          "wbuf_war": [None, None], "ybc_war": None, "xt_war": [None, None]}

    def c_chunk(c, ytoks):
        steps = []
        steps.append(lambda c=c, ytoks=ytoks: c_ybc(c, ytoks))
        for mh in range(2):
            for kg in range(8):
                steps.append(lambda c=c, mh=mh, kg=kg: c_kg(c, mh, kg))
            steps.append(lambda c=c, mh=mh: c_evac(c, mh))
        steps.append(lambda c=c: c_post(c))
        if E.pump_steps:
            E.pump_steps.extend(steps)
        else:
            E.pump_steps = steps

    def c_ybc(c, ytoks):
        ylds = []
        for k in range(32):
            ylds.append(E.dma("scalar", ybc[:, k * 512:(k + 1) * 512],
                              dram["ybf_s"][k][:, c * TC:(c + 1) * TC],
                              deps=[ytoks[k], st["ybc_war"]]))
        st["ylds"] = ylds
        st["etoks"] = []

    def c_kg(c, mh, kg):
        slot = kg % 2
        ylds = st["ylds"]
        wlds = []
        for j in range(4):
            k = kg * 4 + j
            wlds.append(E.dma("scalar",
                              wbuf[slot][:, j * 1024:(j + 1) * 1024],
                              dram["wob_s"][k][:, mh * 1024:(mh + 1) * 1024],
                              deps=[st["wbuf_war"][slot]] if j == 0 else ()))
        last_mm = None
        for j in range(4):
            k = kg * 4 + j
            for mloc in range(8):
                deps = ()
                if mloc == 0:
                    deps = [wlds[j], ylds[k]]
                if k == 0:
                    deps = list(deps) + [st["bank_war"][mloc]]
                last_mm = mm(E, banks[mloc],
                             wbuf[slot][:, j * 1024 + mloc * 128:
                                        j * 1024 + mloc * 128 + 128],
                             ybc[:, k * 512:(k + 1) * 512],
                             start=(k == 0), stop=(k == 31),
                             deps=deps, inc=(j == 3 and mloc == 7))
        st["wbuf_war"][slot] = last_mm
        st["last_mm"] = last_mm

    def c_evac(c, mh):
        for mloc in range(8):
            m = mh * 8 + mloc
            ev = act(E, xr[m], banks[mloc], AF.Copy,
                     deps=[st["last_mm"]] + st["xr_war"][m])
            st["bank_war"][mloc] = ev
            st["etoks"].append(ev)
        if mh == 1:
            st["ybc_war"] = st["last_mm"]

    def c_post(c):
        etoks = st["etoks"]
        war = st["war"]
        ttoks = []
        for m in range(NT_D):
            lx0 = E.dma("scalar", xtmp[m % 2], dram["x0"][m * 128:(m + 1) * 128,
                                                          c * TC:(c + 1) * TC],
                        deps=[st["xt_war"][m % 2]])
            at = vtt(E, xr[m], xr[m][:], xtmp[m % 2][:], ALU.add,
                     deps=[lx0, etoks[m]])
            st["xt_war"][m % 2] = at
            ttoks.append(at)
        ws = war.get("stat")
        war["stat"] = [etoks[8]] + (ws if isinstance(ws, list) else [ws])
        wb2 = war.get("bc")
        war["bc"] = [etoks[9]] + (wb2 if isinstance(wb2, list) else [wb2])
        lnt = layernorm_tiles(E, banks[0], banks[1],
                              [xr[i][:] for i in range(NT_D)], ttoks, TC, msb, sq,
                              war)
        st["bank_war"][0] = [st["bank_war"][0], lnt[-1]]
        st["bank_war"][1] = [st["bank_war"][1], lnt[-1]]
        for i in range(NT_D):
            bt = act(E, x1b, xr[i][:], AF.Copy, deps=[lnt[i], war.get("x1b")])
            da = E.dma("gpsimd", dram["x1_s"][i][:, c * TC:(c + 1) * TC], xr[i],
                       deps=[lnt[i]])
            db = E.dma("gpsimd", dram["x1b_s"][i][:, c * TC:(c + 1) * TC], x1b,
                       deps=[bt])
            war["x1b"] = db
            st["xr_war"][i] = [bt, da, db]

    scan_core(E, dram, ar, ps, ntile=NT_I, nheads=NH, u_s="u0_s", du_s="du0_s",
              de_s="de0_s", bc_s="bc0_s", al="al0", dvec="d0",
              gate_s="gate_s", ybf_s="ybf_s", chunk_cb=c_chunk)
    while E.pump_steps:
        E.pump()



@phase("F0L1",
       needs={"x1_s": "s", "x1b_s": "s", "fb0_s": "s", "fs0": "w",
              "cw1": "w", "cb1": "w", "xp1T": "w", "dt1T": "w", "dtb1": "w"},
       produces=["x2_s", "x2b_s", "u1_s", "du1_s", "de1_s", "bc1_s"])
def p_f0l1(E, dram, ar, ps):
    E.new_phase()
    TC = 512
    ssb = ar.alloc(16, F32)
    xbt = ar.alloc(16 * 512, BF16)
    wfb = [ar.alloc(4 * 2048, BF16) for _ in range(2)]
    oc = [ar.alloc(512, F32) for _ in range(NT_D)]
    fxt = [ar.alloc(512, F32) for _ in range(2)]
    x2b = ar.alloc(512, BF16)
    msb = ar.alloc(4 * 512, F32)
    sq = ar.alloc(512, F32)
    rowtmp = ar.alloc(7 * 512, F32)
    stok = load_pp_vec(E, dram, "fs0", ssb, NT_D)
    ptr_barrier(E, stok)

    fbanks = [ps["A"][:, 1024:1536], ps["A"][:, 1536:2048],
              ps["B"][:, 1024:1536], ps["B"][:, 1536:2048]]
    fst = {"war": {}, "bank": [None] * 4, "wfb_war": [None, None],
           "oc_war": [[] for _ in range(NT_D)], "xbt_war": None,
           "xt_war": [None, None], "x2db": {}}
    war = fst["war"]
    ones_col = E.consts["ones_col"]
    ones_row = E.consts["ones_row"]

    def f0_chunk(c):
        xlds = []
        for k in range(NT_D):
            xlds.append(E.dma("scalar", xbt[:, k * 512:(k + 1) * 512],
                              dram["x1b_s"][k][:, c * TC:(c + 1) * TC],
                              deps=[fst["xbt_war"]] if k == 0 else ()))
        etoks = []
        last_mm = None
        for mq in range(4):
            for kg in range(4):
                slot = kg % 2
                wl = []
                for j in range(4):
                    k = kg * 4 + j
                    wl.append(E.dma("scalar", wfb[slot][:, j * 2048:(j + 1) * 2048],
                                    dram["fb0_s"][k],
                                    deps=[fst["wfb_war"][slot]] if j == 0 else ()))
                for j in range(4):
                    k = kg * 4 + j
                    for mloc in range(4):
                        m = mq * 4 + mloc
                        deps = ()
                        if mloc == 0:
                            deps = [wl[j], xlds[k]]
                        if k == 0:
                            deps = list(deps) + [fst["bank"][mloc], stok]
                        last_mm = mm(E, fbanks[mloc],
                                     wfb[slot][:, j * 2048 + m * 128:
                                               j * 2048 + m * 128 + 128],
                                     xbt[:, k * 512:(k + 1) * 512],
                                     start=(k == 0), stop=(k == 15),
                                     deps=deps, inc=(j == 3 and mloc == 3))
                fst["wfb_war"][slot] = last_mm
            for mloc in range(4):
                m = mq * 4 + mloc
                ev = act(E, oc[m], fbanks[mloc], AF.Copy, scale=ssb[:, m:m + 1],
                         deps=[last_mm] + fst["oc_war"][m])
                fst["bank"][mloc] = ev
                etoks.append(ev)
        fst["xbt_war"] = last_mm
        mt2 = None
        for m in range(NT_D):
            sqt = act(E, sq, oc[m][:, :], AF.Square, deps=[etoks[m], war.get("sq")])
            mt2 = mm(E, fbanks[0][0:1, :], ones_col[:, 0:1], sq[:, 0:512],
                     start=(m == 0), stop=(m == NT_D - 1),
                     deps=[sqt, fst["bank"][0], E.consts_tok] if m == 0 else [sqt],
                     inc=True)
            war["sq"] = mt2
        vrow = rowtmp[0:1, 0:512]
        frow = rowtmp[0:1, 512:1024]
        trow = rowtmp[0:1, 1024:1536]
        grow = rowtmp[0:1, 1536:2048]
        hrow = rowtmp[0:1, 2048:2560]
        xrow = rowtmp[0:1, 2560:3072]
        srow = rowtmp[0:1, 3072:3584]
        v0 = vts(E, srow, fbanks[0][0:1, :], 1.0, None, ALU.mult,
                 deps=[mt2, war.get("rows2")])
        v1 = act(E, xrow, srow, AF.Sqrt, deps=[v0])
        v2 = vts(E, vrow, xrow, FFN_EPS, None, ALU.max, deps=[v1])
        th_ = act(E, trow, vrow, AF.Tanh, deps=[v2])
        dn = vts(E, xrow, vrow, FFN_EPS, None, ALU.add, deps=[v2, v1])
        rc = vrecip(E, grow, xrow, deps=[dn])
        f1 = vtt(E, frow, trow, grow, ALU.mult, deps=[th_, rc])
        on = vtt(E, xrow, frow, vrow, ALU.mult, deps=[f1, rc])
        on2 = vts(E, trow, xrow, FFN_EPS, None, ALU.max, deps=[on, th_])
        rc2 = vrecip(E, xrow, trow, deps=[on2])
        g1 = vts(E, grow, xrow, 0.99, 1.0, ALU.mult, ALU.min, deps=[rc2])
        hf = vtt(E, hrow, frow, grow, ALU.mult, deps=[g1, f1])
        bht = mm(E, fbanks[1][:, :], ones_row[0:1, :], hrow, start=True, stop=True,
                 deps=[hf, fst["bank"][1], E.consts_tok2])
        war["rows2"] = [bht, hf]
        ttoks = []
        for m in range(NT_D):
            ap1 = vtt(E, oc[m], oc[m][:], fbanks[1][:, :], ALU.mult,
                      deps=[bht, etoks[m]])
            lx = E.dma("scalar", fxt[m % 2], dram["x1_s"][m][:, c * TC:(c + 1) * TC],
                       deps=[fst["xt_war"][m % 2]])
            at = vtt(E, oc[m], oc[m][:], fxt[m % 2][:], ALU.add, deps=[ap1, lx])
            fst["xt_war"][m % 2] = at
            ttoks.append(at)
        ws = war.get("stat")
        war["stat"] = [v0] + (ws if isinstance(ws, list) else [ws])
        wb2 = war.get("bc")
        war["bc"] = [ttoks[-1]] + (wb2 if isinstance(wb2, list) else [wb2])
        lnt = layernorm_tiles(E, fbanks[0], fbanks[1],
                              [oc[i][:] for i in range(NT_D)], ttoks, TC, msb, sq,
                              war)
        fst["bank"][0] = [fst["bank"][0], lnt[-1]]
        fst["bank"][1] = [fst["bank"][1], lnt[-1]]
        dbs = []
        for i in range(NT_D):
            bt = act(E, x2b, oc[i][:], AF.Copy, deps=[lnt[i], war.get("x2b")])
            da = E.dma("gpsimd", dram["x2_s"][i][:, c * TC:(c + 1) * TC], oc[i],
                       deps=[lnt[i]])
            db = E.dma("gpsimd", dram["x2b_s"][i][:, c * TC:(c + 1) * TC], x2b,
                       deps=[bt])
            war["x2b"] = db
            fst["oc_war"][i] = [bt, da, db]
            dbs.append(db)
        fst["x2db"][c] = dbs

    def src_toks(c, m):
        t = list(fst["x2db"].get(c, []))
        if c > 0:
            t += fst["x2db"].get(c - 1, [])
        return t

    proj_phase(E, dram, ar, ps, ntile=NT_D, nheads=1, src="x2b_s", cwn="cw1",
               cbn="cb1", xpn="xp1T", dtn="dt1T", dtbn="dtb1", u_s="u1_s",
               du_s="du1_s", de_s="de1_s", bc_s="bc1_s", silu=False,
               pre_chunk=f0_chunk, src_toks=src_toks)


@phase("W1", needs={"wqgT": "w", "wquT": "w", "wqdT": "w", "woT": "w",
                    "fw0": "w"},
       produces=["wgb_s", "wub_s", "wdb_s", "wob_s", "fb0_s"])
def p_w1(E, dram, ar, ps):
    HW = HID // 2
    it = [ar.alloc(HW, I32) for _ in range(2)]
    bt = [ar.alloc(HW, BF16) for _ in range(2)]
    ar.base = ar.off
    state = {"war": [None, None], "dtok": [None, None], "i": 0}
    jobs = []
    for srcn, dst, nstrip, width in [("woT", "wob_s", 32, D),
                                     ("fw0", "fb0_s", 16, D),
                                     ("wqgT", "wgb_s", 16, HID),
                                     ("wquT", "wub_s", 16, HID),
                                     ("wqdT", "wdb_s", NSLAB, D)]:
        nh = (width + HW - 1) // HW
        for k in range(nstrip):
            for hwi in range(nh):
                lo = hwi * HW
                hi = min(width, lo + HW)
                jobs.append((srcn, dst, k, lo, hi))

    def make_step(j):
        srcn, dst, k, lo, hi = j

        def step():
            slot = state["i"] % 2
            state["i"] += 1
            n = hi - lo
            if srcn in ("woT", "fw0"):
                src_i = it[slot][:, 0:n].bitcast(F32)
                bias = 0.0
            else:
                src_i = it[slot][:, 0:n]
                bias = -128.0
            dst_b = bt[slot][:, 0:n]
            t = E.dma("sync", src_i, dram[srcn][k][:, lo:hi],
                      deps=[state["war"][slot]])
            cv = act(E, dst_b, src_i, AF.Copy, bias=bias,
                     deps=[t, state["dtok"][slot]])
            state["war"][slot] = cv
            state["dtok"][slot] = E.dma("gpsimd", dram[dst][k][:, lo:hi],
                                        dst_b, deps=[cv])

        return step

    E.pump_steps = [make_step(j) for j in jobs]


@phase("B1F1",
       needs={"u1_s": "s", "du1_s": "s", "de1_s": "s", "bc1_s": "s", "x2_s": "s",
              "wgb_s": "s", "wub_s": "s", "wdb_s": "s",
              "al1": "w", "d1": "w", "sg": "w", "su": "w", "sd": "w"},
       produces=["x3_s", "x3b_s", "h_s", "out_s"])
def p_b1f1(E, dram, ar, ps):
    E.new_phase()
    TC = 256
    TB = 1024
    NCH = S // TC
    yc = [ar.alloc(TC, F32) for _ in range(NT_D)]
    xtmp = [ar.alloc(512, F32) for _ in range(2)]
    x3b = ar.alloc(TC, BF16)
    msb = ar.alloc(4 * 512, F32)
    sq = ar.alloc(512, F32)
    sgb = ar.alloc(NSLAB, F32)
    sub = ar.alloc(NSLAB, F32)
    sdb = ar.alloc(NT_D, F32)
    x3bb = ar.alloc(16 * TB, BF16)
    wgt = [ar.alloc(16 * 128, BF16) for _ in range(2)]
    wut = [ar.alloc(16 * 128, BF16) for _ in range(2)]
    wdt = [ar.alloc(512, BF16) for _ in range(4)]
    hdt = [ar.alloc(512, BF16) for _ in range(4)]
    gs = ar.alloc(TB, BF16)
    us = ar.alloc(TB, BF16)
    htmp = [ar.alloc(TB, BF16) for _ in range(2)]
    oc = [ar.alloc(512, F32) for _ in range(NT_D)]
    fxtmp = [ar.alloc(512, F32) for _ in range(2)]

    svec = load_pp_vec(E, dram, "sg", sgb, NSLAB)
    svec += load_pp_vec(E, dram, "su", sub, NSLAB)
    svec += load_pp_vec(E, dram, "sd", sdb, NT_D)
    ptr_barrier(E, svec)

    banksA = [ps["A"][:, i * 512:(i + 1) * 512] for i in range(4)]
    banksB = [ps["B"][:, i * 512:(i + 1) * 512] for i in range(4)]

    st = {"war": {}, "ttoks": [None] * NT_D, "yc_war": [None] * NT_D,
          "x3db": [[] for _ in range(NCH)],
          "hb_war": [None] * NSLAB, "grp_war": [None, None], "gs_war": None,
          "wg_war": [None, None], "wd_war": [None] * 4,
          "hd_war": [None] * 4,
          "x3bb_war": [], "xt_war": [None, None],
          "abank": [None] * 4, "oc_war": [[] for _ in range(NT_D)],
          "fwar": {}, "out_last": None}
    war = st["war"]

    def post(c, tile, yv, y2tok):
        lx = E.dma("scalar", xtmp[tile % 2][:, 0:TC],
                   dram["x2_s"][tile][:, c * TC:(c + 1) * TC],
                   deps=[st["xt_war"][tile % 2]])
        at = vtt(E, yc[tile], yv[:], xtmp[tile % 2][:, 0:TC], ALU.add,
                 deps=[y2tok, lx, st["yc_war"][tile]])
        st["xt_war"][tile % 2] = at
        st["ttoks"][tile] = at
        if tile == NT_D - 1:
            lnt = layernorm_tiles(E, banksB[2], banksB[3],
                                  [yc[i][:] for i in range(NT_D)], st["ttoks"],
                                  TC, msb, sq, war)
            last = None
            for i in range(NT_D):
                bt = act(E, x3b, yc[i][:], AF.Copy, deps=[lnt[i], war.get("x3b")])
                da = E.dma("gpsimd", dram["x3_s"][i][:, c * TC:(c + 1) * TC],
                           yc[i], deps=[lnt[i]])
                db = E.dma("gpsimd", dram["x3b_s"][i][:, c * TC:(c + 1) * TC],
                           x3b, deps=[bt])
                war["x3b"] = db
                st["yc_war"][i] = [bt, da, db]
                st["x3db"][c].append(db)
                last = db
            return last
        return at

    def f_x3load(blk):
        toks = []
        nb = NCH // 2
        dbs = [t for c in range(blk * nb, blk * nb + nb) for t in st["x3db"][c]]
        for k in range(NT_D):
            toks.append(E.dma("sync", x3bb[:, k * TB:(k + 1) * TB],
                              dram["x3b_s"][k][:, blk * TB:(blk + 1) * TB],
                              deps=(dbs if k == 0 else []) + st["x3bb_war"]))
        st["xtk"] = toks

    def f_wload(blk, slab):
        slot = slab % 2
        wg_l, wu_l = [], []
        for k in range(NT_D):
            wg_l.append(E.dma("sync", wgt[slot][:, k * 128:(k + 1) * 128],
                              dram["wgb_s"][k][:, slab * 128:(slab + 1) * 128],
                              deps=[st["wg_war"][slot]] if k == 0 else ()))
            wu_l.append(E.dma("sync", wut[slot][:, k * 128:(k + 1) * 128],
                              dram["wub_s"][k][:, slab * 128:(slab + 1) * 128]))
        st["wg_l"], st["wu_l"] = wg_l, wu_l

    def f_slab(blk, slab):
        slot = slab % 2
        wg_l, wu_l = st["wg_l"], st["wu_l"]
        mg = None
        mu = None
        for k in range(NT_D):
            for th in range(2):
                deps = [wg_l[k], st["xtk"][k]] if th == 0 else ()
                if k == 0:
                    deps = list(deps) + [st["abank"][th]]
                mg = mm(E, banksA[th],
                        wgt[slot][:, k * 128:(k + 1) * 128],
                        x3bb[:, k * TB + th * 512: k * TB + th * 512 + 512],
                        start=(k == 0), stop=(k == NT_D - 1),
                        deps=deps, inc=(k == NT_D - 1 and th == 1))
            for th in range(2):
                deps = [wu_l[k]] if th == 0 else ()
                if k == 0:
                    deps = list(deps) + [st["abank"][2 + th]]
                mu = mm(E, banksA[2 + th],
                        wut[slot][:, k * 128:(k + 1) * 128],
                        x3bb[:, k * TB + th * 512: k * TB + th * 512 + 512],
                        start=(k == 0), stop=(k == NT_D - 1),
                        deps=deps, inc=(k == NT_D - 1 and th == 1))
        pp = ps["A"]
        ge = act(E, gs, pp[:, 0:1024], AF.Silu, scale=sgb[:, slab:slab + 1],
                 deps=[mg, st["gs_war"]])
        ue = act(E, us, pp[:, 1024:2048], AF.Copy, scale=sub[:, slab:slab + 1],
                 deps=[mu])
        ht = vtt(E, htmp[slot], gs[:], us[:], ALU.mult,
                 deps=[ge, ue, st["hb_war"][slab]])
        hd = E.dma("gpsimd", dram["h_s"][slab][:, blk * TB:(blk + 1) * TB],
                   htmp[slot], deps=[ht])
        st["gs_war"] = ht
        st["hb_war"][slab] = hd
        st["abank"][0] = ge
        st["abank"][1] = ge
        st["abank"][2] = ue
        st["abank"][3] = ue
        st["wg_war"][slot] = ht

    def f_down(blk, th, q, kg):
        mt = None
        lds = {}
        mtk = {}

        def issue_load(ki):
            k = kg * 11 + ki
            slot = k % 4
            dep = mtk.get(k - 4, st["wd_war"][slot])
            wl = E.dma("sync", wdt[slot],
                       dram["wdb_s"][k][:, q * 512:(q + 1) * 512], deps=[dep])
            hl = E.dma("sync", hdt[slot],
                       dram["h_s"][k][:, blk * TB + th * 512: blk * TB + th * 512 + 512],
                       deps=[st["hb_war"][k], dep])
            lds[k] = (wl, hl)

        for ki in range(4):
            issue_load(ki)
        for ki in range(11):
            k = kg * 11 + ki
            slot = k % 4
            wl, hl = lds[k]
            for mloc in range(4):
                deps = [wl, hl] if mloc == 0 else ()
                if k == 0:
                    deps = list(deps) + [st["abank"][mloc]] + svec
                mt = mm(E, banksA[mloc],
                        wdt[slot][:, mloc * 128:(mloc + 1) * 128],
                        hdt[slot][:],
                        start=(k == 0), stop=(k == NSLAB - 1),
                        deps=deps, inc=(mloc == 3))
            mtk[k] = mt
            st["wd_war"][slot] = mt
            st["hd_war"][slot] = mt
            if ki + 4 < 11:
                issue_load(ki + 4)
        if kg == 3:
            for mloc in range(4):
                m = q * 4 + mloc
                ev = act(E, oc[m], banksA[mloc], AF.Copy, scale=sdb[:, m:m + 1],
                         deps=[mt] + st["oc_war"][m])
                st["abank"][mloc] = ev
                st["fwar"][("ev", m)] = ev

    def f_ln(blk, th):
        fwar = st["fwar"]
        ttoks = []
        for m in range(NT_D):
            lx = E.dma("sync", fxtmp[m % 2],
                       dram["x3_s"][m][:, blk * TB + th * 512: blk * TB + th * 512 + 512],
                       deps=[fwar.get(("xt", m % 2))])
            at = vtt(E, oc[m], oc[m][:], fxtmp[m % 2][:], ALU.add,
                     deps=[fwar[("ev", m)], lx])
            fwar[("xt", m % 2)] = at
            ttoks.append(at)
        lnt = layernorm_tiles(E, banksB[2], banksB[3],
                              [oc[i][:] for i in range(NT_D)], ttoks, 512, msb, sq,
                              st["war"])
        for i in range(NT_D):
            da = E.dma("gpsimd",
                       dram["out_s"][i][:, blk * TB + th * 512: blk * TB + th * 512 + 512],
                       oc[i], deps=[lnt[i]])
            st["oc_war"][i] = [da]
            st["out_last"] = da
        if th == 1 and blk == 0:
            st["x3bb_war"] = [st["hb_war"][NSLAB - 1]]

    def register_block(blk):
        steps = [lambda blk=blk: f_x3load(blk)]
        for slab in range(NSLAB):
            steps.append(lambda blk=blk, slab=slab: f_wload(blk, slab))
            steps.append(lambda blk=blk, slab=slab: f_slab(blk, slab))
        for th in range(2):
            for q in range(4):
                for kg in range(4):
                    steps.append(lambda blk=blk, th=th, q=q, kg=kg:
                                 f_down(blk, th, q, kg))
            steps.append(lambda blk=blk, th=th: f_ln(blk, th))
        if E.pump_steps:
            E.pump_steps.extend(steps)
        else:
            E.pump_steps = steps

    def b1_cb(c, stores):
        if c == NCH // 2 - 1:
            register_block(0)
        elif c == NCH - 1:
            register_block(1)

    scan_core(E, dram, ar, ps, ntile=NT_D, nheads=1, u_s="u1_s", du_s="du1_s",
              de_s="de1_s", bc_s="bc1_s", al="al1", dvec="d1", post=post, TC=TC,
              chunk_cb=b1_cb)
    while E.pump_steps:
        E.pump()


def make_consts(E, ar):
    nc = E.nc
    ones_col = nc.alloc_sbuf_tensor("ones_col", [128, 1], F32)
    ones_row = nc.alloc_sbuf_tensor("ones_row", [1, 128], F32)
    scr = nc.alloc_sbuf_tensor("scr", [128, 8], F32)
    cvals = nc.alloc_sbuf_tensor("cvals", [128, 4], F32)
    E.consts = {"ones_col": ones_col, "ones_row": ones_row, "scr": scr}
    E.new_phase()
    E.consts_tok = vmemset(E, ones_col[:], 1.0)
    E.consts_tok2 = vmemset(E, ones_row[:], 1.0)
    vmemset(E, scr[:], 0.0)
    for i, v in enumerate([LN_EPS, FFN_EPS, 0.99]):
        E.consts_tok2 = vmemset(E, cvals[:, i:i + 1], v)
        nc.const_aps.register(F32, v, cvals[:, i:i + 1])


_BUILD_CACHE = {}


def build_launch(group, final_outs):
    key = (tuple(group), tuple(sorted(final_outs)))
    if key in _BUILD_CACHE:
        return _BUILD_CACHE[key]
    nc = bass.Bass("TRN2", target_bir_lowering=False)
    stack = ExitStack()
    produced = set()
    ext_in = set()
    for g in group:
        _, needs, produces = PHASES[g]
        for n in needs:
            if n not in produced:
                ext_in.add(n)
        produced.update(produces)
    ext_out = set(final_outs) & produced
    internal = produced - ext_out
    dram = {}
    for n in sorted(ext_in):
        shape, dt = SPEC[n]
        dram[n] = nc.dram_tensor(n, list(shape), dt, kind="ExternalInput")
    for n in sorted(ext_out):
        shape, dt = SPEC[n]
        dram[n] = nc.dram_tensor(n, list(shape), dt, kind="ExternalOutput")
    for n in sorted(internal):
        shape, dt = SPEC[n]
        dram[n] = nc.dram_tensor(n, list(shape), dt, kind="Internal")

    E = Emit(nc, stack)
    ar = Arena(nc, 206 * 1024)
    psA = nc.alloc_psum_tensor("psA", [128, 2048], F32)
    psB = nc.alloc_psum_tensor("psB", [128, 2048], F32)
    ps = {"A": psA, "B": psB}
    make_consts(E, ar)
    for g in group:
        fn, _, _ = PHASES[g]
        ar.reset()
        fn(E, dram, ar, ps)
        if g == "A2":
            while E.pump_steps:
                E.pump()
            ar.base = 0
    final_deps = [(s, E.semcnt[s]) for s in E.semcnt if E.semcnt[s] > 0]
    E.wait_only("gpsimd", final_deps)
    E.finalize()
    res = (nc, sorted(ext_in), sorted(ext_out))
    _BUILD_CACHE[key] = res
    return res


def prep_weights(inputs):
    def C(a):
        return np.ascontiguousarray(np.asarray(a))

    w = {}
    ip = np.asarray(inputs["L0_attn_in_proj_w"])
    w["wxT"] = C(ip[0::2].T).reshape(16, 128, I4)
    w["wgT"] = C(ip[1::2].T).reshape(16, 128, I4)
    w["cw0"] = C(inputs["L0_attn_conv_w"])
    w["cb0"] = C(inputs["L0_attn_conv_b"])
    xp = np.asarray(inputs["L0_attn_x_proj_weight"])
    w["xp0T"] = C(np.transpose(xp, (0, 2, 1))).reshape(NH, 16, 128, 160)
    dt = np.asarray(inputs["L0_attn_dt_proj_weight"])
    w["dt0T"] = C(np.transpose(dt, (0, 2, 1)))
    w["dtb0"] = C(np.asarray(inputs["L0_attn_dt_proj_bias"]).reshape(-1))
    w["al0"] = C(np.asarray(inputs["L0_attn_A_log"]).reshape(I4, NST))
    w["d0"] = C(np.asarray(inputs["L0_attn_D"]).reshape(-1))
    w["woT"] = C(np.asarray(inputs["L0_attn_out_proj_w"]).T).reshape(32, 128, D)
    w["fw0"] = C(inputs["L0_ffn_weight"]).reshape(16, 128, D)
    w["fs0"] = C(inputs["L0_ffn_scale"])
    w["cw1"] = C(np.asarray(inputs["L1_attn_conv_w"]).reshape(D, KC))
    w["cb1"] = C(inputs["L1_attn_conv_b"])
    w["xp1T"] = C(np.asarray(inputs["L1_attn_xproj_w"]).T).reshape(16, 128, 160)
    w["dt1T"] = C(np.asarray(inputs["L1_attn_dtproj_w"]).T)
    w["dtb1"] = C(inputs["L1_attn_dtproj_b"])
    w["al1"] = C(inputs["L1_attn_A_log"])
    w["d1"] = C(inputs["L1_attn_D_skip"])
    w["wqgT"] = C(np.asarray(inputs["L1_ffn_wq_gate"]).T).reshape(16, 128, HID)
    w["wquT"] = C(np.asarray(inputs["L1_ffn_wq_up"]).T).reshape(16, 128, HID)
    w["wqdT"] = C(np.asarray(inputs["L1_ffn_wq_down"]).T).reshape(NSLAB, 128, D)
    w["sg"] = C(inputs["L1_ffn_s_gate"])
    w["su"] = C(inputs["L1_ffn_s_up"])
    w["sd"] = C(inputs["L1_ffn_s_down"])
    for k in w:
        assert w[k].shape == SPEC[k][0], (k, w[k].shape, SPEC[k][0])
        want = np.int32 if SPEC[k][1] == I32 else np.float32
        assert w[k].dtype == want, (k, w[k].dtype)
    return w


LAUNCH_GROUPS = [["W1", "A1", "A2", "B0C", "F0L1", "B1F1"]]
FINAL = "out_s"


def kernel(**inputs):
    w = prep_weights(inputs)
    x = np.asarray(inputs["L0_attn_hidden_states"])
    pool = [dict(w) for _ in range(B)]
    for i in range(B):
        pool[i]["x0"] = np.ascontiguousarray(x[i].T)

    groups = LAUNCH_GROUPS
    for gi, group in enumerate(groups):
        later_needs = set()
        for g2 in [g for grp in groups[gi + 1:] for g in grp]:
            later_needs.update(PHASES[g2][1].keys())
        produced = set()
        for g in group:
            produced.update(PHASES[g][2])
        outs_needed = (produced & later_needs) | ({FINAL} & produced)
        nc, ins, outs = build_launch(group, outs_needed)
        in_maps = [{n: pool[i][n] for n in ins} for i in range(B)]
        res = run_bass_kernel_spmd(nc, in_maps, core_ids=list(range(B)))
        for i in range(B):
            for n in outs:
                pool[i][n] = res.results[i][n]

    out = np.stack([pool[i][FINAL].reshape(D, S).T for i in range(B)])
    return np.ascontiguousarray(out.astype(np.float32))


if __name__ == "__main__":
    pass


# version 13
# speedup vs baseline: 1.1220x; 1.0141x over previous
import numpy as np
import ml_dtypes
from contextlib import ExitStack

import concourse.bass as bass
import concourse.mybir as mybir
from concourse.bass_utils import run_bass_kernel_spmd

F32 = mybir.dt.float32
BF16 = mybir.dt.bfloat16
I32 = mybir.dt.int32
AF = mybir.ActivationFunctionType
ALU = mybir.AluOpType
BF16NP = ml_dtypes.bfloat16

B = 8
S = 2048
D = 2048
I4 = 4096
NH = 2
HD = 2048
NST = 16
KC = 4
HID = 5632
NSLAB = HID // 128

NT_D = D // 128
NT_I = I4 // 128

LN_EPS = 1e-5
FFN_EPS = 1e-7

ENGINES = ("sync", "tensor", "vector", "scalar", "gpsimd")


def _flat(deps):
    out = []
    for d in deps:
        if d is None:
            continue
        if isinstance(d, list):
            for x in d:
                if x is not None:
                    out.append(x)
        else:
            out.append(d)
    return out


class Emit:
    def __init__(self, nc, stack):
        self.nc = nc
        self.stack = stack
        self.recs = {e: [] for e in ENGINES}
        self.last_wait = {e: {} for e in ENGINES}
        self.semcnt = {}
        self.cur = None
        self.dmas = None
        self._semi = 0
        self.consts = {}

    def _new_sem(self):
        self._semi += 1
        h = self.stack.enter_context(self.nc.semaphore(f"s{self._semi}"))
        self.semcnt[h] = 0
        return h

    NPOOL = 12

    def new_phase(self):
        barrier = [(s, c) for s, c in self.semcnt.items() if c > 0]
        self.cur = {e: self._new_sem() for e in ("tensor", "vector", "scalar", "gpsimd")}
        if not hasattr(self, "pools"):
            self.pools = {q: [[self._new_sem(), None] for _ in range(self.NPOOL)]
                          for q in ("sync", "gpsimd", "scalar")}
            self.pool_idx = {q: 0 for q in self.pools}
        if barrier:
            for e in ENGINES:
                self._record(e, None, barrier, None, 0)

    def _record(self, eng, fn, deps, sem, amt):
        waits = []
        for dsem, dval in _flat(deps):
            if dval <= self.last_wait[eng].get(dsem, -1):
                continue
            waits.append((dsem, dval))
            self.last_wait[eng][dsem] = dval
        self.recs[eng].append((waits, fn, sem, amt))

    def op(self, eng, fn, deps=(), inc=True):
        sem = self.cur[eng] if inc else None
        tok = None
        if inc:
            self.semcnt[sem] += 1
            tok = (sem, self.semcnt[sem])
        self._record(eng, fn, deps, sem, 1)
        return tok

    def dma(self, q, out, in_, deps=(), slow=False):
        slot = self.pools[q][self.pool_idx[q] % self.NPOOL]
        self.pool_idx[q] += 1
        sem, last = slot
        self.semcnt[sem] += 16
        kw = {"allow_slow_non_contiguous": True} if slow else {}

        def fn(e, out=out, in_=in_, kw=kw):
            return e.dma_start(out=out, in_=in_, **kw)

        self._record(q, fn, list(deps) + [last], sem, 16)
        tok = (sem, self.semcnt[sem])
        slot[1] = tok
        return tok

    def wait_only(self, eng, deps):
        self._record(eng, None, deps, None, 0)

    pump_steps = None

    def pump(self):
        if self.pump_steps:
            self.pump_steps.pop(0)()

    def finalize(self):
        with self.nc.Block() as block:
            for eng in ENGINES:
                recs = self.recs[eng]
                if not recs:
                    continue

                def runner(e, recs=recs):
                    for waits, fn, sem, amt in recs:
                        for (s, v) in waits:
                            e.wait_ge(s, v)
                        if fn is not None:
                            inst = fn(e)
                            if sem is not None and inst is not None:
                                inst.then_inc(sem, amt)

                getattr(block, eng)(runner)


class Arena:
    def __init__(self, nc, nbytes):
        self.t = nc.alloc_sbuf_tensor("arena", [128, nbytes], mybir.dt.uint8)
        self.nbytes = nbytes
        self.off = 0

    def alloc(self, cols, dtype):
        esz = {F32: 4, BF16: 2, I32: 4}[dtype]
        self.off = (self.off + 31) & ~31
        start = self.off
        self.off += cols * esz
        assert self.off <= self.nbytes, f"arena overflow: {self.off} > {self.nbytes}"
        return self.t[:, start : start + cols * esz].bitcast(dtype)

    def reset(self, off=None):
        self.off = self.base if off is None else off

    base = 0


PHASES = {}


def phase(name, needs, produces):
    def deco(fn):
        PHASES[name] = (fn, needs, produces)
        return fn

    return deco


SPEC = {
    "x0": ((D, S), F32),
    "wxT": ((16, 128, I4), F32),
    "wgT": ((16, 128, I4), F32),
    "cw0": ((I4, KC), F32),
    "cb0": ((I4,), F32),
    "xp0T": ((NH, 16, 128, 160), F32),
    "dt0T": ((NH, 128, HD), F32),
    "dtb0": ((I4,), F32),
    "al0": ((I4, NST), F32),
    "d0": ((I4,), F32),
    "woT": ((32, 128, D), F32),
    "fw0": ((16, 128, D), F32),
    "fs0": ((D,), F32),
    "cw1": ((D, KC), F32),
    "cb1": ((D,), F32),
    "xp1T": ((16, 128, 160), F32),
    "dt1T": ((128, D), F32),
    "dtb1": ((D,), F32),
    "al1": ((D, NST), F32),
    "d1": ((D,), F32),
    "wqgT": ((16, 128, HID), I32),
    "wquT": ((16, 128, HID), I32),
    "wqdT": ((NSLAB, 128, D), I32),
    "sg": ((HID,), F32),
    "su": ((HID,), F32),
    "sd": ((D,), F32),
    "xb_s": ((NT_I, 128, S), BF16),
    "gate_s": ((NT_I, 128, S), BF16),
    "u0_s": ((NT_I, 128, S), BF16),
    "du0_s": ((NT_I, 128, S), BF16),
    "de0_s": ((NT_I, 128, S), BF16),
    "bc0_s": ((NH, 32, S), BF16),
    "ybf_s": ((NT_I, 128, S), BF16),
    "xr0_s": ((NT_D, 128, S), F32),
    "x1_s": ((NT_D, 128, S), F32),
    "x1b_s": ((NT_D, 128, S), BF16),
    "x2_s": ((NT_D, 128, S), F32),
    "x2b_s": ((NT_D, 128, S), BF16),
    "u1_s": ((NT_D, 128, S), BF16),
    "du1_s": ((NT_D, 128, S), BF16),
    "de1_s": ((NT_D, 128, S), BF16),
    "bc1_s": ((1, 32, S), BF16),
    "x3_s": ((NT_D, 128, S), F32),
    "x3b_s": ((NT_D, 128, S), BF16),
    "wgb_s": ((16, 128, HID), BF16),
    "wub_s": ((16, 128, HID), BF16),
    "wdb_s": ((NSLAB, 128, D), BF16),
    "wob_s": ((32, 128, D), BF16),
    "h_s": ((NSLAB, 128, S), BF16),
    "out_s": ((NT_D, 128, S), F32),
}


def act(E, out, in_, func, bias=0.0, scale=1.0, deps=()):
    def fn(e, out=out, in_=in_, func=func, bias=bias, scale=scale):
        return e.activation(out=out, in_=in_, func=func, bias=bias, scale=scale)

    return E.op("scalar", fn, deps=deps)


def vtt(E, out, in0, in1, alu, deps=(), eng="vector"):
    def fn(e, out=out, in0=in0, in1=in1, alu=alu):
        return e.tensor_tensor(out=out, in0=in0, in1=in1, op=alu)

    return E.op(eng, fn, deps=deps)


def vts(E, out, in0, s1, s2, op0, op1=None, deps=(), eng="vector"):
    def fn(e, out=out, in0=in0, s1=s1, s2=s2, op0=op0, op1=op1):
        if op1 is None:
            return e.tensor_scalar(out=out, in0=in0, scalar1=s1, scalar2=None, op0=op0)
        return e.tensor_scalar(out=out, in0=in0, scalar1=s1, scalar2=s2, op0=op0, op1=op1)

    return E.op(eng, fn, deps=deps)


def vstt(E, out, in0, scalar, in1, op0, op1, deps=(), eng="vector"):
    def fn(e, out=out, in0=in0, scalar=scalar, in1=in1, op0=op0, op1=op1):
        return e.scalar_tensor_tensor(out=out, in0=in0, scalar=scalar, in1=in1,
                                      op0=op0, op1=op1)

    return E.op(eng, fn, deps=deps)


def vcopy(E, out, in_, deps=(), eng="vector"):
    def fn(e, out=out, in_=in_):
        return e.tensor_copy(out=out, in_=in_)

    return E.op(eng, fn, deps=deps)


def vmemset(E, ap, val, deps=(), eng="vector"):
    def fn(e, ap=ap, val=val):
        return e.memset(ap, val)

    return E.op(eng, fn, deps=deps)


def vscan(E, out, d0, d1, initial, deps=()):
    def fn(e, out=out, d0=d0, d1=d1, initial=initial):
        return e.tensor_tensor_scan(out=out, data0=d0, data1=d1, initial=initial,
                                    op0=ALU.mult, op1=ALU.add)

    return E.op("vector", fn, deps=deps)


def vrecip(E, out, in_, deps=()):
    def fn(e, out=out, in_=in_):
        return e.reciprocal(out=out, in_=in_)

    return E.op("vector", fn, deps=deps)


def mm(E, out, lhsT, rhs, start, stop, deps=(), inc=None):
    if inc is None:
        inc = stop

    def fn(e, out=out, lhsT=lhsT, rhs=rhs, start=start, stop=stop):
        return e.matmul(out, lhsT, rhs, start=start, stop=stop)

    return E.op("tensor", fn, deps=deps, inc=inc)


def ptr_barrier(E, deps):
    scr = E.consts["scr"]
    for eng in ("scalar", "vector"):
        E.wait_only(eng, deps)
        for i in range(10):
            if eng == "scalar":
                E.op(eng, lambda e: e.activation(out=scr[0:1, 0:1], in_=scr[0:1, 1:2],
                                                 func=AF.Copy), inc=False)
            else:
                E.op(eng, lambda e: e.tensor_copy(out=scr[0:1, 2:3], in_=scr[0:1, 3:4]),
                     inc=False)


def load_pp_vec(E, dram, name, sb, ncols, deps=()):
    toks = []
    for k in range(ncols):
        src = dram[name][k * 128:(k + 1) * 128].unsqueeze(1)
        toks.append(E.dma("sync", sb[:, k:k + 1], src, deps=deps if k == 0 else (),
                          slow=True))
    return toks


def conv4(E, acc, xbc, cw, m, bias_col, deps):
    TC = acc[0].shape[-1]
    a, b = acc
    if bias_col is not None:
        t0 = vts(E, a, xbc[:, 0:TC], cw[:, m * KC:m * KC + 1], bias_col,
                 ALU.mult, ALU.add, deps=deps)
    else:
        t0 = vts(E, a, xbc[:, 0:TC], cw[:, m * KC:m * KC + 1], None, ALU.mult,
                 deps=deps)
    t1 = vstt(E, b, xbc[:, 1:TC + 1], cw[:, m * KC + 1:m * KC + 2], a, ALU.mult,
              ALU.add, deps=[t0])
    t2 = vstt(E, a, xbc[:, 2:TC + 2], cw[:, m * KC + 2:m * KC + 3], b, ALU.mult,
              ALU.add, deps=[t1])
    t3 = vstt(E, b, xbc[:, 3:TC + 3], cw[:, m * KC + 3:m * KC + 4], a, ALU.mult,
              ALU.add, deps=[t2])
    return t3


def layernorm_tiles(E, stat_bank, bc_bank, tiles, tile_toks, tc, msb, sq, war):
    ones_col = E.consts["ones_col"]
    ones_row = E.consts["ones_row"]
    ntile = len(tiles)
    ntot = ntile * 128
    ps_s = stat_bank[0:1, 0:tc]
    ps_b = bc_bank[:, 0:tc]

    mt1 = None
    for i, tl in enumerate(tiles):
        mt1 = mm(E, ps_s, ones_col[:, 0:1], tl, start=(i == 0), stop=(i == ntile - 1),
                 deps=[tile_toks[i], war.get("stat"), E.consts_tok] if i == 0
                 else [tile_toks[i]], inc=(i == ntile - 1))
    m_row = msb[0:1, 0:tc]
    e2_row = msb[0:1, tc:2 * tc]
    r_row = msb[0:1, 2 * tc:3 * tc]
    t_row = msb[0:1, 3 * tc:4 * tc]
    a = vts(E, m_row, ps_s, 1.0 / ntot, None, ALU.mult, deps=[mt1, war.get("rows")])
    mt2 = None
    sq_toks = []
    for i, tl in enumerate(tiles):
        sqt = act(E, sq[:, 0:tc], tl, AF.Square, deps=[tile_toks[i], war.get("sq")])
        sq_toks.append(sqt)
        mt2 = mm(E, ps_s, ones_col[:, 0:1], sq[:, 0:tc], start=(i == 0),
                 stop=(i == ntile - 1),
                 deps=[sqt, a] if i == 0 else [sqt], inc=True)
        war["sq"] = mt2
    bb = vts(E, e2_row, ps_s, 1.0 / ntot, None, ALU.mult, deps=[mt2])
    war["stat"] = bb
    c1 = vtt(E, t_row, m_row, m_row, ALU.mult, deps=[a])
    c2 = vtt(E, e2_row, e2_row, t_row, ALU.subtract, deps=[bb, c1])
    c3 = act(E, t_row, e2_row, AF.Sqrt, bias=LN_EPS, deps=[c2])
    c4 = vrecip(E, r_row, t_row, deps=[c3])
    bmt = mm(E, ps_b, ones_row[0:1, :], m_row, start=True, stop=True,
             deps=[a, war.get("bc"), E.consts_tok2])
    subs = []
    for i, tl in enumerate(tiles):
        subs.append(vtt(E, tl, tl, ps_b, ALU.subtract,
                        deps=[bmt, mt1, mt2, sq_toks[i]]))
    brt = mm(E, ps_b, ones_row[0:1, :], r_row, start=True, stop=True,
             deps=[c4] + subs)
    out = []
    for i, tl in enumerate(tiles):
        out.append(vtt(E, tl, tl, ps_b, ALU.mult, deps=[brt, subs[i]]))
    war["bc"] = out[-1]
    war["rows"] = [bmt, brt]
    return out


@phase("A1", needs={"x0": "w", "wxT": "w", "wgT": "w"}, produces=["xb_s", "gate_s"])
def p_a1(E, dram, ar, ps):
    E.new_phase()
    xbf = ar.alloc(NT_D * S, BF16)
    wtmp = [ar.alloc(2048, F32) for _ in range(2)]
    whalf = ar.alloc(16 * 2048, BF16)
    xe = [ar.alloc(S, BF16) for _ in range(2)]

    wtmp_war = [None, None]
    ctoks = []
    for k in range(NT_D):
        t = E.dma("sync", wtmp[k % 2], dram["x0"][k * 128:(k + 1) * 128, :],
                  deps=[wtmp_war[k % 2]])
        ct = act(E, xbf[:, k * S:(k + 1) * S], wtmp[k % 2][:], AF.Copy, deps=[t])
        wtmp_war[k % 2] = ct
        ctoks.append(ct)
    xready = ctoks[-1]

    grp = [ps["A"], ps["B"]]
    grp_war = [None, None]
    xe_war = [None, None]
    prev_mm = None

    for wname, oname in (("wxT", "xb_s"), ("wgT", "gate_s")):
        for half in range(2):
            wt = None
            for k in range(NT_D):
                t = E.dma("sync", wtmp[k % 2],
                          dram[wname][k, :, half * 2048:(half + 1) * 2048],
                          deps=[wtmp_war[k % 2]])
                wt = act(E, whalf[:, k * 2048:(k + 1) * 2048], wtmp[k % 2][:],
                         AF.Copy, deps=[t, prev_mm])
                wtmp_war[k % 2] = wt
            for mloc in range(16):
                m = half * 16 + mloc
                gi = mloc % 2
                pp = grp[gi]
                mtok = None
                for k in range(NT_D):
                    for tq in range(4):
                        first = (k == 0 and tq == 0)
                        mtok = mm(E, pp[:, tq * 512:(tq + 1) * 512],
                                  whalf[:, k * 2048 + mloc * 128: k * 2048 + mloc * 128 + 128],
                                  xbf[:, k * S + tq * 512: k * S + tq * 512 + 512],
                                  start=(k == 0), stop=(k == NT_D - 1),
                                  deps=[wt, xready, grp_war[gi]] if first else (),
                                  inc=(k == NT_D - 1 and tq == 3))
                prev_mm = mtok
                fu = AF.Silu if oname == "gate_s" else AF.Copy
                ev = act(E, xe[gi], pp[:], fu, deps=[mtok, xe_war[gi]])
                grp_war[gi] = ev
                xe_war[gi] = E.dma("gpsimd", dram[oname][m], xe[gi], deps=[ev])
    if "dbg_s" in dram:
        for k in range(NT_D):
            E.dma("gpsimd", dram["dbg_s"][k], xbf[:, k * S:(k + 1) * S],
                  deps=[prev_mm])


def proj_phase(E, dram, ar, ps, *, ntile, nheads, src, cwn, cbn, xpn, dtn, dtbn,
               u_s, du_s, de_s, bc_s, silu):
    E.new_phase()
    TC = 512
    NCH = S // TC
    tph = ntile // nheads

    cw = ar.alloc(ntile * KC, F32)
    cb = ar.alloc(ntile, F32)
    dtb = ar.alloc(ntile, F32)
    xpb = ar.alloc(nheads * 16 * 160, BF16)
    dttb = ar.alloc(nheads * HD, BF16)
    wtmp = ar.alloc(2048, F32)
    xbc = [ar.alloc(TC + KC, BF16) for _ in range(2)]
    acc = [ar.alloc(TC, F32) for _ in range(2)]
    ubf = ar.alloc(ntile * TC, BF16)
    dtinb = ar.alloc(TC, BF16)
    bct = ar.alloc(TC, BF16)
    t_r = [ar.alloc(TC, F32) for _ in range(2)]
    t_a = [ar.alloc(TC, F32) for _ in range(2)]
    t_l = [ar.alloc(TC, F32) for _ in range(2)]
    de_t = [ar.alloc(TC, BF16) for _ in range(2)]
    du_t = [ar.alloc(TC, BF16) for _ in range(2)]

    vready = []
    for k in range(ntile):
        vready.append(E.dma("sync", cw[:, k * KC:(k + 1) * KC],
                            dram[cwn][k * 128:(k + 1) * 128, :]))
    vready += load_pp_vec(E, dram, cbn, cb, ntile)
    vready += load_pp_vec(E, dram, dtbn, dtb, ntile)
    wtok = None
    for h in range(nheads):
        for k in range(16):
            src_ap = dram[xpn][h, k] if nheads > 1 else dram[xpn][k]
            t = E.dma("sync", wtmp[:, 0:160], src_ap, deps=[wtok])
            wtok = act(E, xpb[:, (h * 16 + k) * 160:(h * 16 + k + 1) * 160],
                       wtmp[:, 0:160], AF.Copy, deps=[t])
    for h in range(nheads):
        src_ap = dram[dtn][h] if nheads > 1 else dram[dtn][:]
        t = E.dma("sync", wtmp, src_ap, deps=[wtok])
        wtok = act(E, dttb[:, h * HD:(h + 1) * HD], wtmp[:], AF.Copy, deps=[t])
    wready = wtok
    ptr_barrier(E, vready)

    ps_dt = ps["A"][:, 0:512]
    ps_bc = ps["A"][0:32, 512:1024]
    ps_dp = [ps["B"][:, 0:512], ps["B"][:, 512:1024]]
    ps_dt_war = None
    ps_bc_war = None
    ps_dp_war = [None, None]
    dtinb_war = None
    bct_war = None
    chain_war = [None, None]
    dd_war = [None, None]

    u_war = [None] * ntile
    xbc_war = [None, None]
    acc_war = None
    for c in range(NCH):
        utoks = [None] * ntile
        ustoks = [None] * ntile
        for m in range(ntile):
            bslot = m % 2
            if c == 0:
                mz = vmemset(E, xbc[bslot][:, 0:KC - 1], 0.0, deps=[xbc_war[bslot]])
                lt = E.dma("sync", xbc[bslot][:, KC - 1:KC - 1 + TC],
                           dram[src][m][:, 0:TC], deps=[mz])
            else:
                lt = E.dma("sync", xbc[bslot][:, 0:TC + KC - 1],
                           dram[src][m][:, c * TC - (KC - 1): c * TC + TC],
                           deps=[xbc_war[bslot]])
            t3 = conv4(E, acc, xbc[bslot], cw, m,
                       None if silu else cb[:, m:m + 1],
                       [lt, vready, u_war[m], acc_war])
            xbc_war[bslot] = t3
            fu = AF.Silu if silu else AF.Copy
            bias = cb[:, m:m + 1] if silu else 0.0
            ub = act(E, ubf[:, m * TC:(m + 1) * TC], acc[1][:, :], fu, bias=bias,
                     deps=[t3])
            acc_war = [ub]
            ustoks[m] = ub
            utoks[m] = ub
            E.dma("gpsimd", dram[u_s][m][:, c * TC:(c + 1) * TC],
                  ubf[:, m * TC:(m + 1) * TC], deps=[ub])
            E.pump()
        prev_head_mt = None
        for h in range(nheads):
            mt_dt = None
            mt_bc = None
            for k in range(tph):
                tile = h * tph + k
                base = (h * 16 + k) * 160 if nheads > 1 else k * 160
                first_deps = [wready, ps_dt_war, ps_bc_war] + utoks if k == 0 else ()
                mt_dt = mm(E, ps_dt, xpb[:, base:base + 128],
                           ubf[:, tile * TC:(tile + 1) * TC],
                           start=(k == 0), stop=(k == tph - 1), deps=first_deps,
                           inc=(k == tph - 1))
                mt_bc = mm(E, ps_bc, xpb[:, base + 128:base + 160],
                           ubf[:, tile * TC:(tile + 1) * TC],
                           start=(k == 0), stop=(k == tph - 1), inc=(k == tph - 1))
            ev_dt = act(E, dtinb[:], ps_dt, AF.Copy, deps=[mt_dt, dtinb_war])
            ps_dt_war = ev_dt
            ev_bc = vcopy(E, bct[0:32, :], ps_bc, deps=[mt_bc, bct_war])
            ps_bc_war = ev_bc
            bdst = dram[bc_s][h] if nheads > 1 else dram[bc_s][0]
            bct_war = E.dma("gpsimd", bdst[:, c * TC:(c + 1) * TC], bct[0:32, :],
                            deps=[ev_bc])
            for mi in range(tph):
                tile = h * tph + mi
                slot = mi % 2
                mt = mm(E, ps_dp[slot],
                        dttb[:, h * HD + mi * 128: h * HD + (mi + 1) * 128],
                        dtinb[:], start=True, stop=True,
                        deps=[ev_dt, ps_dp_war[slot]])
                prev_head_mt = mt
                rr = act(E, t_r[slot], ps_dp[slot], AF.Relu,
                         bias=dtb[:, tile:tile + 1], deps=[mt, chain_war[slot]])
                aa = act(E, t_a[slot], ps_dp[slot], AF.Abs,
                         bias=dtb[:, tile:tile + 1], deps=[mt])
                ps_dp_war[slot] = aa
                ee = act(E, t_a[slot], t_a[slot][:], AF.Exp, scale=-1.0, deps=[aa])
                ll = act(E, t_l[slot], t_a[slot][:], AF.Ln, bias=1.0, deps=[ee])
                dd = vtt(E, de_t[slot], t_r[slot][:], t_l[slot][:], ALU.add,
                         deps=[ll, rr, dd_war[slot]])
                uu = vtt(E, du_t[slot], de_t[slot][:],
                         ubf[:, tile * TC:(tile + 1) * TC], ALU.mult,
                         deps=[dd, ustoks[tile]])
                chain_war[slot] = dd
                d1 = E.dma("gpsimd", dram[de_s][tile][:, c * TC:(c + 1) * TC],
                           de_t[slot], deps=[dd])
                d2 = E.dma("gpsimd", dram[du_s][tile][:, c * TC:(c + 1) * TC],
                           du_t[slot], deps=[uu])
                dd_war[slot] = [d1, d2]
                u_war[tile] = [uu, d1, d2, mt_dt, mt_bc]
            dtinb_war = prev_head_mt


@phase("A2",
       needs={"xb_s": "s", "cw0": "w", "cb0": "w", "xp0T": "w", "dt0T": "w", "dtb0": "w"},
       produces=["u0_s", "du0_s", "de0_s", "bc0_s"])
def p_a2(E, dram, ar, ps):
    proj_phase(E, dram, ar, ps, ntile=NT_I, nheads=NH, src="xb_s", cwn="cw0",
               cbn="cb0", xpn="xp0T", dtn="dt0T", dtbn="dtb0", u_s="u0_s",
               du_s="du0_s", de_s="de0_s", bc_s="bc0_s", silu=True)


@phase("A2L1",
       needs={"x2b_s": "s", "cw1": "w", "cb1": "w", "xp1T": "w", "dt1T": "w", "dtb1": "w"},
       produces=["u1_s", "du1_s", "de1_s", "bc1_s"])
def p_a2l1(E, dram, ar, ps):
    proj_phase(E, dram, ar, ps, ntile=NT_D, nheads=1, src="x2b_s", cwn="cw1",
               cbn="cb1", xpn="xp1T", dtn="dt1T", dtbn="dtb1", u_s="u1_s",
               du_s="du1_s", de_s="de1_s", bc_s="bc1_s", silu=False)


def scan_core(E, dram, ar, ps, *, ntile, nheads, u_s, du_s, de_s, bc_s, al, dvec,
              gate_s=None, ybf_s=None, post=None, TC=512, chunk_cb=None):
    tph = ntile // nheads
    NCH = S // TC

    asc = ar.alloc(ntile * NST, F32)
    dsb = ar.alloc(ntile, F32)
    carry = ar.alloc(ntile * NST, F32)
    Bt = ar.alloc(NST * TC, BF16)
    Ct = ar.alloc(NST * TC, BF16)
    dA = ar.alloc(NST * TC, BF16)
    dBu = ar.alloc(NST * TC, BF16)
    hh = ar.alloc(NST * TC, BF16)
    trA = ar.alloc(8 * TC, BF16)
    trB = ar.alloc(4 * TC, BF16)
    yv = ar.alloc(TC, F32)
    de_t = [ar.alloc(TC, BF16) for _ in range(2)]
    du_t = [ar.alloc(TC, BF16) for _ in range(2)]
    u_t = [ar.alloc(TC, BF16) for _ in range(2)]
    gt = [ar.alloc(TC, BF16) for _ in range(2)] if gate_s else None
    ybt = ar.alloc(TC, BF16)

    atoks = []
    for k in range(ntile):
        atoks.append(E.dma("sync", asc[:, k * NST:(k + 1) * NST],
                           dram[al][k * 128:(k + 1) * 128, :]))
    t = act(E, asc, asc[:], AF.Exp, deps=atoks)

    def _neg(e, out=asc, in_=asc):
        return e.mul(out=out, in_=in_[:], mul=-1.0)

    atok = E.op("scalar", _neg, deps=[t])
    dtok = load_pp_vec(E, dram, dvec, dsb, ntile)
    ptr_barrier(E, [atok] + dtok)
    carry_tok = [None] * ntile

    war = {}
    scan_last = atok
    tree_last = None
    y_last = None
    for c in range(NCH):
        chunk_stores = []
        for h in range(nheads):
            bsrc = dram[bc_s][h]
            bt = E.dma("sync", Bt[:].rearrange("p (n t) -> p n t", n=NST),
                       bsrc[0:NST, c * TC:(c + 1) * TC].unsqueeze(0)
                       .broadcast_to((128, NST, TC)),
                       deps=[war.get("Bt")])
            ct2 = E.dma("sync", Ct[:].rearrange("p (n t) -> p n t", n=NST),
                        bsrc[NST:2 * NST, c * TC:(c + 1) * TC].unsqueeze(0)
                        .broadcast_to((128, NST, TC)),
                        deps=[war.get("Ct")])
            for mi in range(tph):
                tile = h * tph + mi
                slot = mi % 2
                ld = E.dma("sync", de_t[slot], dram[de_s][tile][:, c * TC:(c + 1) * TC],
                           deps=[war.get(("de", slot))])
                lu = E.dma("sync", du_t[slot], dram[du_s][tile][:, c * TC:(c + 1) * TC],
                           deps=[war.get(("du", slot))])
                lt = E.dma("sync", u_t[slot], dram[u_s][tile][:, c * TC:(c + 1) * TC],
                           deps=[war.get(("ut", slot))])
                if gate_s:
                    lg = E.dma("sync", gt[slot],
                               dram[gate_s][tile][:, c * TC:(c + 1) * TC],
                               deps=[war.get(("gt", slot))])
                extoks = []
                for n in range(NST):
                    extoks.append(act(E, dA[:, n * TC:(n + 1) * TC], de_t[slot][:],
                                      AF.Exp,
                                      scale=asc[:, tile * NST + n: tile * NST + n + 1],
                                      deps=[ld, atok, scan_last]))
                war[("de", slot)] = extoks[-1]
                du3 = du_t[slot][:].unsqueeze(1).broadcast_to((128, NST, TC))
                dbu_tok = vtt(E, dBu[:].rearrange("p (n t) -> p n t", n=NST), du3,
                              Bt[:].rearrange("p (n t) -> p n t", n=NST), ALU.mult,
                              deps=[lu, bt, tree_last])
                war[("du", slot)] = dbu_tok
                war["Bt"] = dbu_tok
                stoks = []
                for n in range(NST):
                    init = 0.0 if c == 0 else \
                        carry[:, tile * NST + n: tile * NST + n + 1]
                    stoks.append(vscan(E, hh[:, n * TC:(n + 1) * TC],
                                       dA[:, n * TC:(n + 1) * TC],
                                       dBu[:, n * TC:(n + 1) * TC],
                                       init,
                                       deps=[extoks[n], dbu_tok, carry_tok[tile]]))
                scan_last = stoks[-1]
                hl = hh[:].rearrange("p (n t) -> p n t", n=NST)[:, :, TC - 1:TC]
                cext = vcopy(E, carry[:, tile * NST:(tile + 1) * NST].unsqueeze(2),
                             hl, deps=stoks)
                carry_tok[tile] = cext
                gtok = vtt(E, dBu, hh[:], Ct[:], ALU.mult, deps=stoks + [ct2, cext])
                war["Ct"] = gtok
                r1 = vtt(E, trA[:, 0:8 * TC], dBu[:, 0:8 * TC], dBu[:, 8 * TC:16 * TC],
                         ALU.add, deps=[gtok, war.get("trA")])
                r2 = vtt(E, trB[:, 0:4 * TC], trA[:, 0:4 * TC], trA[:, 4 * TC:8 * TC],
                         ALU.add, deps=[r1, war.get("trB")])
                r3 = vtt(E, trA[:, 0:2 * TC], trB[:, 0:2 * TC], trB[:, 2 * TC:4 * TC],
                         ALU.add, deps=[r2])
                war["trB"] = r3
                r4 = vtt(E, yv, trA[:, 0:TC], trA[:, TC:2 * TC], ALU.add, deps=[r3])
                war["trA"] = r4
                tree_last = r1
                y2 = vstt(E, yv, u_t[slot][:], dsb[:, tile:tile + 1], yv[:],
                          ALU.mult, ALU.add, deps=[r4, lt, dtok])
                war[("ut", slot)] = y2
                if gate_s:
                    y3 = vtt(E, ybt, yv[:], gt[slot][:], ALU.mult, deps=[y2, lg, y_last])
                    war[("gt", slot)] = y3
                    y_last = E.dma("gpsimd", dram[ybf_s][tile][:, c * TC:(c + 1) * TC],
                                   ybt, deps=[y3])
                    chunk_stores.append(y_last)
                else:
                    y_last = post(c, tile, yv, y2)
                E.pump()
                E.pump()
        if chunk_cb is not None:
            chunk_cb(c, chunk_stores)


@phase("B0C",
       needs={"u0_s": "s", "du0_s": "s", "de0_s": "s", "bc0_s": "s", "gate_s": "s",
              "wob_s": "s", "al0": "w", "d0": "w", "x0": "w"},
       produces=["ybf_s", "x1_s", "x1b_s"])
def p_b0c(E, dram, ar, ps):
    E.new_phase()
    TC = 512
    ybc = ar.alloc(32 * 512, BF16)
    wbuf = [ar.alloc(4 * 1024, BF16) for _ in range(2)]
    xr = [ar.alloc(512, F32) for _ in range(NT_D)]
    xtmp = [ar.alloc(512, F32) for _ in range(2)]
    x1b = ar.alloc(512, BF16)
    msb = ar.alloc(4 * 512, F32)
    sq = ar.alloc(512, F32)
    banks = [ps["A"][:, i * 512:(i + 1) * 512] for i in range(4)] + \
            [ps["B"][:, i * 512:(i + 1) * 512] for i in range(4)]
    st = {"war": {}, "xr_war": [[] for _ in range(NT_D)], "bank_war": [None] * 8,
          "wbuf_war": [None, None], "ybc_war": None, "xt_war": [None, None]}

    def c_chunk(c, ytoks):
        steps = []
        steps.append(lambda c=c, ytoks=ytoks: c_ybc(c, ytoks))
        for mh in range(2):
            for kg in range(8):
                steps.append(lambda c=c, mh=mh, kg=kg: c_kg(c, mh, kg))
            steps.append(lambda c=c, mh=mh: c_evac(c, mh))
        steps.append(lambda c=c: c_post(c))
        if E.pump_steps:
            E.pump_steps.extend(steps)
        else:
            E.pump_steps = steps

    def c_ybc(c, ytoks):
        ylds = []
        for k in range(32):
            ylds.append(E.dma("scalar", ybc[:, k * 512:(k + 1) * 512],
                              dram["ybf_s"][k][:, c * TC:(c + 1) * TC],
                              deps=[ytoks[k], st["ybc_war"]]))
        st["ylds"] = ylds
        st["etoks"] = []

    def c_kg(c, mh, kg):
        slot = kg % 2
        ylds = st["ylds"]
        wlds = []
        for j in range(4):
            k = kg * 4 + j
            wlds.append(E.dma("scalar",
                              wbuf[slot][:, j * 1024:(j + 1) * 1024],
                              dram["wob_s"][k][:, mh * 1024:(mh + 1) * 1024],
                              deps=[st["wbuf_war"][slot]] if j == 0 else ()))
        last_mm = None
        for j in range(4):
            k = kg * 4 + j
            for mloc in range(8):
                deps = ()
                if mloc == 0:
                    deps = [wlds[j], ylds[k]]
                if k == 0:
                    deps = list(deps) + [st["bank_war"][mloc]]
                last_mm = mm(E, banks[mloc],
                             wbuf[slot][:, j * 1024 + mloc * 128:
                                        j * 1024 + mloc * 128 + 128],
                             ybc[:, k * 512:(k + 1) * 512],
                             start=(k == 0), stop=(k == 31),
                             deps=deps, inc=(j == 3 and mloc == 7))
        st["wbuf_war"][slot] = last_mm
        st["last_mm"] = last_mm

    def c_evac(c, mh):
        for mloc in range(8):
            m = mh * 8 + mloc
            ev = act(E, xr[m], banks[mloc], AF.Copy,
                     deps=[st["last_mm"]] + st["xr_war"][m])
            st["bank_war"][mloc] = ev
            st["etoks"].append(ev)
        if mh == 1:
            st["ybc_war"] = st["last_mm"]

    def c_post(c):
        etoks = st["etoks"]
        war = st["war"]
        ttoks = []
        for m in range(NT_D):
            lx0 = E.dma("scalar", xtmp[m % 2], dram["x0"][m * 128:(m + 1) * 128,
                                                          c * TC:(c + 1) * TC],
                        deps=[st["xt_war"][m % 2]])
            at = vtt(E, xr[m], xr[m][:], xtmp[m % 2][:], ALU.add,
                     deps=[lx0, etoks[m]])
            st["xt_war"][m % 2] = at
            ttoks.append(at)
        ws = war.get("stat")
        war["stat"] = [etoks[8]] + (ws if isinstance(ws, list) else [ws])
        wb2 = war.get("bc")
        war["bc"] = [etoks[9]] + (wb2 if isinstance(wb2, list) else [wb2])
        lnt = layernorm_tiles(E, banks[0], banks[1],
                              [xr[i][:] for i in range(NT_D)], ttoks, TC, msb, sq,
                              war)
        st["bank_war"][0] = [st["bank_war"][0], lnt[-1]]
        st["bank_war"][1] = [st["bank_war"][1], lnt[-1]]
        for i in range(NT_D):
            bt = act(E, x1b, xr[i][:], AF.Copy, deps=[lnt[i], war.get("x1b")])
            da = E.dma("gpsimd", dram["x1_s"][i][:, c * TC:(c + 1) * TC], xr[i],
                       deps=[lnt[i]])
            db = E.dma("gpsimd", dram["x1b_s"][i][:, c * TC:(c + 1) * TC], x1b,
                       deps=[bt])
            war["x1b"] = db
            st["xr_war"][i] = [bt, da, db]

    scan_core(E, dram, ar, ps, ntile=NT_I, nheads=NH, u_s="u0_s", du_s="du0_s",
              de_s="de0_s", bc_s="bc0_s", al="al0", dvec="d0",
              gate_s="gate_s", ybf_s="ybf_s", chunk_cb=c_chunk)
    while E.pump_steps:
        E.pump()



@phase("F0", needs={"x1_s": "s", "x1b_s": "s", "fw0": "w", "fs0": "w"},
       produces=["x2_s", "x2b_s"])
def p_f0(E, dram, ar, ps):
    E.new_phase()
    TC = 512
    NCH = S // TC
    wtmp = [ar.alloc(2048, F32)]
    wtmp = [wtmp[0], wtmp[0]]
    wb = ar.alloc(16 * 2048, BF16)
    xbt = [ar.alloc(16 * 512, BF16) for _ in range(2)]
    ssb = ar.alloc(16, F32)
    oc = [[ar.alloc(512, F32) for _ in range(NT_D)] for _ in range(2)]
    xtmp = [ar.alloc(512, F32) for _ in range(2)]
    x2b = ar.alloc(512, BF16)
    msb = ar.alloc(4 * 512, F32)
    sq = ar.alloc(512, F32)
    rowtmp = ar.alloc(7 * 512, F32)

    stok = load_pp_vec(E, dram, "fs0", ssb, NT_D)
    ptr_barrier(E, stok)
    wtmp_war = [None, None]
    wtoks = None
    for k in range(16):
        t = E.dma("sync", wtmp[k % 2], dram["fw0"][k], deps=[wtmp_war[k % 2]])
        wtoks = act(E, wb[:, k * 2048:(k + 1) * 2048], wtmp[k % 2][:], AF.Copy,
                    deps=[t])
        wtmp_war[k % 2] = wtoks
    xbt_war = [None, None]

    banks = [ps["A"][:, i * 512:(i + 1) * 512] for i in range(4)] + \
            [ps["B"][:, 0:512], ps["B"][:, 512:1024]]
    stat_bank = ps["B"][:, 1024:1536]
    bc_bank = ps["B"][:, 1536:2048]
    bank_war = [None] * 6
    war = {}
    oc_war = [[[] for _ in range(NT_D)] for _ in range(2)]
    ones_col = E.consts["ones_col"]
    ones_row = E.consts["ones_row"]
    pend = {}

    def f_mm(c):
        par = c % 2
        xlds = []
        for k in range(16):
            xlds.append(E.dma("scalar", xbt[par][:, k * 512:(k + 1) * 512],
                              dram["x1b_s"][k][:, c * TC:(c + 1) * TC],
                              deps=[xbt_war[par]] if k == 0 else ()))
        etoks = []
        mtok = None
        for m in range(NT_D):
            bk = m % 6
            for k in range(16):
                deps = [xlds[k]]
                if k == 0:
                    deps = deps + [wtoks, bank_war[bk]] + stok
                mtok = mm(E, banks[bk],
                          wb[:, k * 2048 + m * 128: k * 2048 + m * 128 + 128],
                          xbt[par][:, k * 512:(k + 1) * 512],
                          start=(k == 0), stop=(k == 15),
                          deps=deps, inc=(k == 15))
            ev = act(E, oc[par][m], banks[bk], AF.Copy, scale=ssb[:, m:m + 1],
                     deps=[mtok] + oc_war[par][m])
            bank_war[bk] = ev
            etoks.append(ev)
        xbt_war[par] = mtok
        pend[c] = etoks

    def f_post(c):
        par = c % 2
        etoks = pend.pop(c)
        occ = oc[par]
        mt2 = None
        for m in range(NT_D):
            sqt = act(E, sq, occ[m][:, :], AF.Square, deps=[etoks[m], war.get("sq")])
            mt2 = mm(E, stat_bank[0:1, 0:TC], ones_col[:, 0:1], sq[:, 0:TC],
                     start=(m == 0), stop=(m == NT_D - 1),
                     deps=[sqt, war.get("stat"), E.consts_tok] if m == 0 else [sqt],
                     inc=True)
            war["sq"] = mt2
        vrow = rowtmp[0:1, 0:512]
        frow = rowtmp[0:1, 512:1024]
        trow = rowtmp[0:1, 1024:1536]
        grow = rowtmp[0:1, 1536:2048]
        hrow = rowtmp[0:1, 2048:2560]
        xrow = rowtmp[0:1, 2560:3072]
        srow = rowtmp[0:1, 3072:3584]
        v0 = vts(E, srow, stat_bank[0:1, 0:TC], 1.0, None, ALU.mult,
                 deps=[mt2, war.get("rows2")])
        v1 = act(E, xrow, srow, AF.Sqrt, deps=[v0])
        v2 = vts(E, vrow, xrow, FFN_EPS, None, ALU.max, deps=[v1])
        war["stat"] = v0
        th = act(E, trow, vrow, AF.Tanh, deps=[v2])
        dn = vts(E, xrow, vrow, FFN_EPS, None, ALU.add, deps=[v2, v1])
        rc = vrecip(E, grow, xrow, deps=[dn])
        f1 = vtt(E, frow, trow, grow, ALU.mult, deps=[th, rc])
        on = vtt(E, xrow, frow, vrow, ALU.mult, deps=[f1, rc])
        on2 = vts(E, trow, xrow, FFN_EPS, None, ALU.max, deps=[on, th])
        rc2 = vrecip(E, xrow, trow, deps=[on2])
        g1 = vts(E, grow, xrow, 0.99, 1.0, ALU.mult, ALU.min, deps=[rc2])
        hf = vtt(E, hrow, frow, grow, ALU.mult, deps=[g1, f1])
        bht = mm(E, bc_bank[:, 0:TC], ones_row[0:1, :], hrow, start=True, stop=True,
                 deps=[hf, war.get("bc"), E.consts_tok2])
        war["rows2"] = [bht, hf]
        ttoks = []
        for m in range(NT_D):
            ap1 = vtt(E, occ[m], occ[m][:], bc_bank[:, 0:TC], ALU.mult,
                      deps=[bht, etoks[m]])
            lx = E.dma("sync", xtmp[m % 2], dram["x1_s"][m][:, c * TC:(c + 1) * TC],
                       deps=[war.get(("xt", m % 2))])
            at = vtt(E, occ[m], occ[m][:], xtmp[m % 2][:], ALU.add, deps=[ap1, lx])
            war[("xt", m % 2)] = at
            ttoks.append(at)
        war["bc"] = ttoks[-1]
        lnt = layernorm_tiles(E, stat_bank, bc_bank,
                              [occ[i][:] for i in range(NT_D)], ttoks, TC, msb, sq,
                              war)
        for i in range(NT_D):
            bt = act(E, x2b, occ[i][:], AF.Copy, deps=[lnt[i], war.get("x2b")])
            da = E.dma("gpsimd", dram["x2_s"][i][:, c * TC:(c + 1) * TC], occ[i],
                       deps=[lnt[i]])
            db = E.dma("gpsimd", dram["x2b_s"][i][:, c * TC:(c + 1) * TC], x2b,
                       deps=[bt])
            war["x2b"] = db
            oc_war[par][i] = [bt, da, db]

    for c in range(NCH):
        f_mm(c)
        if c > 0:
            f_post(c - 1)
    f_post(NCH - 1)


@phase("W1", needs={"wqgT": "w", "wquT": "w", "wqdT": "w", "woT": "w"},
       produces=["wgb_s", "wub_s", "wdb_s", "wob_s"])
def p_w1(E, dram, ar, ps):
    HW = HID // 2
    it = [ar.alloc(HW, I32) for _ in range(2)]
    bt = [ar.alloc(HW, BF16) for _ in range(2)]
    ar.base = ar.off
    state = {"war": [None, None], "dtok": [None, None], "i": 0}
    jobs = []
    for srcn, dst, nstrip, width in [("woT", "wob_s", 32, D),
                                     ("wqgT", "wgb_s", 16, HID),
                                     ("wquT", "wub_s", 16, HID),
                                     ("wqdT", "wdb_s", NSLAB, D)]:
        nh = (width + HW - 1) // HW
        for k in range(nstrip):
            for hwi in range(nh):
                lo = hwi * HW
                hi = min(width, lo + HW)
                jobs.append((srcn, dst, k, lo, hi))

    def make_step(j):
        srcn, dst, k, lo, hi = j

        def step():
            slot = state["i"] % 2
            state["i"] += 1
            n = hi - lo
            if srcn == "woT":
                src_i = it[slot][:, 0:n].bitcast(F32)
                bias = 0.0
            else:
                src_i = it[slot][:, 0:n]
                bias = -128.0
            dst_b = bt[slot][:, 0:n]
            t = E.dma("sync", src_i, dram[srcn][k][:, lo:hi],
                      deps=[state["war"][slot]])
            cv = act(E, dst_b, src_i, AF.Copy, bias=bias,
                     deps=[t, state["dtok"][slot]])
            state["war"][slot] = cv
            state["dtok"][slot] = E.dma("gpsimd", dram[dst][k][:, lo:hi],
                                        dst_b, deps=[cv])

        return step

    E.pump_steps = [make_step(j) for j in jobs]


@phase("B1F1",
       needs={"u1_s": "s", "du1_s": "s", "de1_s": "s", "bc1_s": "s", "x2_s": "s",
              "wgb_s": "s", "wub_s": "s", "wdb_s": "s",
              "al1": "w", "d1": "w", "sg": "w", "su": "w", "sd": "w"},
       produces=["x3_s", "x3b_s", "h_s", "out_s"])
def p_b1f1(E, dram, ar, ps):
    E.new_phase()
    TC = 256
    TB = 1024
    NCH = S // TC
    yc = [ar.alloc(TC, F32) for _ in range(NT_D)]
    xtmp = [ar.alloc(512, F32) for _ in range(2)]
    x3b = ar.alloc(TC, BF16)
    msb = ar.alloc(4 * 512, F32)
    sq = ar.alloc(512, F32)
    sgb = ar.alloc(NSLAB, F32)
    sub = ar.alloc(NSLAB, F32)
    sdb = ar.alloc(NT_D, F32)
    x3bb = ar.alloc(16 * TB, BF16)
    wgt = [ar.alloc(16 * 128, BF16) for _ in range(2)]
    wut = [ar.alloc(16 * 128, BF16) for _ in range(2)]
    wdt = [ar.alloc(512, BF16) for _ in range(4)]
    hdt = [ar.alloc(512, BF16) for _ in range(4)]
    gs = ar.alloc(TB, BF16)
    us = ar.alloc(TB, BF16)
    htmp = [ar.alloc(TB, BF16) for _ in range(2)]
    oc = [ar.alloc(512, F32) for _ in range(NT_D)]
    fxtmp = [ar.alloc(512, F32) for _ in range(2)]

    svec = load_pp_vec(E, dram, "sg", sgb, NSLAB)
    svec += load_pp_vec(E, dram, "su", sub, NSLAB)
    svec += load_pp_vec(E, dram, "sd", sdb, NT_D)
    ptr_barrier(E, svec)

    banksA = [ps["A"][:, i * 512:(i + 1) * 512] for i in range(4)]
    banksB = [ps["B"][:, i * 512:(i + 1) * 512] for i in range(4)]

    st = {"war": {}, "ttoks": [None] * NT_D, "yc_war": [None] * NT_D,
          "x3db": [[] for _ in range(NCH)],
          "hb_war": [None] * NSLAB, "grp_war": [None, None], "gs_war": None,
          "wg_war": [None, None], "wd_war": [None] * 4,
          "hd_war": [None] * 4,
          "x3bb_war": [], "xt_war": [None, None],
          "abank": [None] * 4, "oc_war": [[] for _ in range(NT_D)],
          "fwar": {}, "out_last": None}
    war = st["war"]

    def post(c, tile, yv, y2tok):
        lx = E.dma("scalar", xtmp[tile % 2][:, 0:TC],
                   dram["x2_s"][tile][:, c * TC:(c + 1) * TC],
                   deps=[st["xt_war"][tile % 2]])
        at = vtt(E, yc[tile], yv[:], xtmp[tile % 2][:, 0:TC], ALU.add,
                 deps=[y2tok, lx, st["yc_war"][tile]])
        st["xt_war"][tile % 2] = at
        st["ttoks"][tile] = at
        if tile == NT_D - 1:
            lnt = layernorm_tiles(E, banksB[2], banksB[3],
                                  [yc[i][:] for i in range(NT_D)], st["ttoks"],
                                  TC, msb, sq, war)
            last = None
            for i in range(NT_D):
                bt = act(E, x3b, yc[i][:], AF.Copy, deps=[lnt[i], war.get("x3b")])
                da = E.dma("gpsimd", dram["x3_s"][i][:, c * TC:(c + 1) * TC],
                           yc[i], deps=[lnt[i]])
                db = E.dma("gpsimd", dram["x3b_s"][i][:, c * TC:(c + 1) * TC],
                           x3b, deps=[bt])
                war["x3b"] = db
                st["yc_war"][i] = [bt, da, db]
                st["x3db"][c].append(db)
                last = db
            return last
        return at

    def f_x3load(blk):
        toks = []
        nb = NCH // 2
        dbs = [t for c in range(blk * nb, blk * nb + nb) for t in st["x3db"][c]]
        for k in range(NT_D):
            toks.append(E.dma("sync", x3bb[:, k * TB:(k + 1) * TB],
                              dram["x3b_s"][k][:, blk * TB:(blk + 1) * TB],
                              deps=(dbs if k == 0 else []) + st["x3bb_war"]))
        st["xtk"] = toks

    def f_wload(blk, slab):
        slot = slab % 2
        wg_l, wu_l = [], []
        for k in range(NT_D):
            wg_l.append(E.dma("sync", wgt[slot][:, k * 128:(k + 1) * 128],
                              dram["wgb_s"][k][:, slab * 128:(slab + 1) * 128],
                              deps=[st["wg_war"][slot]] if k == 0 else ()))
            wu_l.append(E.dma("sync", wut[slot][:, k * 128:(k + 1) * 128],
                              dram["wub_s"][k][:, slab * 128:(slab + 1) * 128]))
        st["wg_l"], st["wu_l"] = wg_l, wu_l

    def f_slab(blk, slab):
        slot = slab % 2
        wg_l, wu_l = st["wg_l"], st["wu_l"]
        mg = None
        mu = None
        for k in range(NT_D):
            for th in range(2):
                deps = [wg_l[k], st["xtk"][k]] if th == 0 else ()
                if k == 0:
                    deps = list(deps) + [st["abank"][th]]
                mg = mm(E, banksA[th],
                        wgt[slot][:, k * 128:(k + 1) * 128],
                        x3bb[:, k * TB + th * 512: k * TB + th * 512 + 512],
                        start=(k == 0), stop=(k == NT_D - 1),
                        deps=deps, inc=(k == NT_D - 1 and th == 1))
            for th in range(2):
                deps = [wu_l[k]] if th == 0 else ()
                if k == 0:
                    deps = list(deps) + [st["abank"][2 + th]]
                mu = mm(E, banksA[2 + th],
                        wut[slot][:, k * 128:(k + 1) * 128],
                        x3bb[:, k * TB + th * 512: k * TB + th * 512 + 512],
                        start=(k == 0), stop=(k == NT_D - 1),
                        deps=deps, inc=(k == NT_D - 1 and th == 1))
        pp = ps["A"]
        ge = act(E, gs, pp[:, 0:1024], AF.Silu, scale=sgb[:, slab:slab + 1],
                 deps=[mg, st["gs_war"]])
        ue = act(E, us, pp[:, 1024:2048], AF.Copy, scale=sub[:, slab:slab + 1],
                 deps=[mu])
        ht = vtt(E, htmp[slot], gs[:], us[:], ALU.mult,
                 deps=[ge, ue, st["hb_war"][slab]])
        hd = E.dma("gpsimd", dram["h_s"][slab][:, blk * TB:(blk + 1) * TB],
                   htmp[slot], deps=[ht])
        st["gs_war"] = ht
        st["hb_war"][slab] = hd
        st["abank"][0] = ge
        st["abank"][1] = ge
        st["abank"][2] = ue
        st["abank"][3] = ue
        st["wg_war"][slot] = ht

    def f_down(blk, th, q, kg):
        mt = None
        lds = {}
        mtk = {}

        def issue_load(ki):
            k = kg * 11 + ki
            slot = k % 4
            dep = mtk.get(k - 4, st["wd_war"][slot])
            wl = E.dma("sync", wdt[slot],
                       dram["wdb_s"][k][:, q * 512:(q + 1) * 512], deps=[dep])
            hl = E.dma("sync", hdt[slot],
                       dram["h_s"][k][:, blk * TB + th * 512: blk * TB + th * 512 + 512],
                       deps=[st["hb_war"][k], dep])
            lds[k] = (wl, hl)

        for ki in range(4):
            issue_load(ki)
        for ki in range(11):
            k = kg * 11 + ki
            slot = k % 4
            wl, hl = lds[k]
            for mloc in range(4):
                deps = [wl, hl] if mloc == 0 else ()
                if k == 0:
                    deps = list(deps) + [st["abank"][mloc]] + svec
                mt = mm(E, banksA[mloc],
                        wdt[slot][:, mloc * 128:(mloc + 1) * 128],
                        hdt[slot][:],
                        start=(k == 0), stop=(k == NSLAB - 1),
                        deps=deps, inc=(mloc == 3))
            mtk[k] = mt
            st["wd_war"][slot] = mt
            st["hd_war"][slot] = mt
            if ki + 4 < 11:
                issue_load(ki + 4)
        if kg == 3:
            for mloc in range(4):
                m = q * 4 + mloc
                ev = act(E, oc[m], banksA[mloc], AF.Copy, scale=sdb[:, m:m + 1],
                         deps=[mt] + st["oc_war"][m])
                st["abank"][mloc] = ev
                st["fwar"][("ev", m)] = ev

    def f_ln(blk, th):
        fwar = st["fwar"]
        ttoks = []
        for m in range(NT_D):
            lx = E.dma("sync", fxtmp[m % 2],
                       dram["x3_s"][m][:, blk * TB + th * 512: blk * TB + th * 512 + 512],
                       deps=[fwar.get(("xt", m % 2))])
            at = vtt(E, oc[m], oc[m][:], fxtmp[m % 2][:], ALU.add,
                     deps=[fwar[("ev", m)], lx])
            fwar[("xt", m % 2)] = at
            ttoks.append(at)
        lnt = layernorm_tiles(E, banksB[2], banksB[3],
                              [oc[i][:] for i in range(NT_D)], ttoks, 512, msb, sq,
                              st["war"])
        for i in range(NT_D):
            da = E.dma("gpsimd",
                       dram["out_s"][i][:, blk * TB + th * 512: blk * TB + th * 512 + 512],
                       oc[i], deps=[lnt[i]])
            st["oc_war"][i] = [da]
            st["out_last"] = da
        if th == 1 and blk == 0:
            st["x3bb_war"] = [st["hb_war"][NSLAB - 1]]

    def register_block(blk):
        steps = [lambda blk=blk: f_x3load(blk)]
        for slab in range(NSLAB):
            steps.append(lambda blk=blk, slab=slab: f_wload(blk, slab))
            steps.append(lambda blk=blk, slab=slab: f_slab(blk, slab))
        for th in range(2):
            for q in range(4):
                for kg in range(4):
                    steps.append(lambda blk=blk, th=th, q=q, kg=kg:
                                 f_down(blk, th, q, kg))
            steps.append(lambda blk=blk, th=th: f_ln(blk, th))
        if E.pump_steps:
            E.pump_steps.extend(steps)
        else:
            E.pump_steps = steps

    def b1_cb(c, stores):
        if c == NCH // 2 - 1:
            register_block(0)
        elif c == NCH - 1:
            register_block(1)

    scan_core(E, dram, ar, ps, ntile=NT_D, nheads=1, u_s="u1_s", du_s="du1_s",
              de_s="de1_s", bc_s="bc1_s", al="al1", dvec="d1", post=post, TC=TC,
              chunk_cb=b1_cb)
    while E.pump_steps:
        E.pump()


def make_consts(E, ar):
    nc = E.nc
    ones_col = nc.alloc_sbuf_tensor("ones_col", [128, 1], F32)
    ones_row = nc.alloc_sbuf_tensor("ones_row", [1, 128], F32)
    scr = nc.alloc_sbuf_tensor("scr", [128, 8], F32)
    cvals = nc.alloc_sbuf_tensor("cvals", [128, 4], F32)
    E.consts = {"ones_col": ones_col, "ones_row": ones_row, "scr": scr}
    E.new_phase()
    E.consts_tok = vmemset(E, ones_col[:], 1.0)
    E.consts_tok2 = vmemset(E, ones_row[:], 1.0)
    vmemset(E, scr[:], 0.0)
    for i, v in enumerate([LN_EPS, FFN_EPS, 0.99]):
        E.consts_tok2 = vmemset(E, cvals[:, i:i + 1], v)
        nc.const_aps.register(F32, v, cvals[:, i:i + 1])


_BUILD_CACHE = {}


def build_launch(group, final_outs):
    key = (tuple(group), tuple(sorted(final_outs)))
    if key in _BUILD_CACHE:
        return _BUILD_CACHE[key]
    nc = bass.Bass("TRN2", target_bir_lowering=False)
    stack = ExitStack()
    produced = set()
    ext_in = set()
    for g in group:
        _, needs, produces = PHASES[g]
        for n in needs:
            if n not in produced:
                ext_in.add(n)
        produced.update(produces)
    ext_out = set(final_outs) & produced
    internal = produced - ext_out
    dram = {}
    for n in sorted(ext_in):
        shape, dt = SPEC[n]
        dram[n] = nc.dram_tensor(n, list(shape), dt, kind="ExternalInput")
    for n in sorted(ext_out):
        shape, dt = SPEC[n]
        dram[n] = nc.dram_tensor(n, list(shape), dt, kind="ExternalOutput")
    for n in sorted(internal):
        shape, dt = SPEC[n]
        dram[n] = nc.dram_tensor(n, list(shape), dt, kind="Internal")

    E = Emit(nc, stack)
    ar = Arena(nc, 206 * 1024)
    psA = nc.alloc_psum_tensor("psA", [128, 2048], F32)
    psB = nc.alloc_psum_tensor("psB", [128, 2048], F32)
    ps = {"A": psA, "B": psB}
    make_consts(E, ar)
    for g in group:
        fn, _, _ = PHASES[g]
        ar.reset()
        fn(E, dram, ar, ps)
        if g == "A2":
            while E.pump_steps:
                E.pump()
            ar.base = 0
    final_deps = [(s, E.semcnt[s]) for s in E.semcnt if E.semcnt[s] > 0]
    E.wait_only("gpsimd", final_deps)
    E.finalize()
    res = (nc, sorted(ext_in), sorted(ext_out))
    _BUILD_CACHE[key] = res
    return res


def prep_weights(inputs):
    def C(a):
        return np.ascontiguousarray(np.asarray(a))

    w = {}
    ip = np.asarray(inputs["L0_attn_in_proj_w"])
    w["wxT"] = C(ip[0::2].T).reshape(16, 128, I4)
    w["wgT"] = C(ip[1::2].T).reshape(16, 128, I4)
    w["cw0"] = C(inputs["L0_attn_conv_w"])
    w["cb0"] = C(inputs["L0_attn_conv_b"])
    xp = np.asarray(inputs["L0_attn_x_proj_weight"])
    w["xp0T"] = C(np.transpose(xp, (0, 2, 1))).reshape(NH, 16, 128, 160)
    dt = np.asarray(inputs["L0_attn_dt_proj_weight"])
    w["dt0T"] = C(np.transpose(dt, (0, 2, 1)))
    w["dtb0"] = C(np.asarray(inputs["L0_attn_dt_proj_bias"]).reshape(-1))
    w["al0"] = C(np.asarray(inputs["L0_attn_A_log"]).reshape(I4, NST))
    w["d0"] = C(np.asarray(inputs["L0_attn_D"]).reshape(-1))
    w["woT"] = C(np.asarray(inputs["L0_attn_out_proj_w"]).T).reshape(32, 128, D)
    w["fw0"] = C(inputs["L0_ffn_weight"]).reshape(16, 128, D)
    w["fs0"] = C(inputs["L0_ffn_scale"])
    w["cw1"] = C(np.asarray(inputs["L1_attn_conv_w"]).reshape(D, KC))
    w["cb1"] = C(inputs["L1_attn_conv_b"])
    w["xp1T"] = C(np.asarray(inputs["L1_attn_xproj_w"]).T).reshape(16, 128, 160)
    w["dt1T"] = C(np.asarray(inputs["L1_attn_dtproj_w"]).T)
    w["dtb1"] = C(inputs["L1_attn_dtproj_b"])
    w["al1"] = C(inputs["L1_attn_A_log"])
    w["d1"] = C(inputs["L1_attn_D_skip"])
    w["wqgT"] = C(np.asarray(inputs["L1_ffn_wq_gate"]).T).reshape(16, 128, HID)
    w["wquT"] = C(np.asarray(inputs["L1_ffn_wq_up"]).T).reshape(16, 128, HID)
    w["wqdT"] = C(np.asarray(inputs["L1_ffn_wq_down"]).T).reshape(NSLAB, 128, D)
    w["sg"] = C(inputs["L1_ffn_s_gate"])
    w["su"] = C(inputs["L1_ffn_s_up"])
    w["sd"] = C(inputs["L1_ffn_s_down"])
    for k in w:
        assert w[k].shape == SPEC[k][0], (k, w[k].shape, SPEC[k][0])
        want = np.int32 if SPEC[k][1] == I32 else np.float32
        assert w[k].dtype == want, (k, w[k].dtype)
    return w


LAUNCH_GROUPS = [["W1", "A1", "A2", "B0C", "F0", "A2L1", "B1F1"]]
FINAL = "out_s"


def kernel(**inputs):
    w = prep_weights(inputs)
    x = np.asarray(inputs["L0_attn_hidden_states"])
    pool = [dict(w) for _ in range(B)]
    for i in range(B):
        pool[i]["x0"] = np.ascontiguousarray(x[i].T)

    groups = LAUNCH_GROUPS
    for gi, group in enumerate(groups):
        later_needs = set()
        for g2 in [g for grp in groups[gi + 1:] for g in grp]:
            later_needs.update(PHASES[g2][1].keys())
        produced = set()
        for g in group:
            produced.update(PHASES[g][2])
        outs_needed = (produced & later_needs) | ({FINAL} & produced)
        nc, ins, outs = build_launch(group, outs_needed)
        in_maps = [{n: pool[i][n] for n in ins} for i in range(B)]
        res = run_bass_kernel_spmd(nc, in_maps, core_ids=list(range(B)))
        for i in range(B):
            for n in outs:
                pool[i][n] = res.results[i][n]

    out = np.stack([pool[i][FINAL].reshape(D, S).T for i in range(B)])
    return np.ascontiguousarray(out.astype(np.float32))


if __name__ == "__main__":
    pass


# version 14
# speedup vs baseline: 1.1405x; 1.0165x over previous
import numpy as np
import ml_dtypes
from contextlib import ExitStack

import concourse.bass as bass
import concourse.mybir as mybir
from concourse.bass_utils import run_bass_kernel_spmd

F32 = mybir.dt.float32
BF16 = mybir.dt.bfloat16
I32 = mybir.dt.int32
AF = mybir.ActivationFunctionType
ALU = mybir.AluOpType
BF16NP = ml_dtypes.bfloat16

B = 8
S = 2048
D = 2048
I4 = 4096
NH = 2
HD = 2048
NST = 16
KC = 4
HID = 5632
NSLAB = HID // 128

NT_D = D // 128
NT_I = I4 // 128

LN_EPS = 1e-5
FFN_EPS = 1e-7

ENGINES = ("sync", "tensor", "vector", "scalar", "gpsimd")


def _flat(deps):
    out = []
    for d in deps:
        if d is None:
            continue
        if isinstance(d, list):
            for x in d:
                if x is not None:
                    out.append(x)
        else:
            out.append(d)
    return out


class Emit:
    def __init__(self, nc, stack):
        self.nc = nc
        self.stack = stack
        self.recs = {e: [] for e in ENGINES}
        self.last_wait = {e: {} for e in ENGINES}
        self.semcnt = {}
        self.cur = None
        self.dmas = None
        self._semi = 0
        self.consts = {}

    def _new_sem(self):
        self._semi += 1
        h = self.stack.enter_context(self.nc.semaphore(f"s{self._semi}"))
        self.semcnt[h] = 0
        return h

    NPOOL = 12

    def new_phase(self):
        barrier = [(s, c) for s, c in self.semcnt.items() if c > 0]
        self.cur = {e: self._new_sem() for e in ("tensor", "vector", "scalar", "gpsimd")}
        if not hasattr(self, "pools"):
            self.pools = {q: [[self._new_sem(), None] for _ in range(self.NPOOL)]
                          for q in ("sync", "gpsimd", "scalar")}
            self.pool_idx = {q: 0 for q in self.pools}
        if barrier:
            for e in ENGINES:
                self._record(e, None, barrier, None, 0)

    def _record(self, eng, fn, deps, sem, amt):
        waits = []
        for dsem, dval in _flat(deps):
            if dval <= self.last_wait[eng].get(dsem, -1):
                continue
            waits.append((dsem, dval))
            self.last_wait[eng][dsem] = dval
        self.recs[eng].append((waits, fn, sem, amt))

    def op(self, eng, fn, deps=(), inc=True):
        sem = self.cur[eng] if inc else None
        tok = None
        if inc:
            self.semcnt[sem] += 1
            tok = (sem, self.semcnt[sem])
        self._record(eng, fn, deps, sem, 1)
        return tok

    def dma(self, q, out, in_, deps=(), slow=False):
        slot = self.pools[q][self.pool_idx[q] % self.NPOOL]
        self.pool_idx[q] += 1
        sem, last = slot
        self.semcnt[sem] += 16
        kw = {"allow_slow_non_contiguous": True} if slow else {}

        def fn(e, out=out, in_=in_, kw=kw):
            return e.dma_start(out=out, in_=in_, **kw)

        self._record(q, fn, list(deps) + [last], sem, 16)
        tok = (sem, self.semcnt[sem])
        slot[1] = tok
        return tok

    def wait_only(self, eng, deps):
        self._record(eng, None, deps, None, 0)

    pump_steps = None

    def pump(self):
        if self.pump_steps:
            self.pump_steps.pop(0)()

    def finalize(self):
        with self.nc.Block() as block:
            for eng in ENGINES:
                recs = self.recs[eng]
                if not recs:
                    continue

                def runner(e, recs=recs):
                    for waits, fn, sem, amt in recs:
                        for (s, v) in waits:
                            e.wait_ge(s, v)
                        if fn is not None:
                            inst = fn(e)
                            if sem is not None and inst is not None:
                                inst.then_inc(sem, amt)

                getattr(block, eng)(runner)


class Arena:
    def __init__(self, nc, nbytes):
        self.t = nc.alloc_sbuf_tensor("arena", [128, nbytes], mybir.dt.uint8)
        self.nbytes = nbytes
        self.off = 0

    def alloc(self, cols, dtype):
        esz = {F32: 4, BF16: 2, I32: 4}[dtype]
        self.off = (self.off + 31) & ~31
        start = self.off
        self.off += cols * esz
        assert self.off <= self.nbytes, f"arena overflow: {self.off} > {self.nbytes}"
        return self.t[:, start : start + cols * esz].bitcast(dtype)

    def reset(self, off=None):
        self.off = self.base if off is None else off

    base = 0


PHASES = {}


def phase(name, needs, produces):
    def deco(fn):
        PHASES[name] = (fn, needs, produces)
        return fn

    return deco


SPEC = {
    "x0": ((D, S), F32),
    "wxT": ((16, 128, I4), F32),
    "wgT": ((16, 128, I4), F32),
    "cw0": ((I4, KC), F32),
    "cb0": ((I4,), F32),
    "xp0T": ((NH, 16, 128, 160), F32),
    "dt0T": ((NH, 128, HD), F32),
    "dtb0": ((I4,), F32),
    "al0": ((I4, NST), F32),
    "d0": ((I4,), F32),
    "woT": ((32, 128, D), F32),
    "fw0": ((16, 128, D), F32),
    "fs0": ((D,), F32),
    "cw1": ((D, KC), F32),
    "cb1": ((D,), F32),
    "xp1T": ((16, 128, 160), F32),
    "dt1T": ((128, D), F32),
    "dtb1": ((D,), F32),
    "al1": ((D, NST), F32),
    "d1": ((D,), F32),
    "wqgT": ((16, 128, HID), I32),
    "wquT": ((16, 128, HID), I32),
    "wqdT": ((NSLAB, 128, D), I32),
    "sg": ((HID,), F32),
    "su": ((HID,), F32),
    "sd": ((D,), F32),
    "xb_s": ((NT_I, 128, S), BF16),
    "gate_s": ((NT_I, 128, S), BF16),
    "u0_s": ((NT_I, 128, S), BF16),
    "du0_s": ((NT_I, 128, S), BF16),
    "de0_s": ((NT_I, 128, S), BF16),
    "bc0_s": ((NH, 32, S), BF16),
    "ybf_s": ((NT_I, 128, S), BF16),
    "xr0_s": ((NT_D, 128, S), F32),
    "x1_s": ((NT_D, 128, S), F32),
    "x1b_s": ((NT_D, 128, S), BF16),
    "x2_s": ((NT_D, 128, S), F32),
    "x2b_s": ((NT_D, 128, S), BF16),
    "u1_s": ((NT_D, 128, S), BF16),
    "du1_s": ((NT_D, 128, S), BF16),
    "de1_s": ((NT_D, 128, S), BF16),
    "bc1_s": ((1, 32, S), BF16),
    "x3_s": ((NT_D, 128, S), F32),
    "x3b_s": ((NT_D, 128, S), BF16),
    "wgb_s": ((16, 128, HID), BF16),
    "wub_s": ((16, 128, HID), BF16),
    "wdb_s": ((NSLAB, 128, D), BF16),
    "wob_s": ((32, 128, D), BF16),
    "h_s": ((NSLAB, 128, S), BF16),
    "out_s": ((NT_D, 128, S), F32),
}


def act(E, out, in_, func, bias=0.0, scale=1.0, deps=()):
    def fn(e, out=out, in_=in_, func=func, bias=bias, scale=scale):
        return e.activation(out=out, in_=in_, func=func, bias=bias, scale=scale)

    return E.op("scalar", fn, deps=deps)


def vtt(E, out, in0, in1, alu, deps=(), eng="vector"):
    def fn(e, out=out, in0=in0, in1=in1, alu=alu):
        return e.tensor_tensor(out=out, in0=in0, in1=in1, op=alu)

    return E.op(eng, fn, deps=deps)


def vts(E, out, in0, s1, s2, op0, op1=None, deps=(), eng="vector"):
    def fn(e, out=out, in0=in0, s1=s1, s2=s2, op0=op0, op1=op1):
        if op1 is None:
            return e.tensor_scalar(out=out, in0=in0, scalar1=s1, scalar2=None, op0=op0)
        return e.tensor_scalar(out=out, in0=in0, scalar1=s1, scalar2=s2, op0=op0, op1=op1)

    return E.op(eng, fn, deps=deps)


def vstt(E, out, in0, scalar, in1, op0, op1, deps=(), eng="vector"):
    def fn(e, out=out, in0=in0, scalar=scalar, in1=in1, op0=op0, op1=op1):
        return e.scalar_tensor_tensor(out=out, in0=in0, scalar=scalar, in1=in1,
                                      op0=op0, op1=op1)

    return E.op(eng, fn, deps=deps)


def vcopy(E, out, in_, deps=(), eng="vector"):
    def fn(e, out=out, in_=in_):
        return e.tensor_copy(out=out, in_=in_)

    return E.op(eng, fn, deps=deps)


def vmemset(E, ap, val, deps=(), eng="vector"):
    def fn(e, ap=ap, val=val):
        return e.memset(ap, val)

    return E.op(eng, fn, deps=deps)


def vscan(E, out, d0, d1, initial, deps=()):
    def fn(e, out=out, d0=d0, d1=d1, initial=initial):
        return e.tensor_tensor_scan(out=out, data0=d0, data1=d1, initial=initial,
                                    op0=ALU.mult, op1=ALU.add)

    return E.op("vector", fn, deps=deps)


def vrecip(E, out, in_, deps=()):
    def fn(e, out=out, in_=in_):
        return e.reciprocal(out=out, in_=in_)

    return E.op("vector", fn, deps=deps)


def mm(E, out, lhsT, rhs, start, stop, deps=(), inc=None):
    if inc is None:
        inc = stop

    def fn(e, out=out, lhsT=lhsT, rhs=rhs, start=start, stop=stop):
        return e.matmul(out, lhsT, rhs, start=start, stop=stop)

    return E.op("tensor", fn, deps=deps, inc=inc)


def ptr_barrier(E, deps):
    scr = E.consts["scr"]
    for eng in ("scalar", "vector"):
        E.wait_only(eng, deps)
        for i in range(10):
            if eng == "scalar":
                E.op(eng, lambda e: e.activation(out=scr[0:1, 0:1], in_=scr[0:1, 1:2],
                                                 func=AF.Copy), inc=False)
            else:
                E.op(eng, lambda e: e.tensor_copy(out=scr[0:1, 2:3], in_=scr[0:1, 3:4]),
                     inc=False)


def load_pp_vec(E, dram, name, sb, ncols, deps=()):
    toks = []
    for k in range(ncols):
        src = dram[name][k * 128:(k + 1) * 128].unsqueeze(1)
        toks.append(E.dma("sync", sb[:, k:k + 1], src, deps=deps if k == 0 else (),
                          slow=True))
    return toks


def conv4(E, acc, xbc, cw, m, bias_col, deps):
    TC = acc[0].shape[-1]
    a, b = acc
    if bias_col is not None:
        t0 = vts(E, a, xbc[:, 0:TC], cw[:, m * KC:m * KC + 1], bias_col,
                 ALU.mult, ALU.add, deps=deps)
    else:
        t0 = vts(E, a, xbc[:, 0:TC], cw[:, m * KC:m * KC + 1], None, ALU.mult,
                 deps=deps)
    t1 = vstt(E, b, xbc[:, 1:TC + 1], cw[:, m * KC + 1:m * KC + 2], a, ALU.mult,
              ALU.add, deps=[t0])
    t2 = vstt(E, a, xbc[:, 2:TC + 2], cw[:, m * KC + 2:m * KC + 3], b, ALU.mult,
              ALU.add, deps=[t1])
    t3 = vstt(E, b, xbc[:, 3:TC + 3], cw[:, m * KC + 3:m * KC + 4], a, ALU.mult,
              ALU.add, deps=[t2])
    return t3


def layernorm_tiles(E, stat_bank, bc_bank, tiles, tile_toks, tc, msb, sq, war):
    ones_col = E.consts["ones_col"]
    ones_row = E.consts["ones_row"]
    ntile = len(tiles)
    ntot = ntile * 128
    ps_s = stat_bank[0:1, 0:tc]
    ps_b = bc_bank[:, 0:tc]

    mt1 = None
    for i, tl in enumerate(tiles):
        mt1 = mm(E, ps_s, ones_col[:, 0:1], tl, start=(i == 0), stop=(i == ntile - 1),
                 deps=[tile_toks[i], war.get("stat"), E.consts_tok] if i == 0
                 else [tile_toks[i]], inc=(i == ntile - 1))
    m_row = msb[0:1, 0:tc]
    e2_row = msb[0:1, tc:2 * tc]
    r_row = msb[0:1, 2 * tc:3 * tc]
    t_row = msb[0:1, 3 * tc:4 * tc]
    a = vts(E, m_row, ps_s, 1.0 / ntot, None, ALU.mult, deps=[mt1, war.get("rows")])
    mt2 = None
    sq_toks = []
    for i, tl in enumerate(tiles):
        sqt = act(E, sq[:, 0:tc], tl, AF.Square, deps=[tile_toks[i], war.get("sq")])
        sq_toks.append(sqt)
        mt2 = mm(E, ps_s, ones_col[:, 0:1], sq[:, 0:tc], start=(i == 0),
                 stop=(i == ntile - 1),
                 deps=[sqt, a] if i == 0 else [sqt], inc=True)
        war["sq"] = mt2
    bb = vts(E, e2_row, ps_s, 1.0 / ntot, None, ALU.mult, deps=[mt2])
    war["stat"] = bb
    c1 = vtt(E, t_row, m_row, m_row, ALU.mult, deps=[a])
    c2 = vtt(E, e2_row, e2_row, t_row, ALU.subtract, deps=[bb, c1])
    c3 = act(E, t_row, e2_row, AF.Sqrt, bias=LN_EPS, deps=[c2])
    c4 = vrecip(E, r_row, t_row, deps=[c3])
    bmt = mm(E, ps_b, ones_row[0:1, :], m_row, start=True, stop=True,
             deps=[a, war.get("bc"), E.consts_tok2])
    subs = []
    for i, tl in enumerate(tiles):
        subs.append(vtt(E, tl, tl, ps_b, ALU.subtract,
                        deps=[bmt, mt1, mt2, sq_toks[i]]))
    brt = mm(E, ps_b, ones_row[0:1, :], r_row, start=True, stop=True,
             deps=[c4] + subs)
    out = []
    for i, tl in enumerate(tiles):
        out.append(vtt(E, tl, tl, ps_b, ALU.mult, deps=[brt, subs[i]]))
    war["bc"] = out[-1]
    war["rows"] = [bmt, brt]
    return out


@phase("A1", needs={"x0": "w", "wxT": "w", "wgT": "w"}, produces=["xb_s", "gate_s"])
def p_a1(E, dram, ar, ps):
    E.new_phase()
    xbf = ar.alloc(NT_D * S, BF16)
    wtmp = [ar.alloc(2048, F32) for _ in range(2)]
    xtmp2 = [ar.alloc(2048, F32) for _ in range(2)]
    whalf = ar.alloc(16 * 2048, BF16)
    xe = [ar.alloc(S, BF16) for _ in range(2)]

    wtmp_war = [None, None]
    xtmp_war = [None, None]
    ctoks = []
    for k in range(NT_D):
        t = E.dma("scalar", xtmp2[k % 2], dram["x0"][k * 128:(k + 1) * 128, :],
                  deps=[xtmp_war[k % 2]])
        ct = act(E, xbf[:, k * S:(k + 1) * S], xtmp2[k % 2][:], AF.Copy, deps=[t])
        xtmp_war[k % 2] = ct
        ctoks.append(ct)
    xready = ctoks[-1]

    grp = [ps["A"], ps["B"]]
    grp_war = [None, None]
    xe_war = [None, None]
    prev_mm = None

    for wname, oname in (("wxT", "xb_s"), ("wgT", "gate_s")):
        for half in range(2):
            wt = None
            for k in range(NT_D):
                t = E.dma("sync", wtmp[k % 2],
                          dram[wname][k, :, half * 2048:(half + 1) * 2048],
                          deps=[wtmp_war[k % 2]])
                wt = act(E, whalf[:, k * 2048:(k + 1) * 2048], wtmp[k % 2][:],
                         AF.Copy, deps=[t, prev_mm])
                wtmp_war[k % 2] = wt
            for mloc in range(16):
                m = half * 16 + mloc
                gi = mloc % 2
                pp = grp[gi]
                mtok = None
                for k in range(NT_D):
                    for tq in range(4):
                        first = (k == 0 and tq == 0)
                        mtok = mm(E, pp[:, tq * 512:(tq + 1) * 512],
                                  whalf[:, k * 2048 + mloc * 128: k * 2048 + mloc * 128 + 128],
                                  xbf[:, k * S + tq * 512: k * S + tq * 512 + 512],
                                  start=(k == 0), stop=(k == NT_D - 1),
                                  deps=[wt, xready, grp_war[gi]] if first else (),
                                  inc=(k == NT_D - 1 and tq == 3))
                prev_mm = mtok
                fu = AF.Silu if oname == "gate_s" else AF.Copy
                ev = act(E, xe[gi], pp[:], fu, deps=[mtok, xe_war[gi]])
                grp_war[gi] = ev
                xe_war[gi] = E.dma("gpsimd", dram[oname][m], xe[gi], deps=[ev])
    if "dbg_s" in dram:
        for k in range(NT_D):
            E.dma("gpsimd", dram["dbg_s"][k], xbf[:, k * S:(k + 1) * S],
                  deps=[prev_mm])


def proj_phase(E, dram, ar, ps, *, ntile, nheads, src, cwn, cbn, xpn, dtn, dtbn,
               u_s, du_s, de_s, bc_s, silu):
    E.new_phase()
    TC = 512
    NCH = S // TC
    tph = ntile // nheads

    cw = ar.alloc(ntile * KC, F32)
    cb = ar.alloc(ntile, F32)
    dtb = ar.alloc(ntile, F32)
    xpb = ar.alloc(nheads * 16 * 160, BF16)
    dttb = ar.alloc(nheads * HD, BF16)
    wtmp = ar.alloc(2048, F32)
    xbc = [ar.alloc(TC + KC, BF16) for _ in range(2)]
    acc = [ar.alloc(TC, F32) for _ in range(2)]
    ubf = ar.alloc(ntile * TC, BF16)
    dtinb = ar.alloc(TC, BF16)
    bct = ar.alloc(TC, BF16)
    t_r = [ar.alloc(TC, F32) for _ in range(2)]
    t_a = [ar.alloc(TC, F32) for _ in range(2)]
    t_l = [ar.alloc(TC, F32) for _ in range(2)]
    de_t = [ar.alloc(TC, BF16) for _ in range(2)]
    du_t = [ar.alloc(TC, BF16) for _ in range(2)]

    vready = []
    for k in range(ntile):
        vready.append(E.dma("sync", cw[:, k * KC:(k + 1) * KC],
                            dram[cwn][k * 128:(k + 1) * 128, :]))
    vready += load_pp_vec(E, dram, cbn, cb, ntile)
    vready += load_pp_vec(E, dram, dtbn, dtb, ntile)
    wtok = None
    for h in range(nheads):
        for k in range(16):
            src_ap = dram[xpn][h, k] if nheads > 1 else dram[xpn][k]
            t = E.dma("sync", wtmp[:, 0:160], src_ap, deps=[wtok])
            wtok = act(E, xpb[:, (h * 16 + k) * 160:(h * 16 + k + 1) * 160],
                       wtmp[:, 0:160], AF.Copy, deps=[t])
    for h in range(nheads):
        src_ap = dram[dtn][h] if nheads > 1 else dram[dtn][:]
        t = E.dma("sync", wtmp, src_ap, deps=[wtok])
        wtok = act(E, dttb[:, h * HD:(h + 1) * HD], wtmp[:], AF.Copy, deps=[t])
    wready = wtok
    ptr_barrier(E, vready)

    ps_dt = ps["A"][:, 0:512]
    ps_bc = ps["A"][0:32, 512:1024]
    ps_dp = [ps["B"][:, 0:512], ps["B"][:, 512:1024]]
    ps_dt_war = None
    ps_bc_war = None
    ps_dp_war = [None, None]
    dtinb_war = None
    bct_war = None
    chain_war = [None, None]
    dd_war = [None, None]

    u_war = [None] * ntile
    xbc_war = [None, None]
    acc_war = None
    for c in range(NCH):
        utoks = [None] * ntile
        ustoks = [None] * ntile
        for m in range(ntile):
            bslot = m % 2
            if c == 0:
                mz = vmemset(E, xbc[bslot][:, 0:KC - 1], 0.0, deps=[xbc_war[bslot]])
                lt = E.dma("sync", xbc[bslot][:, KC - 1:KC - 1 + TC],
                           dram[src][m][:, 0:TC], deps=[mz])
            else:
                lt = E.dma("sync", xbc[bslot][:, 0:TC + KC - 1],
                           dram[src][m][:, c * TC - (KC - 1): c * TC + TC],
                           deps=[xbc_war[bslot]])
            t3 = conv4(E, acc, xbc[bslot], cw, m,
                       None if silu else cb[:, m:m + 1],
                       [lt, vready, u_war[m], acc_war])
            xbc_war[bslot] = t3
            fu = AF.Silu if silu else AF.Copy
            bias = cb[:, m:m + 1] if silu else 0.0
            ub = act(E, ubf[:, m * TC:(m + 1) * TC], acc[1][:, :], fu, bias=bias,
                     deps=[t3])
            acc_war = [ub]
            ustoks[m] = ub
            utoks[m] = ub
            E.dma("gpsimd", dram[u_s][m][:, c * TC:(c + 1) * TC],
                  ubf[:, m * TC:(m + 1) * TC], deps=[ub])
            E.pump()
        prev_head_mt = None
        for h in range(nheads):
            mt_dt = None
            mt_bc = None
            for k in range(tph):
                tile = h * tph + k
                base = (h * 16 + k) * 160 if nheads > 1 else k * 160
                first_deps = [wready, ps_dt_war, ps_bc_war] + utoks if k == 0 else ()
                mt_dt = mm(E, ps_dt, xpb[:, base:base + 128],
                           ubf[:, tile * TC:(tile + 1) * TC],
                           start=(k == 0), stop=(k == tph - 1), deps=first_deps,
                           inc=(k == tph - 1))
                mt_bc = mm(E, ps_bc, xpb[:, base + 128:base + 160],
                           ubf[:, tile * TC:(tile + 1) * TC],
                           start=(k == 0), stop=(k == tph - 1), inc=(k == tph - 1))
            ev_dt = act(E, dtinb[:], ps_dt, AF.Copy, deps=[mt_dt, dtinb_war])
            ps_dt_war = ev_dt
            ev_bc = vcopy(E, bct[0:32, :], ps_bc, deps=[mt_bc, bct_war])
            ps_bc_war = ev_bc
            bdst = dram[bc_s][h] if nheads > 1 else dram[bc_s][0]
            bct_war = E.dma("gpsimd", bdst[:, c * TC:(c + 1) * TC], bct[0:32, :],
                            deps=[ev_bc])
            for mi in range(tph):
                tile = h * tph + mi
                slot = mi % 2
                mt = mm(E, ps_dp[slot],
                        dttb[:, h * HD + mi * 128: h * HD + (mi + 1) * 128],
                        dtinb[:], start=True, stop=True,
                        deps=[ev_dt, ps_dp_war[slot]])
                prev_head_mt = mt
                rr = act(E, t_r[slot], ps_dp[slot], AF.Relu,
                         bias=dtb[:, tile:tile + 1], deps=[mt, chain_war[slot]])
                aa = act(E, t_a[slot], ps_dp[slot], AF.Abs,
                         bias=dtb[:, tile:tile + 1], deps=[mt])
                ps_dp_war[slot] = aa
                ee = act(E, t_a[slot], t_a[slot][:], AF.Exp, scale=-1.0, deps=[aa])
                ll = act(E, t_l[slot], t_a[slot][:], AF.Ln, bias=1.0, deps=[ee])
                dd = vtt(E, de_t[slot], t_r[slot][:], t_l[slot][:], ALU.add,
                         deps=[ll, rr, dd_war[slot]])
                uu = vtt(E, du_t[slot], de_t[slot][:],
                         ubf[:, tile * TC:(tile + 1) * TC], ALU.mult,
                         deps=[dd, ustoks[tile]])
                chain_war[slot] = dd
                d1 = E.dma("gpsimd", dram[de_s][tile][:, c * TC:(c + 1) * TC],
                           de_t[slot], deps=[dd])
                d2 = E.dma("gpsimd", dram[du_s][tile][:, c * TC:(c + 1) * TC],
                           du_t[slot], deps=[uu])
                dd_war[slot] = [d1, d2]
                u_war[tile] = [uu, d1, d2, mt_dt, mt_bc]
            dtinb_war = prev_head_mt


@phase("A2",
       needs={"xb_s": "s", "cw0": "w", "cb0": "w", "xp0T": "w", "dt0T": "w", "dtb0": "w"},
       produces=["u0_s", "du0_s", "de0_s", "bc0_s"])
def p_a2(E, dram, ar, ps):
    proj_phase(E, dram, ar, ps, ntile=NT_I, nheads=NH, src="xb_s", cwn="cw0",
               cbn="cb0", xpn="xp0T", dtn="dt0T", dtbn="dtb0", u_s="u0_s",
               du_s="du0_s", de_s="de0_s", bc_s="bc0_s", silu=True)


@phase("A2L1",
       needs={"x2b_s": "s", "cw1": "w", "cb1": "w", "xp1T": "w", "dt1T": "w", "dtb1": "w"},
       produces=["u1_s", "du1_s", "de1_s", "bc1_s"])
def p_a2l1(E, dram, ar, ps):
    proj_phase(E, dram, ar, ps, ntile=NT_D, nheads=1, src="x2b_s", cwn="cw1",
               cbn="cb1", xpn="xp1T", dtn="dt1T", dtbn="dtb1", u_s="u1_s",
               du_s="du1_s", de_s="de1_s", bc_s="bc1_s", silu=False)


def scan_core(E, dram, ar, ps, *, ntile, nheads, u_s, du_s, de_s, bc_s, al, dvec,
              gate_s=None, ybf_s=None, post=None, TC=512, chunk_cb=None):
    tph = ntile // nheads
    NCH = S // TC

    asc = ar.alloc(ntile * NST, F32)
    dsb = ar.alloc(ntile, F32)
    carry = ar.alloc(ntile * NST, F32)
    Bt = ar.alloc(NST * TC, BF16)
    Ct = ar.alloc(NST * TC, BF16)
    dA = ar.alloc(NST * TC, BF16)
    dBu = ar.alloc(NST * TC, BF16)
    hh = ar.alloc(NST * TC, BF16)
    trA = ar.alloc(8 * TC, BF16)
    trB = ar.alloc(4 * TC, BF16)
    yv = ar.alloc(TC, F32)
    de_t = [ar.alloc(TC, BF16) for _ in range(2)]
    du_t = [ar.alloc(TC, BF16) for _ in range(2)]
    u_t = [ar.alloc(TC, BF16) for _ in range(2)]
    gt = [ar.alloc(TC, BF16) for _ in range(2)] if gate_s else None
    ybt = ar.alloc(TC, BF16)

    atoks = []
    for k in range(ntile):
        atoks.append(E.dma("sync", asc[:, k * NST:(k + 1) * NST],
                           dram[al][k * 128:(k + 1) * 128, :]))
    t = act(E, asc, asc[:], AF.Exp, deps=atoks)

    def _neg(e, out=asc, in_=asc):
        return e.mul(out=out, in_=in_[:], mul=-1.0)

    atok = E.op("scalar", _neg, deps=[t])
    dtok = load_pp_vec(E, dram, dvec, dsb, ntile)
    ptr_barrier(E, [atok] + dtok)
    carry_tok = [None] * ntile

    war = {}
    scan_last = atok
    tree_last = None
    y_last = None
    for c in range(NCH):
        chunk_stores = []
        for h in range(nheads):
            bsrc = dram[bc_s][h]
            bt = E.dma("sync", Bt[:].rearrange("p (n t) -> p n t", n=NST),
                       bsrc[0:NST, c * TC:(c + 1) * TC].unsqueeze(0)
                       .broadcast_to((128, NST, TC)),
                       deps=[war.get("Bt")])
            ct2 = E.dma("sync", Ct[:].rearrange("p (n t) -> p n t", n=NST),
                        bsrc[NST:2 * NST, c * TC:(c + 1) * TC].unsqueeze(0)
                        .broadcast_to((128, NST, TC)),
                        deps=[war.get("Ct")])
            for mi in range(tph):
                tile = h * tph + mi
                slot = mi % 2
                ld = E.dma("sync", de_t[slot], dram[de_s][tile][:, c * TC:(c + 1) * TC],
                           deps=[war.get(("de", slot))])
                lu = E.dma("sync", du_t[slot], dram[du_s][tile][:, c * TC:(c + 1) * TC],
                           deps=[war.get(("du", slot))])
                lt = E.dma("sync", u_t[slot], dram[u_s][tile][:, c * TC:(c + 1) * TC],
                           deps=[war.get(("ut", slot))])
                if gate_s:
                    lg = E.dma("sync", gt[slot],
                               dram[gate_s][tile][:, c * TC:(c + 1) * TC],
                               deps=[war.get(("gt", slot))])
                extoks = []
                for n in range(NST):
                    extoks.append(act(E, dA[:, n * TC:(n + 1) * TC], de_t[slot][:],
                                      AF.Exp,
                                      scale=asc[:, tile * NST + n: tile * NST + n + 1],
                                      deps=[ld, atok, scan_last]))
                war[("de", slot)] = extoks[-1]
                du3 = du_t[slot][:].unsqueeze(1).broadcast_to((128, NST, TC))
                dbu_tok = vtt(E, dBu[:].rearrange("p (n t) -> p n t", n=NST), du3,
                              Bt[:].rearrange("p (n t) -> p n t", n=NST), ALU.mult,
                              deps=[lu, bt, tree_last])
                war[("du", slot)] = dbu_tok
                war["Bt"] = dbu_tok
                stoks = []
                for n in range(NST):
                    init = 0.0 if c == 0 else \
                        carry[:, tile * NST + n: tile * NST + n + 1]
                    stoks.append(vscan(E, hh[:, n * TC:(n + 1) * TC],
                                       dA[:, n * TC:(n + 1) * TC],
                                       dBu[:, n * TC:(n + 1) * TC],
                                       init,
                                       deps=[extoks[n], dbu_tok, carry_tok[tile]]))
                scan_last = stoks[-1]
                hl = hh[:].rearrange("p (n t) -> p n t", n=NST)[:, :, TC - 1:TC]
                cext = vcopy(E, carry[:, tile * NST:(tile + 1) * NST].unsqueeze(2),
                             hl, deps=stoks)
                carry_tok[tile] = cext
                gtok = vtt(E, dBu, hh[:], Ct[:], ALU.mult, deps=stoks + [ct2, cext])
                war["Ct"] = gtok
                r1 = vtt(E, trA[:, 0:8 * TC], dBu[:, 0:8 * TC], dBu[:, 8 * TC:16 * TC],
                         ALU.add, deps=[gtok, war.get("trA")])
                r2 = vtt(E, trB[:, 0:4 * TC], trA[:, 0:4 * TC], trA[:, 4 * TC:8 * TC],
                         ALU.add, deps=[r1, war.get("trB")])
                r3 = vtt(E, trA[:, 0:2 * TC], trB[:, 0:2 * TC], trB[:, 2 * TC:4 * TC],
                         ALU.add, deps=[r2])
                war["trB"] = r3
                r4 = vtt(E, yv, trA[:, 0:TC], trA[:, TC:2 * TC], ALU.add, deps=[r3])
                war["trA"] = r4
                tree_last = r1
                y2 = vstt(E, yv, u_t[slot][:], dsb[:, tile:tile + 1], yv[:],
                          ALU.mult, ALU.add, deps=[r4, lt, dtok])
                war[("ut", slot)] = y2
                if gate_s:
                    y3 = vtt(E, ybt, yv[:], gt[slot][:], ALU.mult, deps=[y2, lg, y_last])
                    war[("gt", slot)] = y3
                    y_last = E.dma("gpsimd", dram[ybf_s][tile][:, c * TC:(c + 1) * TC],
                                   ybt, deps=[y3])
                    chunk_stores.append(y_last)
                else:
                    y_last = post(c, tile, yv, y2)
                E.pump()
                E.pump()
        if chunk_cb is not None:
            chunk_cb(c, chunk_stores)


@phase("B0C",
       needs={"u0_s": "s", "du0_s": "s", "de0_s": "s", "bc0_s": "s", "gate_s": "s",
              "wob_s": "s", "al0": "w", "d0": "w", "x0": "w"},
       produces=["ybf_s", "x1_s", "x1b_s"])
def p_b0c(E, dram, ar, ps):
    E.new_phase()
    TC = 512
    ybc = ar.alloc(32 * 512, BF16)
    wbuf = [ar.alloc(4 * 1024, BF16) for _ in range(2)]
    xr = [ar.alloc(512, F32) for _ in range(NT_D)]
    xtmp = [ar.alloc(512, F32) for _ in range(2)]
    x1b = ar.alloc(512, BF16)
    msb = ar.alloc(4 * 512, F32)
    sq = ar.alloc(512, F32)
    banks = [ps["A"][:, i * 512:(i + 1) * 512] for i in range(4)] + \
            [ps["B"][:, i * 512:(i + 1) * 512] for i in range(4)]
    st = {"war": {}, "xr_war": [[] for _ in range(NT_D)], "bank_war": [None] * 8,
          "wbuf_war": [None, None], "ybc_war": None, "xt_war": [None, None]}

    def c_chunk(c, ytoks):
        steps = []
        steps.append(lambda c=c, ytoks=ytoks: c_ybc(c, ytoks))
        for mh in range(2):
            for kg in range(8):
                steps.append(lambda c=c, mh=mh, kg=kg: c_kg(c, mh, kg))
            steps.append(lambda c=c, mh=mh: c_evac(c, mh))
        steps.append(lambda c=c: c_post(c))
        if E.pump_steps:
            E.pump_steps.extend(steps)
        else:
            E.pump_steps = steps

    def c_ybc(c, ytoks):
        ylds = []
        for k in range(32):
            ylds.append(E.dma("scalar", ybc[:, k * 512:(k + 1) * 512],
                              dram["ybf_s"][k][:, c * TC:(c + 1) * TC],
                              deps=[ytoks[k], st["ybc_war"]]))
        st["ylds"] = ylds
        st["etoks"] = []

    def c_kg(c, mh, kg):
        slot = kg % 2
        ylds = st["ylds"]
        wlds = []
        for j in range(4):
            k = kg * 4 + j
            wlds.append(E.dma("scalar",
                              wbuf[slot][:, j * 1024:(j + 1) * 1024],
                              dram["wob_s"][k][:, mh * 1024:(mh + 1) * 1024],
                              deps=[st["wbuf_war"][slot]] if j == 0 else ()))
        last_mm = None
        for j in range(4):
            k = kg * 4 + j
            for mloc in range(8):
                deps = ()
                if mloc == 0:
                    deps = [wlds[j], ylds[k]]
                if k == 0:
                    deps = list(deps) + [st["bank_war"][mloc]]
                last_mm = mm(E, banks[mloc],
                             wbuf[slot][:, j * 1024 + mloc * 128:
                                        j * 1024 + mloc * 128 + 128],
                             ybc[:, k * 512:(k + 1) * 512],
                             start=(k == 0), stop=(k == 31),
                             deps=deps, inc=(j == 3 and mloc == 7))
        st["wbuf_war"][slot] = last_mm
        st["last_mm"] = last_mm

    def c_evac(c, mh):
        for mloc in range(8):
            m = mh * 8 + mloc
            ev = act(E, xr[m], banks[mloc], AF.Copy,
                     deps=[st["last_mm"]] + st["xr_war"][m])
            st["bank_war"][mloc] = ev
            st["etoks"].append(ev)
        if mh == 1:
            st["ybc_war"] = st["last_mm"]

    def c_post(c):
        etoks = st["etoks"]
        war = st["war"]
        ttoks = []
        for m in range(NT_D):
            lx0 = E.dma("scalar", xtmp[m % 2], dram["x0"][m * 128:(m + 1) * 128,
                                                          c * TC:(c + 1) * TC],
                        deps=[st["xt_war"][m % 2]])
            at = vtt(E, xr[m], xr[m][:], xtmp[m % 2][:], ALU.add,
                     deps=[lx0, etoks[m]])
            st["xt_war"][m % 2] = at
            ttoks.append(at)
        ws = war.get("stat")
        war["stat"] = [etoks[8]] + (ws if isinstance(ws, list) else [ws])
        wb2 = war.get("bc")
        war["bc"] = [etoks[9]] + (wb2 if isinstance(wb2, list) else [wb2])
        lnt = layernorm_tiles(E, banks[0], banks[1],
                              [xr[i][:] for i in range(NT_D)], ttoks, TC, msb, sq,
                              war)
        st["bank_war"][0] = [st["bank_war"][0], lnt[-1]]
        st["bank_war"][1] = [st["bank_war"][1], lnt[-1]]
        for i in range(NT_D):
            bt = act(E, x1b, xr[i][:], AF.Copy, deps=[lnt[i], war.get("x1b")])
            da = E.dma("gpsimd", dram["x1_s"][i][:, c * TC:(c + 1) * TC], xr[i],
                       deps=[lnt[i]])
            db = E.dma("gpsimd", dram["x1b_s"][i][:, c * TC:(c + 1) * TC], x1b,
                       deps=[bt])
            war["x1b"] = db
            st["xr_war"][i] = [bt, da, db]

    scan_core(E, dram, ar, ps, ntile=NT_I, nheads=NH, u_s="u0_s", du_s="du0_s",
              de_s="de0_s", bc_s="bc0_s", al="al0", dvec="d0",
              gate_s="gate_s", ybf_s="ybf_s", chunk_cb=c_chunk)
    while E.pump_steps:
        E.pump()



@phase("F0", needs={"x1_s": "s", "x1b_s": "s", "fw0": "w", "fs0": "w"},
       produces=["x2_s", "x2b_s"])
def p_f0(E, dram, ar, ps):
    E.new_phase()
    TC = 512
    NCH = S // TC
    wtmp = [ar.alloc(2048, F32) for _ in range(2)]
    wb = ar.alloc(16 * 2048, BF16)
    xb = ar.alloc(16 * 2048, BF16)
    ssb = ar.alloc(16, F32)
    oc = [ar.alloc(512, F32) for _ in range(NT_D)]
    xtmp = [ar.alloc(512, F32) for _ in range(2)]
    x2b = ar.alloc(512, BF16)
    msb = ar.alloc(4 * 512, F32)
    sq = ar.alloc(512, F32)
    rowtmp = ar.alloc(7 * 512, F32)

    stok = load_pp_vec(E, dram, "fs0", ssb, NT_D)
    ptr_barrier(E, stok)
    wtmp_war = [None, None]
    wtoks = None
    for k in range(16):
        t = E.dma("sync", wtmp[k % 2], dram["fw0"][k], deps=[wtmp_war[k % 2]])
        wtoks = act(E, wb[:, k * 2048:(k + 1) * 2048], wtmp[k % 2][:], AF.Copy,
                    deps=[t])
        wtmp_war[k % 2] = wtoks
    xtoks = [None] * NT_D
    for k in range(16):
        xtoks[k] = E.dma("sync", xb[:, k * 2048:(k + 1) * 2048], dram["x1b_s"][k])

    banks = [ps["A"][:, i * 512:(i + 1) * 512] for i in range(4)] + \
            [ps["B"][:, 0:512], ps["B"][:, 512:1024]]
    stat_bank = ps["B"][:, 1024:1536]
    bc_bank = ps["B"][:, 1536:2048]
    bank_war = [None] * 6
    war = {}
    oc_war = [None] * NT_D
    ones_col = E.consts["ones_col"]
    ones_row = E.consts["ones_row"]

    for c in range(NCH):
        etoks = []
        for m in range(NT_D):
            bk = m % 6
            mtok = None
            for k in range(16):
                mtok = mm(E, banks[bk],
                          wb[:, k * 2048 + m * 128: k * 2048 + m * 128 + 128],
                          xb[:, k * 2048 + c * TC: k * 2048 + c * TC + TC],
                          start=(k == 0), stop=(k == 15),
                          deps=[wtoks, bank_war[bk]] + xtoks + stok
                          if k == 0 else (),
                          inc=(k == 15))
            ev = act(E, oc[m], banks[bk], AF.Copy, scale=ssb[:, m:m + 1],
                     deps=[mtok, oc_war[m]])
            bank_war[bk] = ev
            etoks.append(ev)
        mt2 = None
        for m in range(NT_D):
            sqt = act(E, sq, oc[m][:, :], AF.Square, deps=[etoks[m], war.get("sq")])
            mt2 = mm(E, stat_bank[0:1, 0:TC], ones_col[:, 0:1], sq[:, 0:TC],
                     start=(m == 0), stop=(m == NT_D - 1),
                     deps=[sqt, war.get("stat"), E.consts_tok] if m == 0 else [sqt],
                     inc=True)
            war["sq"] = mt2
        vrow = rowtmp[0:1, 0:512]
        frow = rowtmp[0:1, 512:1024]
        trow = rowtmp[0:1, 1024:1536]
        grow = rowtmp[0:1, 1536:2048]
        hrow = rowtmp[0:1, 2048:2560]
        xrow = rowtmp[0:1, 2560:3072]
        srow = rowtmp[0:1, 3072:3584]
        v0 = vts(E, srow, stat_bank[0:1, 0:TC], 1.0, None, ALU.mult,
                 deps=[mt2, war.get("rows2")])
        v1 = act(E, xrow, srow, AF.Sqrt, deps=[v0])
        v2 = vts(E, vrow, xrow, FFN_EPS, None, ALU.max, deps=[v1])
        war["stat"] = v0
        th = act(E, trow, vrow, AF.Tanh, deps=[v2])
        dn = vts(E, xrow, vrow, FFN_EPS, None, ALU.add, deps=[v2, v1])
        rc = vrecip(E, grow, xrow, deps=[dn])
        f1 = vtt(E, frow, trow, grow, ALU.mult, deps=[th, rc])
        on = vtt(E, xrow, frow, vrow, ALU.mult, deps=[f1, rc])
        on2 = vts(E, trow, xrow, FFN_EPS, None, ALU.max, deps=[on, th])
        rc2 = vrecip(E, xrow, trow, deps=[on2])
        g1 = vts(E, grow, xrow, 0.99, 1.0, ALU.mult, ALU.min, deps=[rc2])
        hf = vtt(E, hrow, frow, grow, ALU.mult, deps=[g1, f1])
        bht = mm(E, bc_bank[:, 0:TC], ones_row[0:1, :], hrow, start=True, stop=True,
                 deps=[hf, war.get("bc"), E.consts_tok2])
        war["rows2"] = [bht, hf]
        ttoks = []
        for m in range(NT_D):
            ap1 = vtt(E, oc[m], oc[m][:], bc_bank[:, 0:TC], ALU.mult,
                      deps=[bht, etoks[m]])
            lx = E.dma("sync", xtmp[m % 2], dram["x1_s"][m][:, c * TC:(c + 1) * TC],
                       deps=[war.get(("xt", m % 2))])
            at = vtt(E, oc[m], oc[m][:], xtmp[m % 2][:], ALU.add, deps=[ap1, lx])
            war[("xt", m % 2)] = at
            ttoks.append(at)
        war["bc"] = ttoks[-1]
        lnt = layernorm_tiles(E, stat_bank, bc_bank,
                              [oc[i][:] for i in range(NT_D)], ttoks, TC, msb, sq,
                              war)
        for i in range(NT_D):
            bt = act(E, x2b, oc[i][:], AF.Copy, deps=[lnt[i], war.get("x2b")])
            da = E.dma("gpsimd", dram["x2_s"][i][:, c * TC:(c + 1) * TC], oc[i],
                       deps=[lnt[i]])
            db = E.dma("gpsimd", dram["x2b_s"][i][:, c * TC:(c + 1) * TC], x2b,
                       deps=[bt])
            war["x2b"] = db
            oc_war[i] = [bt, da, db]


@phase("W1", needs={"wqgT": "w", "wquT": "w", "wqdT": "w", "woT": "w"},
       produces=["wgb_s", "wub_s", "wdb_s", "wob_s"])
def p_w1(E, dram, ar, ps):
    HW = HID // 2
    it = [ar.alloc(HW, I32) for _ in range(2)]
    bt = [ar.alloc(HW, BF16) for _ in range(2)]
    ar.base = ar.off
    state = {"war": [None, None], "dtok": [None, None], "i": 0}
    jobs = []
    for srcn, dst, nstrip, width in [("woT", "wob_s", 32, D),
                                     ("wqgT", "wgb_s", 16, HID),
                                     ("wquT", "wub_s", 16, HID),
                                     ("wqdT", "wdb_s", NSLAB, D)]:
        nh = (width + HW - 1) // HW
        for k in range(nstrip):
            for hwi in range(nh):
                lo = hwi * HW
                hi = min(width, lo + HW)
                jobs.append((srcn, dst, k, lo, hi))

    def make_step(j):
        srcn, dst, k, lo, hi = j

        def step():
            slot = state["i"] % 2
            state["i"] += 1
            n = hi - lo
            if srcn == "woT":
                src_i = it[slot][:, 0:n].bitcast(F32)
                bias = 0.0
            else:
                src_i = it[slot][:, 0:n]
                bias = -128.0
            dst_b = bt[slot][:, 0:n]
            t = E.dma("sync", src_i, dram[srcn][k][:, lo:hi],
                      deps=[state["war"][slot]])
            cv = act(E, dst_b, src_i, AF.Copy, bias=bias,
                     deps=[t, state["dtok"][slot]])
            state["war"][slot] = cv
            state["dtok"][slot] = E.dma("gpsimd", dram[dst][k][:, lo:hi],
                                        dst_b, deps=[cv])

        return step

    E.pump_steps = [make_step(j) for j in jobs]


@phase("B1F1",
       needs={"u1_s": "s", "du1_s": "s", "de1_s": "s", "bc1_s": "s", "x2_s": "s",
              "wgb_s": "s", "wub_s": "s", "wdb_s": "s",
              "al1": "w", "d1": "w", "sg": "w", "su": "w", "sd": "w"},
       produces=["x3_s", "x3b_s", "h_s", "out_s"])
def p_b1f1(E, dram, ar, ps):
    E.new_phase()
    TC = 256
    TB = 1024
    NCH = S // TC
    yc = [ar.alloc(TC, F32) for _ in range(NT_D)]
    xtmp = [ar.alloc(512, F32) for _ in range(2)]
    x3b = ar.alloc(TC, BF16)
    msb = ar.alloc(4 * 512, F32)
    sq = ar.alloc(512, F32)
    sgb = ar.alloc(NSLAB, F32)
    sub = ar.alloc(NSLAB, F32)
    sdb = ar.alloc(NT_D, F32)
    x3bb = ar.alloc(16 * TB, BF16)
    wgt = [ar.alloc(16 * 128, BF16) for _ in range(2)]
    wut = [ar.alloc(16 * 128, BF16) for _ in range(2)]
    wdt = [ar.alloc(512, BF16) for _ in range(4)]
    hdt = [ar.alloc(512, BF16) for _ in range(4)]
    gs = ar.alloc(TB, BF16)
    us = ar.alloc(TB, BF16)
    htmp = [ar.alloc(TB, BF16) for _ in range(2)]
    oc = [ar.alloc(512, F32) for _ in range(NT_D)]
    fxtmp = [ar.alloc(512, F32) for _ in range(2)]

    svec = load_pp_vec(E, dram, "sg", sgb, NSLAB)
    svec += load_pp_vec(E, dram, "su", sub, NSLAB)
    svec += load_pp_vec(E, dram, "sd", sdb, NT_D)
    ptr_barrier(E, svec)

    banksA = [ps["A"][:, i * 512:(i + 1) * 512] for i in range(4)]
    banksB = [ps["B"][:, i * 512:(i + 1) * 512] for i in range(4)]

    st = {"war": {}, "ttoks": [None] * NT_D, "yc_war": [None] * NT_D,
          "x3db": [[] for _ in range(NCH)],
          "hb_war": [None] * NSLAB, "grp_war": [None, None], "gs_war": None,
          "wg_war": [None, None], "wd_war": [None] * 4,
          "hd_war": [None] * 4,
          "x3bb_war": [], "xt_war": [None, None],
          "abank": [None] * 4, "oc_war": [[] for _ in range(NT_D)],
          "fwar": {}, "out_last": None}
    war = st["war"]

    def post(c, tile, yv, y2tok):
        lx = E.dma("scalar", xtmp[tile % 2][:, 0:TC],
                   dram["x2_s"][tile][:, c * TC:(c + 1) * TC],
                   deps=[st["xt_war"][tile % 2]])
        at = vtt(E, yc[tile], yv[:], xtmp[tile % 2][:, 0:TC], ALU.add,
                 deps=[y2tok, lx, st["yc_war"][tile]])
        st["xt_war"][tile % 2] = at
        st["ttoks"][tile] = at
        if tile == NT_D - 1:
            lnt = layernorm_tiles(E, banksB[2], banksB[3],
                                  [yc[i][:] for i in range(NT_D)], st["ttoks"],
                                  TC, msb, sq, war)
            last = None
            for i in range(NT_D):
                bt = act(E, x3b, yc[i][:], AF.Copy, deps=[lnt[i], war.get("x3b")])
                da = E.dma("gpsimd", dram["x3_s"][i][:, c * TC:(c + 1) * TC],
                           yc[i], deps=[lnt[i]])
                db = E.dma("gpsimd", dram["x3b_s"][i][:, c * TC:(c + 1) * TC],
                           x3b, deps=[bt])
                war["x3b"] = db
                st["yc_war"][i] = [bt, da, db]
                st["x3db"][c].append(db)
                last = db
            return last
        return at

    def f_x3load(blk):
        toks = []
        nb = NCH // 2
        dbs = [t for c in range(blk * nb, blk * nb + nb) for t in st["x3db"][c]]
        for k in range(NT_D):
            toks.append(E.dma("sync", x3bb[:, k * TB:(k + 1) * TB],
                              dram["x3b_s"][k][:, blk * TB:(blk + 1) * TB],
                              deps=(dbs if k == 0 else []) + st["x3bb_war"]))
        st["xtk"] = toks

    def f_wload(blk, slab):
        slot = slab % 2
        wg_l, wu_l = [], []
        for k in range(NT_D):
            wg_l.append(E.dma("sync", wgt[slot][:, k * 128:(k + 1) * 128],
                              dram["wgb_s"][k][:, slab * 128:(slab + 1) * 128],
                              deps=[st["wg_war"][slot]] if k == 0 else ()))
            wu_l.append(E.dma("sync", wut[slot][:, k * 128:(k + 1) * 128],
                              dram["wub_s"][k][:, slab * 128:(slab + 1) * 128]))
        st["wg_l"], st["wu_l"] = wg_l, wu_l

    def f_slab(blk, slab):
        slot = slab % 2
        wg_l, wu_l = st["wg_l"], st["wu_l"]
        mg = None
        mu = None
        for k in range(NT_D):
            for th in range(2):
                deps = [wg_l[k], st["xtk"][k]] if th == 0 else ()
                if k == 0:
                    deps = list(deps) + [st["abank"][th]]
                mg = mm(E, banksA[th],
                        wgt[slot][:, k * 128:(k + 1) * 128],
                        x3bb[:, k * TB + th * 512: k * TB + th * 512 + 512],
                        start=(k == 0), stop=(k == NT_D - 1),
                        deps=deps, inc=(k == NT_D - 1 and th == 1))
            for th in range(2):
                deps = [wu_l[k]] if th == 0 else ()
                if k == 0:
                    deps = list(deps) + [st["abank"][2 + th]]
                mu = mm(E, banksA[2 + th],
                        wut[slot][:, k * 128:(k + 1) * 128],
                        x3bb[:, k * TB + th * 512: k * TB + th * 512 + 512],
                        start=(k == 0), stop=(k == NT_D - 1),
                        deps=deps, inc=(k == NT_D - 1 and th == 1))
        pp = ps["A"]
        ge = act(E, gs, pp[:, 0:1024], AF.Silu, scale=sgb[:, slab:slab + 1],
                 deps=[mg, st["gs_war"]])
        ue = act(E, us, pp[:, 1024:2048], AF.Copy, scale=sub[:, slab:slab + 1],
                 deps=[mu])
        ht = vtt(E, htmp[slot], gs[:], us[:], ALU.mult,
                 deps=[ge, ue, st["hb_war"][slab]])
        hd = E.dma("gpsimd", dram["h_s"][slab][:, blk * TB:(blk + 1) * TB],
                   htmp[slot], deps=[ht])
        st["gs_war"] = ht
        st["hb_war"][slab] = hd
        st["abank"][0] = ge
        st["abank"][1] = ge
        st["abank"][2] = ue
        st["abank"][3] = ue
        st["wg_war"][slot] = ht

    def f_down(blk, th, q, kg):
        mt = None
        lds = {}
        mtk = {}

        def issue_load(ki):
            k = kg * 11 + ki
            slot = k % 4
            dep = mtk.get(k - 4, st["wd_war"][slot])
            wl = E.dma("sync", wdt[slot],
                       dram["wdb_s"][k][:, q * 512:(q + 1) * 512], deps=[dep])
            hl = E.dma("sync", hdt[slot],
                       dram["h_s"][k][:, blk * TB + th * 512: blk * TB + th * 512 + 512],
                       deps=[st["hb_war"][k], dep])
            lds[k] = (wl, hl)

        for ki in range(4):
            issue_load(ki)
        for ki in range(11):
            k = kg * 11 + ki
            slot = k % 4
            wl, hl = lds[k]
            for mloc in range(4):
                deps = [wl, hl] if mloc == 0 else ()
                if k == 0:
                    deps = list(deps) + [st["abank"][mloc]] + svec
                mt = mm(E, banksA[mloc],
                        wdt[slot][:, mloc * 128:(mloc + 1) * 128],
                        hdt[slot][:],
                        start=(k == 0), stop=(k == NSLAB - 1),
                        deps=deps, inc=(mloc == 3))
            mtk[k] = mt
            st["wd_war"][slot] = mt
            st["hd_war"][slot] = mt
            if ki + 4 < 11:
                issue_load(ki + 4)
        if kg == 3:
            for mloc in range(4):
                m = q * 4 + mloc
                ev = act(E, oc[m], banksA[mloc], AF.Copy, scale=sdb[:, m:m + 1],
                         deps=[mt] + st["oc_war"][m])
                st["abank"][mloc] = ev
                st["fwar"][("ev", m)] = ev

    def f_ln(blk, th):
        fwar = st["fwar"]
        ttoks = []
        for m in range(NT_D):
            lx = E.dma("sync", fxtmp[m % 2],
                       dram["x3_s"][m][:, blk * TB + th * 512: blk * TB + th * 512 + 512],
                       deps=[fwar.get(("xt", m % 2))])
            at = vtt(E, oc[m], oc[m][:], fxtmp[m % 2][:], ALU.add,
                     deps=[fwar[("ev", m)], lx])
            fwar[("xt", m % 2)] = at
            ttoks.append(at)
        lnt = layernorm_tiles(E, banksB[2], banksB[3],
                              [oc[i][:] for i in range(NT_D)], ttoks, 512, msb, sq,
                              st["war"])
        for i in range(NT_D):
            da = E.dma("gpsimd",
                       dram["out_s"][i][:, blk * TB + th * 512: blk * TB + th * 512 + 512],
                       oc[i], deps=[lnt[i]])
            st["oc_war"][i] = [da]
            st["out_last"] = da
        if th == 1 and blk == 0:
            st["x3bb_war"] = [st["hb_war"][NSLAB - 1]]

    def register_block(blk):
        steps = [lambda blk=blk: f_x3load(blk)]
        for slab in range(NSLAB):
            steps.append(lambda blk=blk, slab=slab: f_wload(blk, slab))
            steps.append(lambda blk=blk, slab=slab: f_slab(blk, slab))
        for th in range(2):
            for q in range(4):
                for kg in range(4):
                    steps.append(lambda blk=blk, th=th, q=q, kg=kg:
                                 f_down(blk, th, q, kg))
            steps.append(lambda blk=blk, th=th: f_ln(blk, th))
        if E.pump_steps:
            E.pump_steps.extend(steps)
        else:
            E.pump_steps = steps

    def b1_cb(c, stores):
        if c == NCH // 2 - 1:
            register_block(0)
        elif c == NCH - 1:
            register_block(1)

    scan_core(E, dram, ar, ps, ntile=NT_D, nheads=1, u_s="u1_s", du_s="du1_s",
              de_s="de1_s", bc_s="bc1_s", al="al1", dvec="d1", post=post, TC=TC,
              chunk_cb=b1_cb)
    while E.pump_steps:
        E.pump()


def make_consts(E, ar):
    nc = E.nc
    ones_col = nc.alloc_sbuf_tensor("ones_col", [128, 1], F32)
    ones_row = nc.alloc_sbuf_tensor("ones_row", [1, 128], F32)
    scr = nc.alloc_sbuf_tensor("scr", [128, 8], F32)
    cvals = nc.alloc_sbuf_tensor("cvals", [128, 4], F32)
    E.consts = {"ones_col": ones_col, "ones_row": ones_row, "scr": scr}
    E.new_phase()
    E.consts_tok = vmemset(E, ones_col[:], 1.0)
    E.consts_tok2 = vmemset(E, ones_row[:], 1.0)
    vmemset(E, scr[:], 0.0)
    for i, v in enumerate([LN_EPS, FFN_EPS, 0.99]):
        E.consts_tok2 = vmemset(E, cvals[:, i:i + 1], v)
        nc.const_aps.register(F32, v, cvals[:, i:i + 1])


_BUILD_CACHE = {}


def build_launch(group, final_outs):
    key = (tuple(group), tuple(sorted(final_outs)))
    if key in _BUILD_CACHE:
        return _BUILD_CACHE[key]
    nc = bass.Bass("TRN2", target_bir_lowering=False)
    stack = ExitStack()
    produced = set()
    ext_in = set()
    for g in group:
        _, needs, produces = PHASES[g]
        for n in needs:
            if n not in produced:
                ext_in.add(n)
        produced.update(produces)
    ext_out = set(final_outs) & produced
    internal = produced - ext_out
    dram = {}
    for n in sorted(ext_in):
        shape, dt = SPEC[n]
        dram[n] = nc.dram_tensor(n, list(shape), dt, kind="ExternalInput")
    for n in sorted(ext_out):
        shape, dt = SPEC[n]
        dram[n] = nc.dram_tensor(n, list(shape), dt, kind="ExternalOutput")
    for n in sorted(internal):
        shape, dt = SPEC[n]
        dram[n] = nc.dram_tensor(n, list(shape), dt, kind="Internal")

    E = Emit(nc, stack)
    ar = Arena(nc, 206 * 1024)
    psA = nc.alloc_psum_tensor("psA", [128, 2048], F32)
    psB = nc.alloc_psum_tensor("psB", [128, 2048], F32)
    ps = {"A": psA, "B": psB}
    make_consts(E, ar)
    for g in group:
        fn, _, _ = PHASES[g]
        ar.reset()
        fn(E, dram, ar, ps)
        if g == "A2":
            while E.pump_steps:
                E.pump()
            ar.base = 0
    final_deps = [(s, E.semcnt[s]) for s in E.semcnt if E.semcnt[s] > 0]
    E.wait_only("gpsimd", final_deps)
    E.finalize()
    res = (nc, sorted(ext_in), sorted(ext_out))
    _BUILD_CACHE[key] = res
    return res


def prep_weights(inputs):
    def C(a):
        return np.ascontiguousarray(np.asarray(a))

    w = {}
    ip = np.asarray(inputs["L0_attn_in_proj_w"])
    w["wxT"] = C(ip[0::2].T).reshape(16, 128, I4)
    w["wgT"] = C(ip[1::2].T).reshape(16, 128, I4)
    w["cw0"] = C(inputs["L0_attn_conv_w"])
    w["cb0"] = C(inputs["L0_attn_conv_b"])
    xp = np.asarray(inputs["L0_attn_x_proj_weight"])
    w["xp0T"] = C(np.transpose(xp, (0, 2, 1))).reshape(NH, 16, 128, 160)
    dt = np.asarray(inputs["L0_attn_dt_proj_weight"])
    w["dt0T"] = C(np.transpose(dt, (0, 2, 1)))
    w["dtb0"] = C(np.asarray(inputs["L0_attn_dt_proj_bias"]).reshape(-1))
    w["al0"] = C(np.asarray(inputs["L0_attn_A_log"]).reshape(I4, NST))
    w["d0"] = C(np.asarray(inputs["L0_attn_D"]).reshape(-1))
    w["woT"] = C(np.asarray(inputs["L0_attn_out_proj_w"]).T).reshape(32, 128, D)
    w["fw0"] = C(inputs["L0_ffn_weight"]).reshape(16, 128, D)
    w["fs0"] = C(inputs["L0_ffn_scale"])
    w["cw1"] = C(np.asarray(inputs["L1_attn_conv_w"]).reshape(D, KC))
    w["cb1"] = C(inputs["L1_attn_conv_b"])
    w["xp1T"] = C(np.asarray(inputs["L1_attn_xproj_w"]).T).reshape(16, 128, 160)
    w["dt1T"] = C(np.asarray(inputs["L1_attn_dtproj_w"]).T)
    w["dtb1"] = C(inputs["L1_attn_dtproj_b"])
    w["al1"] = C(inputs["L1_attn_A_log"])
    w["d1"] = C(inputs["L1_attn_D_skip"])
    w["wqgT"] = C(np.asarray(inputs["L1_ffn_wq_gate"]).T).reshape(16, 128, HID)
    w["wquT"] = C(np.asarray(inputs["L1_ffn_wq_up"]).T).reshape(16, 128, HID)
    w["wqdT"] = C(np.asarray(inputs["L1_ffn_wq_down"]).T).reshape(NSLAB, 128, D)
    w["sg"] = C(inputs["L1_ffn_s_gate"])
    w["su"] = C(inputs["L1_ffn_s_up"])
    w["sd"] = C(inputs["L1_ffn_s_down"])
    for k in w:
        assert w[k].shape == SPEC[k][0], (k, w[k].shape, SPEC[k][0])
        want = np.int32 if SPEC[k][1] == I32 else np.float32
        assert w[k].dtype == want, (k, w[k].dtype)
    return w


LAUNCH_GROUPS = [["W1", "A1", "A2", "B0C", "F0", "A2L1", "B1F1"]]
FINAL = "out_s"


def kernel(**inputs):
    w = prep_weights(inputs)
    x = np.asarray(inputs["L0_attn_hidden_states"])
    pool = [dict(w) for _ in range(B)]
    for i in range(B):
        pool[i]["x0"] = np.ascontiguousarray(x[i].T)

    groups = LAUNCH_GROUPS
    for gi, group in enumerate(groups):
        later_needs = set()
        for g2 in [g for grp in groups[gi + 1:] for g in grp]:
            later_needs.update(PHASES[g2][1].keys())
        produced = set()
        for g in group:
            produced.update(PHASES[g][2])
        outs_needed = (produced & later_needs) | ({FINAL} & produced)
        nc, ins, outs = build_launch(group, outs_needed)
        in_maps = [{n: pool[i][n] for n in ins} for i in range(B)]
        res = run_bass_kernel_spmd(nc, in_maps, core_ids=list(range(B)))
        for i in range(B):
            for n in outs:
                pool[i][n] = res.results[i][n]

    out = np.stack([pool[i][FINAL].reshape(D, S).T for i in range(B)])
    return np.ascontiguousarray(out.astype(np.float32))


if __name__ == "__main__":
    pass
